# version 05
# baseline (speedup 1.0000x reference)
import numpy as np
import ml_dtypes
from contextlib import ExitStack
import concourse.bass as bass
import concourse.mybir as mybir
from concourse.bass_utils import run_bass_kernel_spmd

F32 = mybir.dt.float32
BF16 = mybir.dt.bfloat16
I32 = mybir.dt.int32
AF = mybir.ActivationFunctionType
ALU = mybir.AluOpType
AX = mybir.AxisListType
NPBF = ml_dtypes.bfloat16

B, N, D = 4, 4096, 1024
HALF = N // 2
DK = D // 128
H3 = 3072
HT3 = H3 // 128
QSC = float(1.0 / 127.0)
DI, DS, DTR, DC = 2048, 16, 64, 4
SI = DI // 128
EM, RM = 8, 64
LN_EPS = 1e-5
NCORE = 8
TCH = 1024
NCH = N // TCH


class Prog:
    ENGS = ("sync", "vector", "scalar", "tensor", "gpsimd")

    def __init__(self):
        self.nc = bass.Bass("TRN2", target_bir_lowering=False)
        self.ctx = ExitStack()
        self.q = {e: [] for e in self.ENGS}
        self.cnt = {e: 0 for e in self.ENGS}

    def sb(self, name, shape, dt):
        return self.ctx.enter_context(self.nc.sbuf_tensor(name, shape, dt))

    def ps(self, name, shape, dt=F32):
        return self.ctx.enter_context(self.nc.psum_tensor(name, shape, dt))

    def din(self, name, shape, dt):
        return self.nc.dram_tensor(name, shape, dt, kind="ExternalInput").ap()

    def dout(self, name, shape, dt):
        return self.nc.dram_tensor(name, shape, dt, kind="ExternalOutput").ap()

    def dtmp(self, name, shape, dt):
        return self.nc.dram_tensor(name, shape, dt, kind="Internal").ap()

    @staticmethod
    def _is_tok(d):
        return (isinstance(d, tuple) and len(d) == 2 and isinstance(d[1], int)
                and (isinstance(d[0], str)
                     or (isinstance(d[0], tuple) and len(d[0]) == 2 and d[0][0] == "lane")))

    @staticmethod
    def _flat(deps):
        out = []
        for d in deps:
            if d is None:
                continue
            if Prog._is_tok(d):
                out.append(d)
            else:
                out.extend(Prog._flat(d))
        return out

    def _op(self, eng, fn, deps, dma=False, lane=None):
        if lane is not None:
            key = ("lane", lane)
            self.cnt.setdefault(key, 0)
            self.cnt[key] += 16
            self.q[eng].append((fn, Prog._flat(deps), 16, key))
            return (key, self.cnt[key])
        inc = 16 if dma else 1
        self.cnt[eng] += inc
        self.q[eng].append((fn, Prog._flat(deps), inc, None))
        return (eng, self.cnt[eng])

    def dma(self, out, in_, deps=(), eng="sync", key=None):
        assert key is not None, "dma requires a lane key"
        return self._op(eng, lambda e, o=out, i=in_: e.dma_start(out=o, in_=i), deps,
                        dma=True, lane=key)

    def mm(self, out, lhsT, rhs, start, stop, deps=()):
        nf = int(out.ap[-1][1])
        if nf <= 512:
            return self._op("tensor", lambda e, o=out, l=lhsT, r=rhs, s1=start, s2=stop:
                            e.matmul(out=o, lhsT=l, rhs=r, start=s1, stop=s2), deps)
        tok = None
        for c in range(0, nf, 512):
            w = min(512, nf - c)
            tok = self._op("tensor",
                           lambda e, o=out[:, c:c + w], l=lhsT, r=rhs[:, c:c + w], s1=start, s2=stop:
                           e.matmul(out=o, lhsT=l, rhs=r, start=s1, stop=s2), deps)
        return tok

    def tt(self, out, a, b, op, deps=(), eng="vector"):
        return self._op(eng, lambda e, o=out, x=a, y=b, p=op: e.tensor_tensor(out=o, in0=x, in1=y, op=p), deps)

    def ts(self, out, a, scalar, op, deps=(), eng="vector"):
        return self._op(eng, lambda e, o=out, x=a, s=scalar, p=op:
                        e.tensor_single_scalar(out=o, in_=x, scalar=s, op=p), deps)

    def ts2(self, out, a, s1, s2, op0, op1, deps=(), eng="vector"):
        return self._op(eng, lambda e, o=out, x=a, a1=s1, a2=s2, p0=op0, p1=op1:
                        e.tensor_scalar(out=o, in0=x, scalar1=a1, scalar2=a2, op0=p0, op1=p1), deps)

    def stt(self, out, in0, scalar, in1, op0, op1, deps=(), eng="vector"):
        return self._op(eng, lambda e, o=out, x=in0, s=scalar, y=in1, p0=op0, p1=op1:
                        e.scalar_tensor_tensor(out=o, in0=x, scalar=s, in1=y, op0=p0, op1=p1), deps)

    def act(self, out, in_, func, deps=(), scale=None, bias=None):
        kw = {}
        if scale is not None:
            kw["scale"] = scale
        if bias is not None:
            kw["bias"] = bias
        return self._op("scalar", lambda e, o=out, i=in_, f=func, k=kw:
                        e.activation(out=o, in_=i, func=f, **k), deps)

    def copy(self, out, in_, deps=(), eng="scalar"):
        if eng == "scalar":
            return self._op(eng, lambda e, o=out, i=in_: e.copy(out=o, in_=i), deps)
        return self._op(eng, lambda e, o=out, i=in_: e.tensor_copy(out=o, in_=i), deps)

    def red(self, out, in_, op, deps=()):
        return self._op("vector", lambda e, o=out, i=in_, p=op:
                        e.tensor_reduce(out=o, in_=i, axis=AX.X, op=p), deps)

    def scan(self, out, a, b, initial, deps=()):
        return self._op("vector", lambda e, o=out, x=a, y=b, h=initial:
                        e.tensor_tensor_scan(out=o, data0=x, data1=y, initial=h,
                                             op0=ALU.mult, op1=ALU.add), deps)

    def memset(self, ap, v, deps=(), eng="gpsimd"):
        return self._op(eng, lambda e, a=ap, c=v: e.memset(a, c), deps)

    def recip(self, out, in_, deps=()):
        return self._op("vector", lambda e, o=out, i=in_: e.reciprocal(out=o, in_=i), deps)

    def barrier(self):
        toks = [(k, v) for k, v in self.cnt.items() if v > 0]
        for e in self.ENGS:
            self.q[e].append((None, list(toks), 0, None))
        return toks

    def consts(self, values):
        vals = sorted(set(float(v) for v in values))
        if not vals:
            return
        cp = self.sb("constpool", [128, len(vals)], F32)
        toks = []
        for i, v in enumerate(vals):
            toks.append(self.memset(cp[:, i:i + 1], v))
            self.nc.const_aps.register(F32, v, cp[:, i:i + 1])
        for e in self.ENGS:
            self.q[e].append((None, list(toks), 0, None))

    def done(self, tokens):
        self.q["sync"].append((None, Prog._flat(tokens), 0, None))

    def build(self):
        sems = {e: self.ctx.enter_context(self.nc.semaphore(f"sem_{e}")) for e in self.ENGS}
        li = 0
        for k in self.cnt:
            if isinstance(k, tuple):
                sems[k] = self.ctx.enter_context(self.nc.semaphore(f"lane_{li}"))
                li += 1
        block = self.ctx.enter_context(self.nc.Block())

        def body(eng_name):
            def run(engine):
                for fn, deps, inc, lane in self.q[eng_name]:
                    best = {}
                    for (de, dv) in deps:
                        if de == eng_name and eng_name == "tensor":
                            continue
                        if de == eng_name and fn is None:
                            continue
                        best[de] = max(best.get(de, 0), dv)
                    for de in sorted(best, key=str):
                        engine.wait_ge(sems[de], best[de])
                    if fn is not None:
                        fn(engine).then_inc(sems[lane if lane is not None else eng_name], inc)
            return run

        block.sync(body("sync"))
        block.vector(body("vector"))
        block.scalar(body("scalar"))
        block.tensor(body("tensor"))
        block.gpsimd(body("gpsimd"))
        self.ctx.close()
        return self.nc


def conv_weight_slices(p, name, src_ap, n_slices, cols, rot=2, dt_in=F32, ctx=None):
    al = (lambda nm, sh, dt: ctx.enter_context(p.nc.sbuf_tensor(nm, sh, dt))) if ctx is not None else p.sb
    dst = al(name + "_bf", [128, n_slices * cols], BF16)
    tmp = al(name + "_tmp", [128, rot * cols], dt_in)
    toks = []
    prev = [None] * rot
    for k in range(n_slices):
        r = k % rot
        tl = p.dma(tmp[:, r * cols:(r + 1) * cols], src_ap[k * 128:(k + 1) * 128, :], deps=[prev[r]], key=(name, r))
        tc = p.copy(dst[:, k * cols:(k + 1) * cols], tmp[:, r * cols:(r + 1) * cols], deps=[tl])
        prev[r] = tc
        toks.append(tc)
    return dst, toks


def emit_ln(p, x_view, T, tag, x_deps, pa_row, pd_rep, xn_bf, extra_c0=None, ctx=None):
    al = (lambda nm, sh, dt: ctx.enter_context(p.nc.sbuf_tensor(nm, sh, dt))) if ctx is not None else p.sb
    onec_b = al(f"{tag}onecb", [128, 1], BF16)
    onec_f = al(f"{tag}onecf", [128, 1], F32)
    onerow_f = al(f"{tag}onerf", [1, 128], F32)
    t1 = p.memset(onec_b[:], 1.0)
    t2 = p.memset(onec_f[:], 1.0)
    t2b = p.memset(onerow_f[:], 1.0)
    sq = al(f"{tag}sq", [128, 2 * T], BF16)
    mean = al(f"{tag}mean", [1, T], F32)
    rstd = al(f"{tag}rstd", [1, T], F32)
    mrs = al(f"{tag}mrs", [1, T], F32)
    reps = al(f"{tag}reps", [128, 2 * T], F32)

    pa_free = None
    sum_toks = []
    for c in range(0, T, 512):
        for k in range(DK):
            tm = p.mm(pa_row[0:1, 0:512],
                      onec_f[:], x_view(k)[:, c:c + 512], start=(k == 0), stop=(k == DK - 1),
                      deps=[t2, x_deps[k], pa_free])
            sum_toks.append(tm)
        te = p.copy(mean[0:1, c:c + 512], pa_row[0:1, 0:512], deps=[sum_toks[-1]], eng="vector")
        pa_free = te
        sum_toks.append(te)
    t_mean = p.ts(mean[:], mean[:], float(1.0 / D), ALU.mult, deps=[sum_toks[-1]])

    var_toks = []
    last_use = [None, None]
    for c in range(0, T, 512):
        for k in range(DK):
            r = k % 2
            tq = p.act(sq[:, r * T:r * T + 512], x_view(k)[:, c:c + 512], AF.Square,
                       deps=[x_deps[k], last_use[r]])
            tm = p.mm(pa_row[0:1, 0:512], onec_b[:], sq[:, r * T:r * T + 512],
                      start=(k == 0), stop=(k == DK - 1), deps=[t1, tq, pa_free])
            last_use[r] = tm
            var_toks.append(tm)
        te = p.copy(rstd[0:1, c:c + 512], pa_row[0:1, 0:512], deps=[var_toks[-1]], eng="vector")
        pa_free = te
        var_toks.append(te)
    tv1 = p.stt(mrs[:], mean[:], -1.0, mean[:], ALU.mult, ALU.mult, deps=[t_mean, var_toks[-1]])
    tv2 = p.stt(rstd[:], rstd[:], float(1.0 / D), mrs[:], ALU.mult, ALU.add, deps=[tv1, var_toks[-1]])
    tv3 = p.act(rstd[:], rstd[:], AF.Ln, bias=float(LN_EPS), deps=[tv2])
    tv4 = p.act(rstd[:], rstd[:], AF.Exp, scale=-0.5, deps=[tv3])
    tv5 = p.stt(mrs[:], mean[:], -1.0, rstd[:], ALU.mult, ALU.mult, deps=[tv4, tv1])
    rep_toks = []
    for (row, base) in ((rstd, 0), (mrs, T)):
        for c in range(0, T, 512):
            tm = p.mm(pd_rep[:, 0:512], onerow_f[:], row[0:1, c:c + 512], start=True, stop=True,
                      deps=[tv4, tv5, t2b] + ([rep_toks[-1]] if rep_toks else []))
            te = p.copy(reps[:, base + c:base + c + 512], pd_rep[:, 0:512], deps=[tm], eng="vector")
            rep_toks.append(te)
    t_rep = rep_toks[-1]
    out_toks = []
    for k in range(DK):
        ta = p.tt(xn_bf[:, k * T:(k + 1) * T], x_view(k), reps[:, 0:T], ALU.mult,
                  deps=[t_rep, x_deps[k]] + rep_toks)
        tb = p.tt(xn_bf[:, k * T:(k + 1) * T], xn_bf[:, k * T:(k + 1) * T], reps[:, T:2 * T], ALU.add,
                  deps=[ta])
        out_toks.append(tb)
    tc0 = None
    if extra_c0 is not None:
        cts = []
        for k in range(DK):
            tx = p.tt(extra_c0[:, k:k + 1], x_view(k)[:, 0:1], reps[:, 0:1], ALU.mult,
                      deps=[t_rep, x_deps[k]])
            ty = p.tt(extra_c0[:, k:k + 1], extra_c0[:, k:k + 1], reps[:, T:T + 1], ALU.add, deps=[tx])
            cts.append(ty)
        tc0 = cts
    return out_toks, tc0


def build_l1():
    p = Prog()
    T = HALF
    xin = p.din("xT", [D, T], F32)
    wq_in = p.din("Wq", [D, D], F32)
    wk_in = p.din("Wk", [D, D], F32)
    wv_in = p.din("Wv", [D, D], F32)
    wg_in = p.din("wg_t", [D, H3], I32)
    wu_in = p.din("wu_t", [D, H3], I32)
    wd_in = p.din("wd_t", [H3, D], I32)

    ctx_out = p.dout("ctxp", [D, D], F32)
    kden_out = p.dout("kdenp", [128, DK], F32)
    qexp_out = p.dout("qexp", [D, T], BF16)
    qden_out = p.dout("qden", [1, T], F32)
    xpf_out = p.dout("xpf", [D, T], F32)

    wg_i = p.dtmp("wg_i", [DK, 128, H3], BF16)
    wu_i = p.dtmp("wu_i", [DK, 128, H3], BF16)
    wd_i = p.dtmp("wd_i", [HT3, 128, D], BF16)

    PA = p.ps("PA", [128, 1024])
    PB = p.ps("PB", [128, 1024])
    PC = p.ps("PC", [128, 1024])
    PD = p.ps("PD", [128, 512])
    PE = p.ps("PE", [128, 512])
    p.consts([LN_EPS, -0.5, QSC])

    out_toks = []

    scA = ExitStack()
    xn = p.sb("xn", [128, DK * T], BF16)
    x_sb = scA.enter_context(p.nc.sbuf_tensor("x", [128, DK * T], F32))
    tx = [p.dma(x_sb[:, k * T:(k + 1) * T], xin[k * 128:(k + 1) * 128, :], key=("x", k)) for k in range(DK)]
    xn_toks, _ = emit_ln(p, lambda k: x_sb[:, k * T:(k + 1) * T], T, "ln", tx, PD, PE, xn, ctx=scA)
    p.barrier()
    scA.close()

    scB = ExitStack()
    wq_sb, twq = conv_weight_slices(p, "wq", wq_in, DK, 1024, ctx=scB)
    wk_sb, twk = conv_weight_slices(p, "wk", wk_in, DK, 1024, ctx=scB)
    wv_sb, twv = conv_weight_slices(p, "wv", wv_in, DK, 1024, ctx=scB)

    qexp_bf = scB.enter_context(p.nc.sbuf_tensor("qexp_s", [128, 2 * T], BF16))
    qden_sb = scB.enter_context(p.nc.sbuf_tensor("qden_s", [1, T], F32))
    tqden_z = p.memset(qden_sb[:], 0.0, eng="vector")
    onec_b = scB.enter_context(p.nc.sbuf_tensor("onecb2", [128, 1], BF16))
    t_oc = p.memset(onec_b[:], 1.0)
    q_prev = [None, None]
    qd_prev = tqden_z
    pa_free = None
    pb_free = None
    for o in range(DK):
        mm_toks = []
        for th in range(2):
            for k in range(DK):
                tm = p.mm(PA[:, 0:1024] if th == 0 else PB[:, 0:1024],
                          wq_sb[:, k * 1024 + o * 128:k * 1024 + (o + 1) * 128],
                          xn[:, k * T + th * 1024:k * T + (th + 1) * 1024],
                          start=(k == 0), stop=(k == DK - 1),
                          deps=[twq[k], xn_toks[k], pa_free if th == 0 else pb_free])
                mm_toks.append(tm)
        r = o % 2
        te1 = p.act(qexp_bf[:, r * T:r * T + 1024], PA[:, 0:1024], AF.Exp,
                    deps=[mm_toks[DK - 1], q_prev[r]])
        te2 = p.act(qexp_bf[:, r * T + 1024:r * T + 2048], PB[:, 0:1024], AF.Exp,
                    deps=[mm_toks[2 * DK - 1], te1])
        pa_free = te1
        pb_free = te2
        tden = []
        for c in range(0, T, 512):
            tm = p.mm(PD[0:1, 0:512], onec_b[:], qexp_bf[:, r * T + c:r * T + c + 512],
                      start=True, stop=True, deps=[t_oc, te1, te2] + ([tden[-1]] if tden else []))
            ta = p.tt(qden_sb[0:1, c:c + 512], qden_sb[0:1, c:c + 512], PD[0:1, 0:512], ALU.add,
                      deps=[tm, qd_prev])
            tden.append(ta)
        qd_prev = tden[-1]
        tdma = p.dma(qexp_out[o * 128:(o + 1) * 128, :], qexp_bf[:, r * T:(r + 1) * T],
                     deps=[te1, te2], key=("qexpo", r))
        q_prev[r] = tdma
        out_toks.append(tdma)
    out_toks.append(p.dma(qden_out, qden_sb[:], deps=[qd_prev], key=("qdeno",)))

    kbuf = scB.enter_context(p.nc.sbuf_tensor("kbuf", [128, 4 * 1024], BF16))
    vbuf = scB.enter_context(p.nc.sbuf_tensor("vbuf", [128, 4 * 1024], BF16))
    ones_col = scB.enter_context(p.nc.sbuf_tensor("onescol", [128, 1], BF16))
    t_ocol = p.memset(ones_col[:], 1.0)
    ctx_sb = scB.enter_context(p.nc.sbuf_tensor("ctx", [128, DK * 1024], F32))
    tz_ctx = [p.memset(ctx_sb[:, dt * 1024:(dt + 1) * 1024], 0.0, eng="vector") for dt in range(DK)]
    kv_prev = {}
    kden_toks = []
    ctx_toks = {dt: tz_ctx[dt] for dt in range(DK)}
    pa_free = pa_free
    pb_free = pb_free
    pc_free = None
    n_tiles = T // 128
    for g in range(n_tiles // 4):
        tile_toks = []
        for tt in range(4):
            tile = g * 4 + tt
            mmk = []
            mmv = []
            for k in range(DK):
                tm = p.mm(PA[:, 0:1024], xn[:, k * T + tile * 128:k * T + (tile + 1) * 128],
                          wk_sb[:, k * 1024:(k + 1) * 1024], start=(k == 0), stop=(k == DK - 1),
                          deps=[twk[k], xn_toks[k], pa_free])
                mmk.append(tm)
                tm = p.mm(PB[:, 0:1024], xn[:, k * T + tile * 128:k * T + (tile + 1) * 128],
                          wv_sb[:, k * 1024:(k + 1) * 1024], start=(k == 0), stop=(k == DK - 1),
                          deps=[twv[k], xn_toks[k], pb_free])
                mmv.append(tm)
            tke = p.act(kbuf[:, tt * 1024:(tt + 1) * 1024], PA[:, 0:1024], AF.Exp,
                        deps=[mmk[-1], kv_prev.get(("k", tt))])
            tve = p.copy(vbuf[:, tt * 1024:(tt + 1) * 1024], PB[:, 0:1024],
                         deps=[mmv[-1], kv_prev.get(("v", tt))], eng="vector")
            pa_free = tke
            pb_free = tve
            tile_toks.append((tile, tke, tve))
        for dt in range(DK):
            gm = []
            for (tile, tke, tve) in tile_toks:
                tt = tile % 4
                tm = p.mm(PC[:, 0:1024], kbuf[:, tt * 1024 + dt * 128:tt * 1024 + (dt + 1) * 128],
                          vbuf[:, tt * 1024:(tt + 1) * 1024],
                          start=(tt == 0), stop=(tt == 3), deps=[tke, tve, pc_free])
                gm.append(tm)
                tkd = p.mm(PE[:, dt:dt + 1], kbuf[:, tt * 1024 + dt * 128:tt * 1024 + (dt + 1) * 128],
                           ones_col[:], start=(g == 0 and tt == 0), stop=(g == n_tiles // 4 - 1 and tt == 3),
                           deps=[tke, t_ocol])
                kden_toks.append(tkd)
            ta = p.tt(ctx_sb[:, dt * 1024:(dt + 1) * 1024], ctx_sb[:, dt * 1024:(dt + 1) * 1024],
                      PC[:, 0:1024], ALU.add, deps=[gm[-1], ctx_toks[dt]])
            ctx_toks[dt] = ta
            pc_free = ta
        for (tile, tke, tve) in tile_toks:
            kv_prev[("k", tile % 4)] = (ctx_toks[DK - 1], kden_toks[-1])
            kv_prev[("v", tile % 4)] = (ctx_toks[DK - 1],)
    kden_sb = scB.enter_context(p.nc.sbuf_tensor("kden", [128, DK], F32))
    tkd_e = p.copy(kden_sb[:], PE[:, 0:DK], deps=[kden_toks[-1]], eng="vector")
    out_toks.append(p.dma(kden_out, kden_sb[:], deps=[tkd_e], key=("kdeno",)))
    for dt in range(DK):
        out_toks.append(p.dma(ctx_out[dt * 128:(dt + 1) * 128, :], ctx_sb[:, dt * 1024:(dt + 1) * 1024],
                              deps=[ctx_toks[dt]], key=("ctxo",)))
    p.barrier()
    scB.close()

    scC = ExitStack()
    witmp = scC.enter_context(p.nc.sbuf_tensor("witmp", [128, 2 * H3], I32))
    wbtmp = scC.enter_context(p.nc.sbuf_tensor("wbtmp", [128, 2 * H3], BF16))
    pc_prev = [None, None]
    pre_toks = []
    for (src, dst, nsl, cols) in ((wg_in, wg_i, DK, H3), (wu_in, wu_i, DK, H3), (wd_in, wd_i, HT3, D)):
        for k in range(nsl):
            r = k % 2
            tl = p.dma(witmp[:, r * H3:r * H3 + cols], src[k * 128:(k + 1) * 128, :], deps=[pc_prev[r]], key=("wi", r))
            tc = p.copy(wbtmp[:, r * H3:r * H3 + cols], witmp[:, r * H3:r * H3 + cols], deps=[tl])
            td = p.dma(dst[k, :, :], wbtmp[:, r * H3:r * H3 + cols], deps=[tc], key=("wst", r))
            pc_prev[r] = td
            pre_toks.append(td)
    pre_tails = [pc_prev[0], pc_prev[1]]
    p.barrier()
    scC.close()

    hact = p.sb("hact", [128, HT3 * T], BF16)
    gbuf = p.sb("gbuf", [128, 2 * 1024], BF16)
    wg_sb = p.sb("wg_sb", [128, 2 * 1024], BF16)
    wu_sb = p.sb("wu_sb", [128, 2 * 1024], BF16)
    wd_sb = p.sb("wd_sb", [128, 2 * H3], BF16)
    xtmp = p.sb("xtmp", [128, 2 * T], F32)
    ftmp = p.sb("ftmp", [128, 2 * T], F32)
    wgu_prev = [None, None]
    wd_prev = [None, None]
    x_prev = [None, None]
    f_prev = [None, None]
    hact_prev = {}
    pa_free2 = None
    pb_free2 = None
    pc_free2 = None
    gb_prev = [None, None]
    for ht in range(HT3):
        r = ht % 2
        tg = p.dma(wg_sb[:, r * 1024:(r + 1) * 1024],
                   wg_i[:, :, ht * 128:(ht + 1) * 128].rearrange("k p c -> p k c"),
                   deps=[wgu_prev[r]] + pre_tails, key=("wg", r))
        tu = p.dma(wu_sb[:, r * 1024:(r + 1) * 1024],
                   wu_i[:, :, ht * 128:(ht + 1) * 128].rearrange("k p c -> p k c"),
                   deps=[wgu_prev[r]] + pre_tails, key=("wu", r))
        for th in range(2):
            c0 = th * 1024
            mg = []
            mu = []
            for k in range(DK):
                tm = p.mm(PA[:, 0:1024], wg_sb[:, r * 1024 + k * 128:r * 1024 + (k + 1) * 128],
                          xn[:, k * T + c0:k * T + c0 + 1024], start=(k == 0), stop=(k == DK - 1),
                          deps=[tg, xn_toks[k], pa_free2])
                mg.append(tm)
                tm = p.mm(PB[:, 0:1024], wu_sb[:, r * 1024 + k * 128:r * 1024 + (k + 1) * 128],
                          xn[:, k * T + c0:k * T + c0 + 1024], start=(k == 0), stop=(k == DK - 1),
                          deps=[tu, xn_toks[k], pb_free2])
                mu.append(tm)
            tsil = p.act(gbuf[:, r * 1024:(r + 1) * 1024], PA[:, 0:1024], AF.Silu, scale=QSC,
                         deps=[mg[-1], gb_prev[r]])
            thh = p.tt(hact[:, ht * T + c0:ht * T + c0 + 1024], gbuf[:, r * 1024:(r + 1) * 1024],
                       PB[:, 0:1024], ALU.mult,
                       deps=[tsil, mu[-1], hact_prev.get((ht, th))])
            pa_free2 = tsil
            pb_free2 = thh
            gb_prev[r] = thh
            hact_prev[(ht, th)] = thh
            wgu_prev[r] = thh
    for dt in range(DK):
        r = dt % 2
        td = p.dma(wd_sb[:, r * H3:(r + 1) * H3],
                   wd_i[:, :, dt * 128:(dt + 1) * 128].rearrange("k p c -> p k c"),
                   deps=[wd_prev[r]] + pre_tails, key=("wd", r))
        txl = p.dma(xtmp[:, r * T:(r + 1) * T], xin[dt * 128:(dt + 1) * 128, :],
                    deps=[x_prev[r]], key=("xt", r))
        tf_parts = []
        for th in range(2):
            c0 = th * 1024
            mf = []
            for ht in range(HT3):
                tm = p.mm(PC[:, 0:1024], wd_sb[:, r * H3 + ht * 128:r * H3 + (ht + 1) * 128],
                          hact[:, ht * T + c0:ht * T + c0 + 1024], start=(ht == 0), stop=(ht == HT3 - 1),
                          deps=[td, hact_prev[(ht, th)], pc_free2])
                mf.append(tm)
            tf = p.stt(ftmp[:, r * T + c0:r * T + c0 + 1024], PC[:, 0:1024], float(QSC * QSC),
                       xtmp[:, r * T + c0:r * T + c0 + 1024], ALU.mult, ALU.add,
                       deps=[mf[-1], txl, f_prev[r]])
            pc_free2 = tf
            tf_parts.append(tf)
        tdma = p.dma(xpf_out[dt * 128:(dt + 1) * 128, :], ftmp[:, r * T:(r + 1) * T],
                     deps=tf_parts, key=("xpfo", r))
        x_prev[r] = tf_parts[-1]
        f_prev[r] = tdma
        wd_prev[r] = tdma
        out_toks.append(tdma)

    p.done(out_toks)
    return p.build()


def build_l2():
    p = Prog()
    T = HALF
    ctx0_in = p.din("ctxp0", [D, D], F32)
    ctx1_in = p.din("ctxp1", [D, D], F32)
    kd0_in = p.din("kden0", [128, DK], F32)
    kd1_in = p.din("kden1", [128, DK], F32)
    qexp_in = p.din("qexp", [D, T], BF16)
    qden_in = p.din("qden", [1, T], F32)
    xpf_in = p.din("xpf", [D, T], F32)
    wo_in = p.din("Wo", [D, D], F32)
    bo_in = p.din("bo_pm", [128, DK], F32)

    x1_out = p.dout("x1", [D, T], F32)
    xn1_out = p.dout("xn1", [D, T], BF16)
    xc0_out = p.dout("xn1c0", [128, DK], F32)

    p.consts([LN_EPS, -0.5, -1.0, float(-np.log(D))])
    PA = p.ps("PA", [128, 1024])
    PB = p.ps("PB", [128, 1024])
    PC = p.ps("PC", [128, 1024])
    PD = p.ps("PD", [128, 512])
    PE = p.ps("PE", [128, 512])
    out_toks = []

    x1_sb = p.sb("x1_s", [128, DK * T], F32)
    scB = ExitStack()

    kd_sb = scB.enter_context(p.nc.sbuf_tensor("kd", [128, 2 * DK], F32))
    t1 = p.dma(kd_sb[:, 0:DK], kd0_in, key=("kd0",))
    t2 = p.dma(kd_sb[:, DK:2 * DK], kd1_in, key=("kd1",))
    t3 = p.tt(kd_sb[:, 0:DK], kd_sb[:, 0:DK], kd_sb[:, DK:2 * DK], ALU.add, deps=[t1, t2])
    t4 = p.recip(kd_sb[:, DK:2 * DK], kd_sb[:, 0:DK], deps=[t3])

    ctxn = scB.enter_context(p.nc.sbuf_tensor("ctxn", [128, DK * 1024], BF16))
    y_sb = scB.enter_context(p.nc.sbuf_tensor("y", [128, DK * T], BF16))
    scB2 = ExitStack()
    c_tmp = scB2.enter_context(p.nc.sbuf_tensor("ctmp", [128, 2 * 2048], F32))
    c_prev = [None, None]
    ctxn_toks = []
    for dt in range(DK):
        r = dt % 2
        ta = p.dma(c_tmp[:, r * 2048:r * 2048 + 1024], ctx0_in[dt * 128:(dt + 1) * 128, :], deps=[c_prev[r]], key=("c0", r))
        tb = p.dma(c_tmp[:, r * 2048 + 1024:r * 2048 + 2048], ctx1_in[dt * 128:(dt + 1) * 128, :],
                   deps=[c_prev[r]], key=("c1", r))
        tsum = p.tt(c_tmp[:, r * 2048:r * 2048 + 1024], c_tmp[:, r * 2048:r * 2048 + 1024],
                    c_tmp[:, r * 2048 + 1024:r * 2048 + 2048], ALU.add, deps=[ta, tb])
        tn = p.ts2(ctxn[:, dt * 1024:(dt + 1) * 1024], c_tmp[:, r * 2048:r * 2048 + 1024],
                   kd_sb[:, DK + dt:DK + dt + 1], 0.125, ALU.mult, ALU.mult, deps=[tsum, t4])
        c_prev[r] = tn
        ctxn_toks.append(tn)

    p.barrier()
    scB2.close()
    scB3 = ExitStack()
    qexp = scB3.enter_context(p.nc.sbuf_tensor("qexp_s", [128, DK * T], BF16))
    tq = [p.dma(qexp[:, k * T:(k + 1) * T], qexp_in[k * 128:(k + 1) * 128, :], key=("qe", k)) for k in range(DK)]

    y_toks = []
    pab_free = [None, None]
    for et in range(DK):
        for th in range(2):
            pa = PA if th == 0 else PB
            mt = []
            for k in range(DK):
                tm = p.mm(pa[:, 0:1024], ctxn[:, k * 1024 + et * 128:k * 1024 + (et + 1) * 128],
                          qexp[:, k * T + th * 1024:k * T + (th + 1) * 1024],
                          start=(k == 0), stop=(k == DK - 1),
                          deps=[ctxn_toks[k], tq[k], pab_free[th]])
                mt.append(tm)
            te = p.copy(y_sb[:, et * T + th * 1024:et * T + (th + 1) * 1024], pa[:, 0:1024],
                        deps=[mt[-1]] + ([y_toks[-1]] if y_toks else []), eng="vector")
            pab_free[th] = te
            y_toks.append(te)

    p.barrier()
    scB3.close()
    wo_sb, two = conv_weight_slices(p, "wo", wo_in, DK, 1024, ctx=scB)
    bo_sb = scB.enter_context(p.nc.sbuf_tensor("bo", [128, DK], F32))
    tbo = p.dma(bo_sb[:], bo_in, key=("bo",))

    qinv = scB.enter_context(p.nc.sbuf_tensor("qinv_s", [1, T], F32))
    tqd = p.dma(qinv[:], qden_in, key=("qd",))
    tq1 = p.act(qinv[:], qinv[:], AF.Ln, deps=[tqd])
    tq2 = p.act(qinv[:], qinv[:], AF.Exp, scale=-1.0, bias=float(-np.log(D)), deps=[tq1])
    onec_f = scB.enter_context(p.nc.sbuf_tensor("onecf3", [1, 128], F32))
    tof = p.memset(onec_f[:], 1.0)
    qrep = scB.enter_context(p.nc.sbuf_tensor("qrep", [128, T], F32))
    qr_toks = []
    for c in range(0, T, 512):
        tm = p.mm(PD[:, 0:512], onec_f[:], qinv[0:1, c:c + 512], start=True, stop=True,
                  deps=[tq2, tof] + ([qr_toks[-1]] if qr_toks else []))
        te = p.copy(qrep[:, c:c + 512], PD[:, 0:512], deps=[tm], eng="vector")
        qr_toks.append(te)

    xpf_sb = scB.enter_context(p.nc.sbuf_tensor("xpf_s", [128, 2 * T], F32))
    xp_prev = [None, None]
    x1_toks = []
    pab_free = [y_toks[-1], y_toks[-1]]
    for ot in range(DK):
        r = ot % 2
        mt = []
        for th in range(2):
            pa = PA if th == 0 else PB
            for k in range(DK):
                tm = p.mm(pa[:, 0:1024], wo_sb[:, k * 1024 + ot * 128:k * 1024 + (ot + 1) * 128],
                          y_sb[:, k * T + th * 1024:k * T + (th + 1) * 1024],
                          start=(k == 0), stop=(k == DK - 1),
                          deps=[two[k], y_toks[2 * k + th], pab_free[th]])
                mt.append(tm)
        txp = p.dma(xpf_sb[:, r * T:(r + 1) * T], xpf_in[ot * 128:(ot + 1) * 128, :], deps=[xp_prev[r]], key=("xp", r))
        ta1 = p.tt(x1_sb[:, ot * T:ot * T + 1024], PA[:, 0:1024], qrep[:, 0:1024], ALU.mult,
                   deps=[mt[DK - 1], qr_toks[-1]])
        ta2 = p.tt(x1_sb[:, ot * T + 1024:ot * T + 2048], PB[:, 0:1024], qrep[:, 1024:2048], ALU.mult,
                   deps=[mt[-1], qr_toks[-1], ta1])
        tx1 = p.stt(x1_sb[:, ot * T:(ot + 1) * T], x1_sb[:, ot * T:(ot + 1) * T],
                    bo_sb[:, ot:ot + 1], xpf_sb[:, r * T:(r + 1) * T], ALU.add, ALU.add,
                    deps=[ta1, ta2, txp, tbo])
        pab_free = [ta1, ta2]
        xp_prev[r] = tx1
        x1_toks.append(tx1)
        out_toks.append(p.dma(x1_out[ot * 128:(ot + 1) * 128, :], x1_sb[:, ot * T:(ot + 1) * T],
                              deps=[tx1], key=("x1o",)))

    p.barrier()
    scB.close()
    scC = ExitStack()
    xn1 = p.sb("xn1_s", [128, DK * T], BF16)
    xc0 = p.sb("xc0_s", [128, DK], F32)
    xn1_toks, tc0 = emit_ln(p, lambda k: x1_sb[:, k * T:(k + 1) * T], T, "ln1", x1_toks, PD, PE, xn1,
                            extra_c0=xc0, ctx=scC)
    for k in range(DK):
        out_toks.append(p.dma(xn1_out[k * 128:(k + 1) * 128, :], xn1[:, k * T:(k + 1) * T],
                              deps=[xn1_toks[k]], key=("xn1o",)))
    out_toks.append(p.dma(xc0_out, xc0[:], deps=tc0, key=("xc0o",)))
    scC.close()

    p.done(out_toks)
    return p.build()


def build_l3(debug=False):
    p = Prog()
    T = N
    xn_in = p.din("xnf", [D, T], BF16)
    inw_in = p.din("inW_t", [D, 2 * DI], F32)
    outw_in = p.din("outW_t", [DI, D], F32)
    xpw_in = p.din("xpW_t", [DI, 96], F32)
    dtw_in = p.din("dtW_t", [DTR, DI], F32)
    alog_in = p.din("A_lay", [128, SI * DS], F32)
    dtb_in = p.din("dtB_pm", [128, SI], F32)
    cvb_in = p.din("convB_pm", [128, SI], F32)
    dp_in = p.din("Dp_pm", [128, SI], F32)
    cd_in = p.din("convdiag", [SI, DC, 128, 128], F32)
    sel_in = p.din("sel_bc", [128, 32 * 128], F32)
    id_in = p.din("ident", [128, 128], F32)

    mout = p.dout("mout", [D, T], F32)
    if debug:
        dbg_u = p.dout("dbg_u", [128, SI * TCH], BF16)
        dbg_xdbl = p.dout("dbg_xdbl", [128, TCH], BF16)
        dbg_delta = p.dout("dbg_delta", [128, TCH], BF16)
        dbg_da = p.dout("dbg_da", [128, TCH], F32)
        dbg_b = p.dout("dbg_b", [128, TCH], F32)
        dbg_h = p.dout("dbg_h", [128, TCH], F32)
        dbg_brep = p.dout("dbg_brep", [128, TCH], BF16)
        dbg_y = p.dout("dbg_y", [128, SI * TCH], BF16)

    inw_i = p.dtmp("inw_i", [DK, 128, 2 * DI], BF16)
    outw_i = p.dtmp("outw_i", [SI, 128, D], BF16)
    cd_i = p.dtmp("cd_i", [SI, DC * 128, 128], BF16)

    PA = p.ps("PA", [128, 1024])
    PB = p.ps("PB", [128, 1024])
    PC = p.ps("PC", [128, 1024])
    PD = p.ps("PD", [128, 512])
    PE = p.ps("PE", [128, 512])
    p.consts([-1.0, 1.0])
    out_toks = []

    dtw_sb = p.sb("dtw", [64, DI], BF16)
    xpw_sb = p.sb("xpw", [128, SI * 96], BF16)
    sel_sb = p.sb("selbc", [128, 32 * 128], BF16)
    id_sb = p.sb("identb", [128, 128], BF16)
    scP = ExitStack()
    tmp_f = scP.enter_context(p.nc.sbuf_tensor("tmpf", [128, 2 * 4096], F32))
    tmp_b = scP.enter_context(p.nc.sbuf_tensor("tmpb", [128, 2 * 4096], BF16))
    prev = [None, None]
    pre_toks = []
    for k in range(DK):
        r = k % 2
        tl = p.dma(tmp_f[:, r * 4096:(r + 1) * 4096], inw_in[k * 128:(k + 1) * 128, :], deps=[prev[r]], key=("pre", r))
        tc = p.copy(tmp_b[:, r * 4096:(r + 1) * 4096], tmp_f[:, r * 4096:(r + 1) * 4096], deps=[tl])
        td = p.dma(inw_i[k, :, :], tmp_b[:, r * 4096:(r + 1) * 4096], deps=[tc], key=("prest", r))
        prev[r] = td
        pre_toks.append(td)
    for s in range(SI):
        r = s % 2
        tl = p.dma(tmp_f[:, r * 4096:r * 4096 + D], outw_in[s * 128:(s + 1) * 128, :], deps=[prev[r]], key=("pre", r))
        tc = p.copy(tmp_b[:, r * 4096:r * 4096 + D], tmp_f[:, r * 4096:r * 4096 + D], deps=[tl])
        td = p.dma(outw_i[s, :, :], tmp_b[:, r * 4096:r * 4096 + D], deps=[tc], key=("prest", r))
        prev[r] = td
        pre_toks.append(td)
    for s in range(SI):
        r = s % 2
        tl = p.dma(tmp_f[:, r * 4096:r * 4096 + 512],
                   cd_in[s, :, :, :].rearrange("j p c -> p j c"), deps=[prev[r]], key=("pre", r))
        tc = p.copy(tmp_b[:, r * 4096:r * 4096 + 512], tmp_f[:, r * 4096:r * 4096 + 512], deps=[tl])
        td = p.dma(cd_i[s, :, :].rearrange("(j p) c -> p j c", p=128), tmp_b[:, r * 4096:r * 4096 + 512],
                   deps=[tc], key=("prest", r))
        prev[r] = td
        pre_toks.append(td)
    dtw_f = scP.enter_context(p.nc.sbuf_tensor("dtwf", [64, DI], F32))
    t_dtw0 = p.dma(dtw_f[:], dtw_in, key=("dtwl",))
    t_dtw = p.copy(dtw_sb[:], dtw_f[:], deps=[t_dtw0])
    t_pre = [t for t in (prev[0], prev[1]) if t is not None] + pre_toks[-2:]
    tsel0 = p.dma(tmp_f[:, 0:4096], sel_in, deps=[prev[0]], key=("pre", 0))
    t_sel = p.copy(sel_sb[:], tmp_f[:, 0:4096], deps=[tsel0])
    prev[0] = t_sel
    tid0 = p.dma(tmp_f[:, 0:128], id_in, deps=[prev[0]], key=("pre", 0))
    t_id = p.copy(id_sb[:], tmp_f[:, 0:128], deps=[tid0])
    prev[0] = t_id
    t_xpw = []
    for s in range(SI):
        r = s % 2
        tl = p.dma(tmp_f[:, r * 4096:r * 4096 + 96], xpw_in[s * 128:(s + 1) * 128, :], deps=[prev[r]], key=("pre", r))
        tc = p.copy(xpw_sb[:, s * 96:(s + 1) * 96], tmp_f[:, r * 4096:r * 4096 + 96], deps=[tl])
        prev[r] = tc
        t_xpw.append(tc)
    p.barrier()
    scP.close()

    a_sb = p.sb("a_lay", [128, SI * DS], F32)
    t_a0 = p.dma(a_sb[:], alog_in, key=("a0",))
    t_a1 = p.act(a_sb[:], a_sb[:], AF.Exp, deps=[t_a0])
    t_a = p.ts(a_sb[:], a_sb[:], -1.0, ALU.mult, deps=[t_a1])
    dtb_sb = p.sb("dtb", [128, SI], F32)
    t_dtb = p.dma(dtb_sb[:], dtb_in, key=("dtb",))
    cvb_sb = p.sb("cvb", [128, SI], F32)
    t_cvb = p.dma(cvb_sb[:], cvb_in, key=("cvb",))
    dp_sb = p.sb("dp", [128, SI], F32)
    t_dp = p.dma(dp_sb[:], dp_in, key=("dp",))

    NG = 8
    xnc = p.sb("xnc", [128, DK * TCH], BF16)
    u_sb = p.sb("u", [128, SI * TCH], BF16)
    brep = p.sb("brep", [128, NG * TCH], BF16)
    crep = p.sb("crep", [128, NG * TCH], BF16)
    yacc = p.sb("yacc", [128, SI * TCH], BF16)
    xdbl = p.sb("xdbl", [128, TCH], BF16)
    xi_sb = p.sb("xi", [128, 2 * (TCH + 3)], BF16)
    carry = p.sb("carry", [128, SI * DS], F32)
    cdiag = p.sb("cdiag", [128, 2 * 512], BF16)
    inws = p.sb("inws", [128, 2 * 1024], BF16)
    outws = p.sb("outws", [128, 2 * 2048], BF16)
    delta = p.sb("delta", [128, TCH], BF16)
    du_sb = p.sb("du", [128, TCH], BF16)
    sp_r = p.sb("spr", [128, TCH], F32)
    sp_b = p.sb("spb", [128, TCH], F32)
    da_f = p.sb("daf", [128, 2 * TCH], BF16)
    b_f = p.sb("bf_", [128, 2 * TCH], BF16)
    h_f = p.sb("hf", [128, 2 * TCH], BF16)
    prod = p.sb("prod", [128, 2 * TCH], BF16)
    zsil = p.sb("zsil", [128, TCH], BF16)
    mo_sb = p.sb("mo", [128, 1024], F32)
    onec_b = p.sb("onecb", [1, 128], BF16)
    t_ocb = p.memset(onec_b[:], 1.0)

    t_carry0 = p.memset(carry[:], 0.0, eng="vector")
    carry_toks = {}
    xi_carry_toks = {}
    xi_tail = p.sb("xitail", [128, SI * 3], BF16)
    t_xit0 = p.memset(xi_tail[:], 0.0, eng="vector")

    xnc_prev = [None] * DK
    u_prev = [None] * SI
    y_prev = [None] * SI
    yacc_fin = [None] * SI
    brep_prev = [None] * NG
    crep_prev = [None] * NG
    xi_prev = [None, None]
    cd_prev = [None, None]
    inws_prev = [None, None]
    outws_prev = [None, None]
    da_prev = [None, None]
    b_prev = [None, None]
    h_prev = [None, None]
    prod_prev = [None, None]
    delta_prev = [None]
    du_prev = [None]
    sp_prev = [None]
    xdbl_prev = [None]
    zsil_prev = [None]
    mo_prev = [None]
    pa_free = None
    pb_free = None
    pc_free = None

    for ch in range(NCH):
        c0 = ch * TCH
        txn = []
        for k in range(DK):
            t = p.dma(xnc[:, k * TCH:(k + 1) * TCH], xn_in[k * 128:(k + 1) * 128, c0:c0 + TCH],
                      deps=[xnc_prev[k]], key=("xnc", k))
            txn.append(t)
        xpw_toks = []
        for s in range(SI):
            r = s % 2
            til = p.dma(inws[:, r * 1024:(r + 1) * 1024],
                        inw_i[:, :, s * 128:(s + 1) * 128].rearrange("k p c -> p k c"),
                        deps=[inws_prev[r], t_pre], key=("inws", r))
            mt = []
            for k in range(DK):
                tm = p.mm(PA[:, 0:TCH], inws[:, r * 1024 + k * 128:r * 1024 + (k + 1) * 128],
                          xnc[:, k * TCH:(k + 1) * TCH], start=(k == 0), stop=(k == DK - 1),
                          deps=[til, txn[k], pa_free])
                mt.append(tm)
            thd = p.copy(xi_sb[:, r * (TCH + 3):r * (TCH + 3) + 3], xi_tail[:, s * 3:(s + 1) * 3],
                         deps=[xi_prev[r], xi_carry_toks.get(s), t_xit0], eng="vector")
            tev = p.copy(xi_sb[:, r * (TCH + 3) + 3:(r + 1) * (TCH + 3)], PA[:, 0:TCH],
                         deps=[mt[-1], xi_prev[r]])
            pa_free = tev
            ttl = p.copy(xi_tail[:, s * 3:(s + 1) * 3],
                         xi_sb[:, r * (TCH + 3) + TCH:r * (TCH + 3) + TCH + 3],
                         deps=[tev], eng="vector")
            xi_carry_toks[s] = ttl
            tcd = p.dma(cdiag[:, (s % 2) * 512:(s % 2) * 512 + 512],
                        cd_i[s, :, :].rearrange("(j p) c -> p j c", p=128),
                        deps=[cd_prev[s % 2], t_pre], key=("cdg", s % 2))
            mc = []
            for j in range(DC):
                tm = p.mm(PB[:, 0:TCH], cdiag[:, (s % 2) * 512 + j * 128:(s % 2) * 512 + (j + 1) * 128],
                          xi_sb[:, r * (TCH + 3) + j:r * (TCH + 3) + j + TCH],
                          start=(j == 0), stop=(j == DC - 1), deps=[tcd, thd, tev, pb_free])
                mc.append(tm)
            cd_prev[s % 2] = mc[-1]
            xi_prev[r] = (mc[-1], ttl)
            tu = p.act(u_sb[:, s * TCH:(s + 1) * TCH], PB[:, 0:TCH], AF.Silu,
                       bias=cvb_sb[:, s:s + 1], deps=[mc[-1], t_cvb, u_prev[s]])
            pb_free = tu
            u_prev[s] = tu
            inws_prev[r] = mt[-1]
            txp = p.mm(PC[0:96, 0:TCH], xpw_sb[:, s * 96:(s + 1) * 96], u_sb[:, s * TCH:(s + 1) * TCH],
                       start=(s == 0), stop=(s == SI - 1), deps=[tu, t_xpw[s], pc_free])
            xpw_toks.append(txp)
            pc_free = None
        txd = p.copy(xdbl[0:96, :], PC[0:96, 0:TCH], deps=[xpw_toks[-1], xdbl_prev[0]])
        xdbl_prev[0] = txd
        pc_free = txd
        if debug and ch == 0:
            out_toks.append(p.dma(dbg_u, u_sb[:], deps=[u_prev[s2] for s2 in range(SI)], key=("dbg",)))
            out_toks.append(p.dma(dbg_xdbl, xdbl[:], deps=[txd], key=("dbg",)))
        for ng in range(2):
            br_toks = [None] * NG
            cr_toks = [None] * NG
            for nl in range(NG):
                n = ng * NG + nl
                tm = p.mm(PA[:, 0:TCH], sel_sb[64:96, n * 128:(n + 1) * 128], xdbl[64:96, :],
                          start=True, stop=True, deps=[txd, t_sel, pa_free])
                te = p.copy(brep[:, nl * TCH:(nl + 1) * TCH], PA[:, 0:TCH],
                            deps=[tm, brep_prev[nl]], eng="vector")
                pa_free = te
                br_toks[nl] = te
                brep_prev[nl] = te
                tm = p.mm(PB[:, 0:TCH], sel_sb[64:96, (16 + n) * 128:(16 + n + 1) * 128], xdbl[64:96, :],
                          start=True, stop=True, deps=[txd, t_sel, pb_free])
                te = p.copy(crep[:, nl * TCH:(nl + 1) * TCH], PB[:, 0:TCH],
                            deps=[tm, crep_prev[nl]], eng="vector")
                pb_free = te
                cr_toks[nl] = te
                crep_prev[nl] = te
            for s in range(SI):
                tdm = p.mm(PA[:, 0:TCH], dtw_sb[:, s * 128:(s + 1) * 128], xdbl[0:64, :],
                           start=True, stop=True, deps=[t_dtw, txd, pa_free])
                t1 = p.act(sp_r[:], PA[:, 0:TCH], AF.Relu, bias=dtb_sb[:, s:s + 1],
                           deps=[tdm, t_dtb, sp_prev[0]])
                t2 = p.act(sp_b[:], PA[:, 0:TCH], AF.Abs, bias=dtb_sb[:, s:s + 1], deps=[tdm, t1])
                pa_free = t2
                t3 = p.act(sp_b[:], sp_b[:], AF.Exp, scale=-1.0, deps=[t2])
                t4 = p.act(sp_b[:], sp_b[:], AF.Ln, bias=1.0, deps=[t3])
                t5 = p.tt(delta[:], sp_r[:], sp_b[:], ALU.add, deps=[t4, t1, delta_prev[0]])
                sp_prev[0] = t5
                t6 = p.tt(du_sb[:], delta[:], u_sb[:, s * TCH:(s + 1) * TCH], ALU.mult,
                          deps=[t5, u_prev[s], du_prev[0]])
                last_da_use = None
                last_tb = None
                ymm = None
                for nl in range(NG):
                    n = ng * NG + nl
                    rr = nl % 2
                    cidx = s * DS + n
                    tda = p.act(da_f[:, rr * TCH:(rr + 1) * TCH], delta[:], AF.Exp,
                                scale=a_sb[:, cidx:cidx + 1], deps=[t5, t_a, da_prev[rr]])
                    tb = p.tt(b_f[:, rr * TCH:(rr + 1) * TCH], du_sb[:],
                              brep[:, nl * TCH:(nl + 1) * TCH],
                              ALU.mult, deps=[t6, br_toks[nl], b_prev[rr]], eng="gpsimd")
                    th = p.scan(h_f[:, rr * TCH:(rr + 1) * TCH], da_f[:, rr * TCH:(rr + 1) * TCH],
                                b_f[:, rr * TCH:(rr + 1) * TCH], carry[:, cidx:cidx + 1],
                                deps=[tda, tb, carry_toks.get(cidx, t_carry0), h_prev[rr]])
                    tcar = p.copy(carry[:, cidx:cidx + 1], h_f[:, rr * TCH + TCH - 1:rr * TCH + TCH],
                                  deps=[th], eng="gpsimd")
                    carry_toks[cidx] = tcar
                    tpr = p.tt(prod[:, rr * TCH:(rr + 1) * TCH], h_f[:, rr * TCH:(rr + 1) * TCH],
                               crep[:, nl * TCH:(nl + 1) * TCH],
                               ALU.mult, deps=[th, cr_toks[nl], prod_prev[rr]])
                    ymm = p.mm(PC[:, 0:TCH], id_sb[:], prod[:, rr * TCH:(rr + 1) * TCH],
                               start=(nl == 0), stop=(nl == NG - 1),
                               deps=[tpr, t_id, pc_free if nl == 0 else None])
                    prod_prev[rr] = ymm
                    da_prev[rr] = th
                    b_prev[rr] = th
                    h_prev[rr] = (tpr, tcar)
                    last_da_use = th
                    last_tb = tb
                if ng == 0:
                    te_y = p.copy(yacc[:, s * TCH:(s + 1) * TCH], PC[:, 0:TCH],
                                  deps=[ymm, y_prev[s]])
                else:
                    te_y = p.tt(yacc[:, s * TCH:(s + 1) * TCH], yacc[:, s * TCH:(s + 1) * TCH],
                                PC[:, 0:TCH], ALU.add, deps=[ymm, yacc_fin[s]])
                pc_free = te_y
                delta_prev[0] = last_da_use
                du_prev[0] = last_tb
                yacc_fin[s] = te_y
        for s in range(SI):
            tud = p.stt(yacc[:, s * TCH:(s + 1) * TCH], u_sb[:, s * TCH:(s + 1) * TCH],
                        dp_sb[:, s:s + 1], yacc[:, s * TCH:(s + 1) * TCH], ALU.mult, ALU.add,
                        deps=[yacc_fin[s], t_dp, u_prev[s]])
            r = s % 2
            til = p.dma(inws[:, r * 1024:(r + 1) * 1024],
                        inw_i[:, :, (SI + s) * 128:(SI + s + 1) * 128].rearrange("k p c -> p k c"),
                        deps=[inws_prev[r], t_pre], key=("inws", r))
            mt = []
            for k in range(DK):
                tm = p.mm(PB[:, 0:TCH], inws[:, r * 1024 + k * 128:r * 1024 + (k + 1) * 128],
                          xnc[:, k * TCH:(k + 1) * TCH], start=(k == 0), stop=(k == DK - 1),
                          deps=[til, txn[k], pb_free])
                mt.append(tm)
            inws_prev[r] = mt[-1]
            tz = p.act(zsil[:], PB[:, 0:TCH], AF.Silu, deps=[mt[-1], zsil_prev[0]])
            pb_free = tz
            ty = p.tt(yacc[:, s * TCH:(s + 1) * TCH], yacc[:, s * TCH:(s + 1) * TCH], zsil[:],
                      ALU.mult, deps=[tz, tud])
            zsil_prev[0] = ty
            y_prev[s] = ty
        if debug and ch == 0:
            out_toks.append(p.dma(dbg_y, yacc[:], deps=[y_prev[s2] for s2 in range(SI)], key=("dbg",)))
        for ot in range(DK):
            r = ot % 2
            tol = p.dma(outws[:, r * 2048:(r + 1) * 2048],
                        outw_i[:, :, ot * 128:(ot + 1) * 128].rearrange("s p c -> p s c"),
                        deps=[outws_prev[r], t_pre], key=("ow", r))
            mt = []
            for s in range(SI):
                tm = p.mm(PA[:, 0:TCH], outws[:, r * 2048 + s * 128:r * 2048 + (s + 1) * 128],
                          yacc[:, s * TCH:(s + 1) * TCH], start=(s == 0), stop=(s == SI - 1),
                          deps=[tol, y_prev[s], pa_free])
                mt.append(tm)
            tev = p.copy(mo_sb[:], PA[:, 0:TCH], deps=[mt[-1], mo_prev[0]], eng="vector")
            pa_free = tev
            tdm = p.dma(mout[ot * 128:(ot + 1) * 128, c0:c0 + TCH], mo_sb[:], deps=[tev], key=("mo",))
            mo_prev[0] = tdm
            outws_prev[r] = tdm
            out_toks.append(tdm)
        for k in range(DK):
            xnc_prev[k] = out_toks[-1]
    p.done(out_toks)
    return p.build()


def build_l4():
    p = Prog()
    T = HALF
    x1_in = p.din("x1", [D, T], F32)
    xn1_in = p.din("xn1", [D, T], BF16)
    xc0_in = p.din("xn1c0", [128, DK], F32)
    mf_in = p.din("mf", [D, T], F32)
    mb_in = p.din("mb", [D, T], F32)
    wr_in = p.din("wr_t", [D, EM], F32)
    at_in = p.din("a_t", [EM, D, RM], F32)
    bwt_in = p.din("bw_t", [EM, RM, D], F32)

    x2_out = p.dout("x2", [D, T], F32)

    PA = p.ps("PA", [128, 1024])
    PB = p.ps("PB", [128, 1024])
    PD = p.ps("PD", [128, 512])
    out_toks = []

    xn1 = p.sb("xn1_s4", [128, DK * T], BF16)
    txn = [p.dma(xn1[:, k * T:(k + 1) * T], xn1_in[k * 128:(k + 1) * 128, :], key=("xn1l", k)) for k in range(DK)]

    xc0 = p.sb("xc0_s4", [128, DK], F32)
    xc0b = p.sb("xc0b", [128, DK], BF16)
    t1 = p.dma(xc0[:], xc0_in, key=("xc0l",))
    t2 = p.copy(xc0b[:], xc0[:], deps=[t1])
    wr_f = p.sb("wrf", [128, DK * EM], F32)
    wr_b = p.sb("wrb", [128, DK * EM], BF16)
    t3 = p.dma(wr_f[:], wr_in.rearrange("(k p) e -> p k e", p=128), key=("wr",))
    t4 = p.copy(wr_b[:], wr_f[:], deps=[t3])
    lg = p.sb("lg", [1, 4 * EM], F32)
    mml = []
    for k in range(DK):
        tm = p.mm(PD[0:1, 0:EM], xc0b[:, k:k + 1], wr_b[:, k * EM:(k + 1) * EM],
                  start=(k == 0), stop=(k == DK - 1), deps=[t2, t4])
        mml.append(tm)
    L, S_, M2_, G_ = 0, EM, 2 * EM, 3 * EM
    t5 = p.copy(lg[0:1, L:L + EM], PD[0:1, 0:EM], deps=[mml[-1]], eng="vector")
    m1 = p.sb("m1", [1, 4], F32)
    t6 = p.red(m1[0:1, 0:1], lg[0:1, L:L + EM], ALU.max, deps=[t5])
    t7 = p.ts(lg[0:1, S_:S_ + EM], lg[0:1, L:L + EM], m1[0:1, 0:1], ALU.subtract, deps=[t6, t5])
    t8 = p.ts(lg[0:1, M2_:M2_ + EM], lg[0:1, S_:S_ + EM], 0.0, ALU.is_ge, deps=[t7])
    t9 = p.stt(lg[0:1, M2_:M2_ + EM], lg[0:1, M2_:M2_ + EM], -1e30, lg[0:1, L:L + EM],
               ALU.mult, ALU.add, deps=[t8, t5])
    t10 = p.red(m1[0:1, 1:2], lg[0:1, M2_:M2_ + EM], ALU.max, deps=[t9])
    t11 = p.ts(lg[0:1, M2_:M2_ + EM], lg[0:1, L:L + EM], m1[0:1, 1:2], ALU.is_ge, deps=[t10, t9])
    t12 = p.act(lg[0:1, G_:G_ + EM], lg[0:1, S_:S_ + EM], AF.Exp, deps=[t7, t11])
    t13 = p.tt(lg[0:1, G_:G_ + EM], lg[0:1, G_:G_ + EM], lg[0:1, M2_:M2_ + EM], ALU.mult,
               deps=[t12, t11])
    t14 = p.red(m1[0:1, 2:3], lg[0:1, G_:G_ + EM], ALU.add, deps=[t13])
    t15 = p.recip(m1[0:1, 3:4], m1[0:1, 2:3], deps=[t14])
    t16 = p.ts(lg[0:1, G_:G_ + EM], lg[0:1, G_:G_ + EM], m1[0:1, 3:4], ALU.mult, deps=[t15, t13])
    gb = p.sb("gb", [1, EM], BF16)
    t17 = p.copy(gb[:], lg[0:1, G_:G_ + EM], deps=[t16], eng="vector")
    ones64 = p.sb("ones64", [1, 64], BF16)
    t18 = p.memset(ones64[:], 1.0)
    gcol = p.sb("gcol", [64, EM], F32)
    gcol_toks = []
    for e in range(EM):
        tm = p.mm(PD[0:64, 0:1], ones64[:], gb[0:1, e:e + 1], start=True, stop=True,
                  deps=[t17, t18] + ([gcol_toks[-1]] if gcol_toks else []))
        te = p.copy(gcol[:, e:e + 1], PD[0:64, 0:1], deps=[tm], eng="vector")
        gcol_toks.append(te)

    at_sb = p.sb("at", [128, EM * DK * RM], BF16)
    at_f = p.sb("atf", [128, 2 * RM], F32)
    ap_prev = [None, None]
    at_toks = {}
    for e in range(EM):
        for k in range(DK):
            r = (e * DK + k) % 2
            tl = p.dma(at_f[:, r * RM:(r + 1) * RM], at_in[e, k * 128:(k + 1) * 128, :],
                       deps=[ap_prev[r]], key=("at", r))
            tc = p.copy(at_sb[:, (e * DK + k) * RM:(e * DK + k + 1) * RM], at_f[:, r * RM:(r + 1) * RM],
                        deps=[tl])
            ap_prev[r] = tc
            at_toks[(e, k)] = tc
    bwt_sb = p.sb("bwt", [64, EM * D], BF16)
    bwt_f = p.sb("bwtf", [64, 2 * D], F32)
    bp_prev = [None, None]
    bwt_toks = {}
    for e in range(EM):
        r = e % 2
        tl = p.dma(bwt_f[:, r * D:(r + 1) * D], bwt_in[e, :, :], deps=[bp_prev[r]], key=("bw", r))
        tc = p.copy(bwt_sb[:, e * D:(e + 1) * D], bwt_f[:, r * D:(r + 1) * D], deps=[tl])
        bp_prev[r] = tc
        bwt_toks[e] = tc

    midg = p.sb("midg", [64, EM * T], BF16)
    midg_toks = {}
    pa_free = None
    for e in range(EM):
        for th in range(2):
            mt = []
            for k in range(DK):
                tm = p.mm(PA[0:64, 0:1024], at_sb[:, (e * DK + k) * RM:(e * DK + k + 1) * RM],
                          xn1[:, k * T + th * 1024:k * T + (th + 1) * 1024],
                          start=(k == 0), stop=(k == DK - 1),
                          deps=[at_toks[(e, k)], txn[k], pa_free])
                mt.append(tm)
            te = p.ts(midg[0:64, e * T + th * 1024:e * T + (th + 1) * 1024], PA[0:64, 0:1024],
                      gcol[:, e:e + 1], ALU.mult, deps=[mt[-1], gcol_toks[e]])
            pa_free = te
            midg_toks[(e, th)] = te

    xtmp = p.sb("xtmp", [128, 2 * T], F32)
    mtmp = p.sb("mtmp", [128, 2 * T], F32)
    mtmp2 = p.sb("mtmp2", [128, 2 * T], F32)
    o_sb = p.sb("osb", [128, 2 * T], F32)
    prev_r = [None, None]
    pab_free = [pa_free, None]
    for ot in range(DK):
        r = ot % 2
        mt = []
        for th in range(2):
            pa = PA if th == 0 else PB
            for e in range(EM):
                tm = p.mm(pa[:, 0:1024], bwt_sb[0:64, e * D + ot * 128:e * D + (ot + 1) * 128],
                          midg[0:64, e * T + th * 1024:e * T + (th + 1) * 1024],
                          start=(e == 0), stop=(e == EM - 1),
                          deps=[bwt_toks[e], midg_toks[(e, th)], pab_free[th]])
                mt.append(tm)
        tmf = p.dma(mtmp[:, r * T:(r + 1) * T], mf_in[ot * 128:(ot + 1) * 128, :], deps=[prev_r[r]], key=("mf", r))
        tmb = p.dma(mtmp2[:, r * T:(r + 1) * T], mb_in[ot * 128:(ot + 1) * 128, :], deps=[prev_r[r]], key=("mb", r))
        tx1 = p.dma(xtmp[:, r * T:(r + 1) * T], x1_in[ot * 128:(ot + 1) * 128, :], deps=[prev_r[r]], key=("x1l", r))
        ta = p.tt(mtmp[:, r * T:(r + 1) * T], mtmp[:, r * T:(r + 1) * T], mtmp2[:, r * T:(r + 1) * T],
                  ALU.add, deps=[tmf, tmb])
        tb = p.stt(xtmp[:, r * T:(r + 1) * T], mtmp[:, r * T:(r + 1) * T], 0.5,
                   xtmp[:, r * T:(r + 1) * T], ALU.mult, ALU.add, deps=[ta, tx1])
        tc = p.tt(xtmp[:, r * T:(r + 1) * T], xtmp[:, r * T:(r + 1) * T], xn1[:, ot * T:(ot + 1) * T],
                  ALU.add, deps=[tb, txn[ot]])
        td1 = p.tt(o_sb[:, r * T:r * T + 1024], xtmp[:, r * T:r * T + 1024], PA[:, 0:1024], ALU.add,
                   deps=[tc, mt[EM - 1]])
        td2 = p.tt(o_sb[:, r * T + 1024:r * T + 2048], xtmp[:, r * T + 1024:r * T + 2048],
                   PB[:, 0:1024], ALU.add, deps=[tc, mt[-1], td1])
        pab_free = [td1, td2]
        tdm = p.dma(x2_out[ot * 128:(ot + 1) * 128, :], o_sb[:, r * T:(r + 1) * T], deps=[td1, td2], key=("x2o", r))
        prev_r[r] = tdm
        out_toks.append(tdm)

    p.done(out_toks)
    return p.build()


def _t(a):
    return np.ascontiguousarray(np.asarray(a).T)


DBG = None


def kernel(**kw):
    x = np.asarray(kw["L0_attn_x"], dtype=np.float32)
    cores = list(range(NCORE))

    nc1 = build_l1()
    wg_t = _t(kw["L0_ffn_w_gate"]).astype(np.int32)
    wu_t = _t(kw["L0_ffn_w_up"]).astype(np.int32)
    wd_t = _t(kw["L0_ffn_w_down"]).astype(np.int32)
    maps1 = []
    for c in cores:
        b, h = c // 2, c % 2
        maps1.append({
            "xT": _t(x[b, h * HALF:(h + 1) * HALF, :]),
            "Wq": np.asarray(kw["L0_attn_Wq"], np.float32),
            "Wk": np.asarray(kw["L0_attn_Wk"], np.float32),
            "Wv": np.asarray(kw["L0_attn_Wv"], np.float32),
            "wg_t": wg_t, "wu_t": wu_t, "wd_t": wd_t,
        })
    r1 = run_bass_kernel_spmd(nc1, maps1, core_ids=cores).results

    nc2 = build_l2()
    bo_pm = np.ascontiguousarray(np.asarray(kw["L0_attn_bo"], np.float32).reshape(DK, 128).T)
    maps2 = []
    for c in cores:
        b, h = c // 2, c % 2
        maps2.append({
            "ctxp0": r1[b * 2 + 0]["ctxp"], "ctxp1": r1[b * 2 + 1]["ctxp"],
            "kden0": r1[b * 2 + 0]["kdenp"], "kden1": r1[b * 2 + 1]["kdenp"],
            "qexp": r1[c]["qexp"], "qden": r1[c]["qden"], "xpf": r1[c]["xpf"],
            "Wo": np.asarray(kw["L0_attn_Wo"], np.float32), "bo_pm": bo_pm,
        })
    r2 = run_bass_kernel_spmd(nc2, maps2, core_ids=cores).results

    nc3 = build_l3()
    maps3 = []
    w3 = {}
    for d, sfx in ((0, "_f"), (1, "_b")):
        convW = np.asarray(kw[f"L1_attn_convW{sfx}"], np.float32)
        cdiag = np.zeros((SI, DC, 128, 128), np.float32)
        for s in range(SI):
            for j in range(DC):
                np.fill_diagonal(cdiag[s, j], convW[s * 128:(s + 1) * 128, 0, j])
        sel_bc = np.zeros((128, 32 * 128), np.float32)
        for j in range(32):
            sel_bc[64 + j, j * 128:(j + 1) * 128] = 1.0
        alog = np.asarray(kw[f"L1_attn_Alog{sfx}"], np.float32)
        a_lay = np.ascontiguousarray(alog.reshape(SI, 128, DS).transpose(1, 0, 2).reshape(128, SI * DS))
        w3[d] = {
            "inW_t": _t(kw[f"L1_attn_inW{sfx}"]),
            "outW_t": _t(kw[f"L1_attn_outW{sfx}"]),
            "xpW_t": _t(kw[f"L1_attn_xpW{sfx}"]),
            "dtW_t": _t(kw[f"L1_attn_dtW{sfx}"]),
            "A_lay": a_lay,
            "dtB_pm": np.ascontiguousarray(np.asarray(kw[f"L1_attn_dtB{sfx}"], np.float32).reshape(SI, 128).T),
            "convB_pm": np.ascontiguousarray(np.asarray(kw[f"L1_attn_convB{sfx}"], np.float32).reshape(SI, 128).T),
            "Dp_pm": np.ascontiguousarray(np.asarray(kw[f"L1_attn_Dp{sfx}"], np.float32).reshape(SI, 128).T),
            "convdiag": cdiag,
            "sel_bc": sel_bc,
            "ident": np.eye(128, dtype=np.float32),
        }
    for c in cores:
        b, d = c // 2, c % 2
        xnf = np.concatenate([np.asarray(r2[b * 2 + 0]["xn1"]), np.asarray(r2[b * 2 + 1]["xn1"])], axis=1)
        if d == 1:
            xnf = xnf[:, ::-1]
        m = {"xnf": np.ascontiguousarray(xnf)}
        m.update(w3[d])
        maps3.append(m)
    r3 = run_bass_kernel_spmd(nc3, maps3, core_ids=cores).results

    nc4 = build_l4()
    a_t = np.ascontiguousarray(np.asarray(kw["L1_ffn_A"], np.float32).transpose(0, 2, 1))
    bw_t = np.ascontiguousarray(np.asarray(kw["L1_ffn_Bw"], np.float32).transpose(0, 2, 1))
    wr_t = _t(kw["L1_ffn_Wr"])
    maps4 = []
    for c in cores:
        b, h = c // 2, c % 2
        mf = np.asarray(r3[b * 2 + 0]["mout"])
        mb = np.asarray(r3[b * 2 + 1]["mout"])[:, ::-1]
        maps4.append({
            "x1": r2[c]["x1"], "xn1": r2[c]["xn1"], "xn1c0": r2[b * 2 + 0]["xn1c0"],
            "mf": np.ascontiguousarray(mf[:, h * HALF:(h + 1) * HALF]),
            "mb": np.ascontiguousarray(mb[:, h * HALF:(h + 1) * HALF]),
            "wr_t": wr_t, "a_t": a_t, "bw_t": bw_t,
        })
    r4 = run_bass_kernel_spmd(nc4, maps4, core_ids=cores).results

    out = np.empty((B, N, D), np.float32)
    for c in cores:
        b, h = c // 2, c % 2
        out[b, h * HALF:(h + 1) * HALF, :] = np.asarray(r4[c]["x2"]).T
    if DBG is not None:
        DBG.update(r1=r1, r2=r2, r3=r3, r4=r4)
    return out


# version 06
# speedup vs baseline: 1.1696x; 1.1696x over previous
import numpy as np
import ml_dtypes
from contextlib import ExitStack
import concourse.bass as bass
import concourse.mybir as mybir
from concourse.bass_utils import run_bass_kernel_spmd

F32 = mybir.dt.float32
BF16 = mybir.dt.bfloat16
I32 = mybir.dt.int32
AF = mybir.ActivationFunctionType
ALU = mybir.AluOpType
AX = mybir.AxisListType
NPBF = ml_dtypes.bfloat16

B, N, D = 4, 4096, 1024
HALF = N // 2
DK = D // 128
H3 = 3072
HT3 = H3 // 128
QSC = float(1.0 / 127.0)
DI, DS, DTR, DC = 2048, 16, 64, 4
SI = DI // 128
EM, RM = 8, 64
LN_EPS = 1e-5
NCORE = 8
TCH = 1024
NCH = N // TCH


class Prog:
    ENGS = ("sync", "vector", "scalar", "tensor", "gpsimd")

    def __init__(self):
        self.nc = bass.Bass("TRN2", target_bir_lowering=False)
        self.ctx = ExitStack()
        self.q = {e: [] for e in self.ENGS}
        self.cnt = {e: 0 for e in self.ENGS}

    def sb(self, name, shape, dt):
        return self.ctx.enter_context(self.nc.sbuf_tensor(name, shape, dt))

    def ps(self, name, shape, dt=F32):
        return self.ctx.enter_context(self.nc.psum_tensor(name, shape, dt))

    def din(self, name, shape, dt):
        return self.nc.dram_tensor(name, shape, dt, kind="ExternalInput").ap()

    def dout(self, name, shape, dt):
        return self.nc.dram_tensor(name, shape, dt, kind="ExternalOutput").ap()

    def dtmp(self, name, shape, dt):
        return self.nc.dram_tensor(name, shape, dt, kind="Internal").ap()

    @staticmethod
    def _is_tok(d):
        return (isinstance(d, tuple) and len(d) == 2 and isinstance(d[1], int)
                and (isinstance(d[0], str)
                     or (isinstance(d[0], tuple) and len(d[0]) == 2 and d[0][0] == "lane")))

    @staticmethod
    def _flat(deps):
        out = []
        for d in deps:
            if d is None:
                continue
            if Prog._is_tok(d):
                out.append(d)
            else:
                out.extend(Prog._flat(d))
        return out

    def _op(self, eng, fn, deps, dma=False, lane=None):
        if lane is not None:
            key = ("lane", lane)
            self.cnt.setdefault(key, 0)
            self.cnt[key] += 16
            self.q[eng].append((fn, Prog._flat(deps), 16, key))
            return (key, self.cnt[key])
        inc = 16 if dma else 1
        self.cnt[eng] += inc
        self.q[eng].append((fn, Prog._flat(deps), inc, None))
        return (eng, self.cnt[eng])

    def dma(self, out, in_, deps=(), eng="sync", key=None):
        assert key is not None, "dma requires a lane key"
        return self._op(eng, lambda e, o=out, i=in_: e.dma_start(out=o, in_=i), deps,
                        dma=True, lane=key)

    def mm(self, out, lhsT, rhs, start, stop, deps=()):
        nf = int(out.ap[-1][1])
        if nf <= 512:
            return self._op("tensor", lambda e, o=out, l=lhsT, r=rhs, s1=start, s2=stop:
                            e.matmul(out=o, lhsT=l, rhs=r, start=s1, stop=s2), deps)
        tok = None
        for c in range(0, nf, 512):
            w = min(512, nf - c)
            tok = self._op("tensor",
                           lambda e, o=out[:, c:c + w], l=lhsT, r=rhs[:, c:c + w], s1=start, s2=stop:
                           e.matmul(out=o, lhsT=l, rhs=r, start=s1, stop=s2), deps)
        return tok

    def tt(self, out, a, b, op, deps=(), eng="vector"):
        return self._op(eng, lambda e, o=out, x=a, y=b, p=op: e.tensor_tensor(out=o, in0=x, in1=y, op=p), deps)

    def ts(self, out, a, scalar, op, deps=(), eng="vector"):
        return self._op(eng, lambda e, o=out, x=a, s=scalar, p=op:
                        e.tensor_single_scalar(out=o, in_=x, scalar=s, op=p), deps)

    def ts2(self, out, a, s1, s2, op0, op1, deps=(), eng="vector"):
        return self._op(eng, lambda e, o=out, x=a, a1=s1, a2=s2, p0=op0, p1=op1:
                        e.tensor_scalar(out=o, in0=x, scalar1=a1, scalar2=a2, op0=p0, op1=p1), deps)

    def stt(self, out, in0, scalar, in1, op0, op1, deps=(), eng="vector"):
        return self._op(eng, lambda e, o=out, x=in0, s=scalar, y=in1, p0=op0, p1=op1:
                        e.scalar_tensor_tensor(out=o, in0=x, scalar=s, in1=y, op0=p0, op1=p1), deps)

    def act(self, out, in_, func, deps=(), scale=None, bias=None):
        kw = {}
        if scale is not None:
            kw["scale"] = scale
        if bias is not None:
            kw["bias"] = bias
        return self._op("scalar", lambda e, o=out, i=in_, f=func, k=kw:
                        e.activation(out=o, in_=i, func=f, **k), deps)

    def copy(self, out, in_, deps=(), eng="scalar"):
        if eng == "scalar":
            return self._op(eng, lambda e, o=out, i=in_: e.copy(out=o, in_=i), deps)
        return self._op(eng, lambda e, o=out, i=in_: e.tensor_copy(out=o, in_=i), deps)

    def red(self, out, in_, op, deps=()):
        return self._op("vector", lambda e, o=out, i=in_, p=op:
                        e.tensor_reduce(out=o, in_=i, axis=AX.X, op=p), deps)

    def scan(self, out, a, b, initial, deps=()):
        return self._op("vector", lambda e, o=out, x=a, y=b, h=initial:
                        e.tensor_tensor_scan(out=o, data0=x, data1=y, initial=h,
                                             op0=ALU.mult, op1=ALU.add), deps)

    def memset(self, ap, v, deps=(), eng="gpsimd"):
        return self._op(eng, lambda e, a=ap, c=v: e.memset(a, c), deps)

    def recip(self, out, in_, deps=()):
        return self._op("vector", lambda e, o=out, i=in_: e.reciprocal(out=o, in_=i), deps)

    def barrier(self):
        toks = [(k, v) for k, v in self.cnt.items() if v > 0]
        for e in self.ENGS:
            self.q[e].append((None, list(toks), 0, None))
        return toks

    def consts(self, values):
        vals = sorted(set(float(v) for v in values))
        if not vals:
            return
        cp = self.sb("constpool", [128, len(vals)], F32)
        toks = []
        for i, v in enumerate(vals):
            toks.append(self.memset(cp[:, i:i + 1], v))
            self.nc.const_aps.register(F32, v, cp[:, i:i + 1])
        for e in self.ENGS:
            self.q[e].append((None, list(toks), 0, None))

    def done(self, tokens):
        self.q["sync"].append((None, Prog._flat(tokens), 0, None))

    def build(self):
        sems = {e: self.ctx.enter_context(self.nc.semaphore(f"sem_{e}")) for e in self.ENGS}
        li = 0
        for k in self.cnt:
            if isinstance(k, tuple):
                sems[k] = self.ctx.enter_context(self.nc.semaphore(f"lane_{li}"))
                li += 1
        block = self.ctx.enter_context(self.nc.Block())

        def body(eng_name):
            def run(engine):
                for fn, deps, inc, lane in self.q[eng_name]:
                    best = {}
                    for (de, dv) in deps:
                        if de == eng_name and eng_name == "tensor":
                            continue
                        if de == eng_name and fn is None:
                            continue
                        best[de] = max(best.get(de, 0), dv)
                    for de in sorted(best, key=str):
                        engine.wait_ge(sems[de], best[de])
                    if fn is not None:
                        fn(engine).then_inc(sems[lane if lane is not None else eng_name], inc)
            return run

        block.sync(body("sync"))
        block.vector(body("vector"))
        block.scalar(body("scalar"))
        block.tensor(body("tensor"))
        block.gpsimd(body("gpsimd"))
        self.ctx.close()
        return self.nc


def conv_weight_slices(p, name, src_ap, n_slices, cols, rot=2, dt_in=F32, ctx=None):
    al = (lambda nm, sh, dt: ctx.enter_context(p.nc.sbuf_tensor(nm, sh, dt))) if ctx is not None else p.sb
    dst = al(name + "_bf", [128, n_slices * cols], BF16)
    tmp = al(name + "_tmp", [128, rot * cols], dt_in)
    toks = []
    prev = [None] * rot
    for k in range(n_slices):
        r = k % rot
        tl = p.dma(tmp[:, r * cols:(r + 1) * cols], src_ap[k * 128:(k + 1) * 128, :], deps=[prev[r]], key=(name, r))
        tc = p.copy(dst[:, k * cols:(k + 1) * cols], tmp[:, r * cols:(r + 1) * cols], deps=[tl])
        prev[r] = tc
        toks.append(tc)
    return dst, toks


def emit_ln(p, x_view, T, tag, x_deps, pa_row, pd_rep, xn_bf, extra_c0=None, ctx=None):
    al = (lambda nm, sh, dt: ctx.enter_context(p.nc.sbuf_tensor(nm, sh, dt))) if ctx is not None else p.sb
    onec_b = al(f"{tag}onecb", [128, 1], BF16)
    onec_f = al(f"{tag}onecf", [128, 1], F32)
    onerow_f = al(f"{tag}onerf", [1, 128], F32)
    t1 = p.memset(onec_b[:], 1.0)
    t2 = p.memset(onec_f[:], 1.0)
    t2b = p.memset(onerow_f[:], 1.0)
    sq = al(f"{tag}sq", [128, 2 * T], BF16)
    mean = al(f"{tag}mean", [1, T], F32)
    rstd = al(f"{tag}rstd", [1, T], F32)
    mrs = al(f"{tag}mrs", [1, T], F32)
    reps = al(f"{tag}reps", [128, 2 * T], F32)

    pa_free = None
    sum_toks = []
    for c in range(0, T, 512):
        for k in range(DK):
            tm = p.mm(pa_row[0:1, 0:512],
                      onec_f[:], x_view(k)[:, c:c + 512], start=(k == 0), stop=(k == DK - 1),
                      deps=[t2, x_deps[k], pa_free])
            sum_toks.append(tm)
        te = p.copy(mean[0:1, c:c + 512], pa_row[0:1, 0:512], deps=[sum_toks[-1]], eng="vector")
        pa_free = te
        sum_toks.append(te)
    t_mean = p.ts(mean[:], mean[:], float(1.0 / D), ALU.mult, deps=[sum_toks[-1]])

    var_toks = []
    last_use = [None, None]
    for c in range(0, T, 512):
        for k in range(DK):
            r = k % 2
            tq = p.act(sq[:, r * T:r * T + 512], x_view(k)[:, c:c + 512], AF.Square,
                       deps=[x_deps[k], last_use[r]])
            tm = p.mm(pa_row[0:1, 0:512], onec_b[:], sq[:, r * T:r * T + 512],
                      start=(k == 0), stop=(k == DK - 1), deps=[t1, tq, pa_free])
            last_use[r] = tm
            var_toks.append(tm)
        te = p.copy(rstd[0:1, c:c + 512], pa_row[0:1, 0:512], deps=[var_toks[-1]], eng="vector")
        pa_free = te
        var_toks.append(te)
    tv1 = p.stt(mrs[:], mean[:], -1.0, mean[:], ALU.mult, ALU.mult, deps=[t_mean, var_toks[-1]])
    tv2 = p.stt(rstd[:], rstd[:], float(1.0 / D), mrs[:], ALU.mult, ALU.add, deps=[tv1, var_toks[-1]])
    tv3 = p.act(rstd[:], rstd[:], AF.Ln, bias=float(LN_EPS), deps=[tv2])
    tv4 = p.act(rstd[:], rstd[:], AF.Exp, scale=-0.5, deps=[tv3])
    tv5 = p.stt(mrs[:], mean[:], -1.0, rstd[:], ALU.mult, ALU.mult, deps=[tv4, tv1])
    rep_toks = []
    for (row, base) in ((rstd, 0), (mrs, T)):
        for c in range(0, T, 512):
            tm = p.mm(pd_rep[:, 0:512], onerow_f[:], row[0:1, c:c + 512], start=True, stop=True,
                      deps=[tv4, tv5, t2b] + ([rep_toks[-1]] if rep_toks else []))
            te = p.copy(reps[:, base + c:base + c + 512], pd_rep[:, 0:512], deps=[tm], eng="vector")
            rep_toks.append(te)
    t_rep = rep_toks[-1]
    out_toks = []
    for k in range(DK):
        ta = p.tt(xn_bf[:, k * T:(k + 1) * T], x_view(k), reps[:, 0:T], ALU.mult,
                  deps=[t_rep, x_deps[k]] + rep_toks)
        tb = p.tt(xn_bf[:, k * T:(k + 1) * T], xn_bf[:, k * T:(k + 1) * T], reps[:, T:2 * T], ALU.add,
                  deps=[ta])
        out_toks.append(tb)
    tc0 = None
    if extra_c0 is not None:
        cts = []
        for k in range(DK):
            tx = p.tt(extra_c0[:, k:k + 1], x_view(k)[:, 0:1], reps[:, 0:1], ALU.mult,
                      deps=[t_rep, x_deps[k]])
            ty = p.tt(extra_c0[:, k:k + 1], extra_c0[:, k:k + 1], reps[:, T:T + 1], ALU.add, deps=[tx])
            cts.append(ty)
        tc0 = cts
    return out_toks, tc0


def build_l1():
    p = Prog()
    T = HALF
    xin = p.din("xT", [D, T], F32)
    wq_in = p.din("Wq", [D, D], F32)
    wk_in = p.din("Wk", [D, D], F32)
    wv_in = p.din("Wv", [D, D], F32)
    wg_in = p.din("wg_t", [D, H3], I32)
    wu_in = p.din("wu_t", [D, H3], I32)
    wd_in = p.din("wd_t", [H3, D], I32)

    ctx_out = p.dout("ctxp", [D, D], F32)
    kden_out = p.dout("kdenp", [128, DK], F32)
    qexp_out = p.dout("qexp", [D, T], BF16)
    qden_out = p.dout("qden", [1, T], F32)
    xpf_out = p.dout("xpf", [D, T], F32)

    wg_i = p.dtmp("wg_i", [DK, 128, H3], BF16)
    wu_i = p.dtmp("wu_i", [DK, 128, H3], BF16)
    wd_i = p.dtmp("wd_i", [HT3, 128, D], BF16)

    PA = p.ps("PA", [128, 1024])
    PB = p.ps("PB", [128, 1024])
    PC = p.ps("PC", [128, 1024])
    PD = p.ps("PD", [128, 512])
    PE = p.ps("PE", [128, 512])
    p.consts([LN_EPS, -0.5, QSC])

    out_toks = []

    xn = p.sb("xn", [128, DK * HALF], BF16)
    scW = ExitStack()
    witmp = scW.enter_context(p.nc.sbuf_tensor("witmp", [128, 2 * H3], I32))
    wbtmp = scW.enter_context(p.nc.sbuf_tensor("wbtmp", [128, 2 * H3], BF16))
    pc_prev = [None, None]
    pre_toks = []
    for (wsrc, dst, nsl, cols) in ((wg_in, wg_i, DK, H3), (wu_in, wu_i, DK, H3), (wd_in, wd_i, HT3, D)):
        for k in range(nsl):
            r = k % 2
            tl = p.dma(witmp[:, r * H3:r * H3 + cols], wsrc[k * 128:(k + 1) * 128, :], deps=[pc_prev[r]], key=("wi", r))
            tc = p.copy(wbtmp[:, r * H3:r * H3 + cols], witmp[:, r * H3:r * H3 + cols], deps=[tl])
            td = p.dma(dst[k, :, :], wbtmp[:, r * H3:r * H3 + cols], deps=[tc], key=("wst", r))
            pc_prev[r] = td
            pre_toks.append(td)
    pre_tails = [pc_prev[0], pc_prev[1]]

    scA = ExitStack()
    x_sb = scA.enter_context(p.nc.sbuf_tensor("x", [128, DK * T], F32))
    tx = [p.dma(x_sb[:, k * T:(k + 1) * T], xin[k * 128:(k + 1) * 128, :], key=("x", k)) for k in range(DK)]
    xn_toks, _ = emit_ln(p, lambda k: x_sb[:, k * T:(k + 1) * T], T, "ln", tx, PD, PE, xn, ctx=scA)
    p.barrier()
    scA.close()

    scB = ExitStack()
    wq_sb, twq = conv_weight_slices(p, "wq", wq_in, DK, 1024, ctx=scB)
    wk_sb, twk = conv_weight_slices(p, "wk", wk_in, DK, 1024, ctx=scB)
    wv_sb, twv = conv_weight_slices(p, "wv", wv_in, DK, 1024, ctx=scB)

    qexp_bf = scB.enter_context(p.nc.sbuf_tensor("qexp_s", [128, 2 * T], BF16))
    qden_sb = scB.enter_context(p.nc.sbuf_tensor("qden_s", [1, T], F32))
    tqden_z = p.memset(qden_sb[:], 0.0, eng="vector")
    onec_b = scB.enter_context(p.nc.sbuf_tensor("onecb2", [128, 1], BF16))
    t_oc = p.memset(onec_b[:], 1.0)
    q_prev = [None, None]
    qd_prev = tqden_z
    pa_free = None
    pb_free = None
    for o in range(DK):
        mm_toks = []
        for th in range(2):
            for k in range(DK):
                tm = p.mm(PA[:, 0:1024] if th == 0 else PB[:, 0:1024],
                          wq_sb[:, k * 1024 + o * 128:k * 1024 + (o + 1) * 128],
                          xn[:, k * T + th * 1024:k * T + (th + 1) * 1024],
                          start=(k == 0), stop=(k == DK - 1),
                          deps=[twq[k], xn_toks[k], pa_free if th == 0 else pb_free])
                mm_toks.append(tm)
        r = o % 2
        te1 = p.act(qexp_bf[:, r * T:r * T + 1024], PA[:, 0:1024], AF.Exp,
                    deps=[mm_toks[DK - 1], q_prev[r]])
        te2 = p.act(qexp_bf[:, r * T + 1024:r * T + 2048], PB[:, 0:1024], AF.Exp,
                    deps=[mm_toks[2 * DK - 1], te1])
        pa_free = te1
        pb_free = te2
        tden = []
        for c in range(0, T, 512):
            tm = p.mm(PD[0:1, 0:512], onec_b[:], qexp_bf[:, r * T + c:r * T + c + 512],
                      start=True, stop=True, deps=[t_oc, te1, te2] + ([tden[-1]] if tden else []))
            ta = p.tt(qden_sb[0:1, c:c + 512], qden_sb[0:1, c:c + 512], PD[0:1, 0:512], ALU.add,
                      deps=[tm, qd_prev])
            tden.append(ta)
        qd_prev = tden[-1]
        tdma = p.dma(qexp_out[o * 128:(o + 1) * 128, :], qexp_bf[:, r * T:(r + 1) * T],
                     deps=[te1, te2], key=("qexpo", r))
        q_prev[r] = tdma
        out_toks.append(tdma)
    out_toks.append(p.dma(qden_out, qden_sb[:], deps=[qd_prev], key=("qdeno",)))

    kbuf = scB.enter_context(p.nc.sbuf_tensor("kbuf", [128, 4 * 1024], BF16))
    vbuf = scB.enter_context(p.nc.sbuf_tensor("vbuf", [128, 4 * 1024], BF16))
    ones_col = scB.enter_context(p.nc.sbuf_tensor("onescol", [128, 1], BF16))
    t_ocol = p.memset(ones_col[:], 1.0)
    ctx_sb = scB.enter_context(p.nc.sbuf_tensor("ctx", [128, DK * 1024], F32))
    tz_ctx = [p.memset(ctx_sb[:, dt * 1024:(dt + 1) * 1024], 0.0, eng="vector") for dt in range(DK)]
    kv_prev = {}
    kden_toks = []
    ctx_toks = {dt: tz_ctx[dt] for dt in range(DK)}
    pa_free = pa_free
    pb_free = pb_free
    pc_free = None
    n_tiles = T // 128
    for g in range(n_tiles // 4):
        tile_toks = []
        for tt in range(4):
            tile = g * 4 + tt
            mmk = []
            mmv = []
            for k in range(DK):
                tm = p.mm(PA[:, 0:1024], xn[:, k * T + tile * 128:k * T + (tile + 1) * 128],
                          wk_sb[:, k * 1024:(k + 1) * 1024], start=(k == 0), stop=(k == DK - 1),
                          deps=[twk[k], xn_toks[k], pa_free])
                mmk.append(tm)
                tm = p.mm(PB[:, 0:1024], xn[:, k * T + tile * 128:k * T + (tile + 1) * 128],
                          wv_sb[:, k * 1024:(k + 1) * 1024], start=(k == 0), stop=(k == DK - 1),
                          deps=[twv[k], xn_toks[k], pb_free])
                mmv.append(tm)
            tke = p.act(kbuf[:, tt * 1024:(tt + 1) * 1024], PA[:, 0:1024], AF.Exp,
                        deps=[mmk[-1], kv_prev.get(("k", tt))])
            tve = p.copy(vbuf[:, tt * 1024:(tt + 1) * 1024], PB[:, 0:1024],
                         deps=[mmv[-1], kv_prev.get(("v", tt))], eng="vector")
            pa_free = tke
            pb_free = tve
            tile_toks.append((tile, tke, tve))
        for dt in range(DK):
            gm = []
            for (tile, tke, tve) in tile_toks:
                tt = tile % 4
                tm = p.mm(PC[:, 0:1024], kbuf[:, tt * 1024 + dt * 128:tt * 1024 + (dt + 1) * 128],
                          vbuf[:, tt * 1024:(tt + 1) * 1024],
                          start=(tt == 0), stop=(tt == 3), deps=[tke, tve, pc_free])
                gm.append(tm)
                tkd = p.mm(PE[:, dt:dt + 1], kbuf[:, tt * 1024 + dt * 128:tt * 1024 + (dt + 1) * 128],
                           ones_col[:], start=(g == 0 and tt == 0), stop=(g == n_tiles // 4 - 1 and tt == 3),
                           deps=[tke, t_ocol])
                kden_toks.append(tkd)
            ta = p.tt(ctx_sb[:, dt * 1024:(dt + 1) * 1024], ctx_sb[:, dt * 1024:(dt + 1) * 1024],
                      PC[:, 0:1024], ALU.add, deps=[gm[-1], ctx_toks[dt]])
            ctx_toks[dt] = ta
            pc_free = ta
        for (tile, tke, tve) in tile_toks:
            kv_prev[("k", tile % 4)] = (ctx_toks[DK - 1], kden_toks[-1])
            kv_prev[("v", tile % 4)] = (ctx_toks[DK - 1],)
    kden_sb = scB.enter_context(p.nc.sbuf_tensor("kden", [128, DK], F32))
    tkd_e = p.copy(kden_sb[:], PE[:, 0:DK], deps=[kden_toks[-1]], eng="vector")
    out_toks.append(p.dma(kden_out, kden_sb[:], deps=[tkd_e], key=("kdeno",)))
    for dt in range(DK):
        out_toks.append(p.dma(ctx_out[dt * 128:(dt + 1) * 128, :], ctx_sb[:, dt * 1024:(dt + 1) * 1024],
                              deps=[ctx_toks[dt]], key=("ctxo",)))
    p.barrier()
    scB.close()
    scW.close()

    hact = p.sb("hact", [128, HT3 * T], BF16)
    gbuf = p.sb("gbuf", [128, 2 * 1024], BF16)
    wg_sb = p.sb("wg_sb", [128, 2 * 1024], BF16)
    wu_sb = p.sb("wu_sb", [128, 2 * 1024], BF16)
    wd_sb = p.sb("wd_sb", [128, 2 * H3], BF16)
    xtmp = p.sb("xtmp", [128, 2 * T], F32)
    ftmp = p.sb("ftmp", [128, 2 * T], F32)
    wgu_prev = [None, None]
    wd_prev = [None, None]
    x_prev = [None, None]
    f_prev = [None, None]
    hact_prev = {}
    pa_free2 = None
    pb_free2 = None
    pc_free2 = None
    gb_prev = [None, None]
    for ht in range(HT3):
        r = ht % 2
        tg = p.dma(wg_sb[:, r * 1024:(r + 1) * 1024],
                   wg_i[:, :, ht * 128:(ht + 1) * 128].rearrange("k p c -> p k c"),
                   deps=[wgu_prev[r]] + pre_tails, key=("wg", r))
        tu = p.dma(wu_sb[:, r * 1024:(r + 1) * 1024],
                   wu_i[:, :, ht * 128:(ht + 1) * 128].rearrange("k p c -> p k c"),
                   deps=[wgu_prev[r]] + pre_tails, key=("wu", r))
        for th in range(2):
            c0 = th * 1024
            mg = []
            mu = []
            for k in range(DK):
                tm = p.mm(PA[:, 0:1024], wg_sb[:, r * 1024 + k * 128:r * 1024 + (k + 1) * 128],
                          xn[:, k * T + c0:k * T + c0 + 1024], start=(k == 0), stop=(k == DK - 1),
                          deps=[tg, xn_toks[k], pa_free2])
                mg.append(tm)
                tm = p.mm(PB[:, 0:1024], wu_sb[:, r * 1024 + k * 128:r * 1024 + (k + 1) * 128],
                          xn[:, k * T + c0:k * T + c0 + 1024], start=(k == 0), stop=(k == DK - 1),
                          deps=[tu, xn_toks[k], pb_free2])
                mu.append(tm)
            tsil = p.act(gbuf[:, r * 1024:(r + 1) * 1024], PA[:, 0:1024], AF.Silu, scale=QSC,
                         deps=[mg[-1], gb_prev[r]])
            thh = p.tt(hact[:, ht * T + c0:ht * T + c0 + 1024], gbuf[:, r * 1024:(r + 1) * 1024],
                       PB[:, 0:1024], ALU.mult,
                       deps=[tsil, mu[-1], hact_prev.get((ht, th))])
            pa_free2 = tsil
            pb_free2 = thh
            gb_prev[r] = thh
            hact_prev[(ht, th)] = thh
            wgu_prev[r] = thh
    for dt in range(DK):
        r = dt % 2
        td = p.dma(wd_sb[:, r * H3:(r + 1) * H3],
                   wd_i[:, :, dt * 128:(dt + 1) * 128].rearrange("k p c -> p k c"),
                   deps=[wd_prev[r]] + pre_tails, key=("wd", r))
        txl = p.dma(xtmp[:, r * T:(r + 1) * T], xin[dt * 128:(dt + 1) * 128, :],
                    deps=[x_prev[r]], key=("xt", r))
        tf_parts = []
        for th in range(2):
            c0 = th * 1024
            mf = []
            for ht in range(HT3):
                tm = p.mm(PC[:, 0:1024], wd_sb[:, r * H3 + ht * 128:r * H3 + (ht + 1) * 128],
                          hact[:, ht * T + c0:ht * T + c0 + 1024], start=(ht == 0), stop=(ht == HT3 - 1),
                          deps=[td, hact_prev[(ht, th)], pc_free2])
                mf.append(tm)
            tf = p.stt(ftmp[:, r * T + c0:r * T + c0 + 1024], PC[:, 0:1024], float(QSC * QSC),
                       xtmp[:, r * T + c0:r * T + c0 + 1024], ALU.mult, ALU.add,
                       deps=[mf[-1], txl, f_prev[r]])
            pc_free2 = tf
            tf_parts.append(tf)
        tdma = p.dma(xpf_out[dt * 128:(dt + 1) * 128, :], ftmp[:, r * T:(r + 1) * T],
                     deps=tf_parts, key=("xpfo", r))
        x_prev[r] = tf_parts[-1]
        f_prev[r] = tdma
        wd_prev[r] = tdma
        out_toks.append(tdma)

    p.done(out_toks)
    return p.build()


def build_l2():
    p = Prog()
    T = HALF
    ctx0_in = p.din("ctxp0", [D, D], F32)
    ctx1_in = p.din("ctxp1", [D, D], F32)
    kd0_in = p.din("kden0", [128, DK], F32)
    kd1_in = p.din("kden1", [128, DK], F32)
    qexp_in = p.din("qexp", [D, T], BF16)
    qden_in = p.din("qden", [1, T], F32)
    xpf_in = p.din("xpf", [D, T], F32)
    wo_in = p.din("Wo", [D, D], F32)
    bo_in = p.din("bo_pm", [128, DK], F32)

    x1_out = p.dout("x1", [D, T], F32)
    xn1_out = p.dout("xn1", [D, T], BF16)
    xc0_out = p.dout("xn1c0", [128, DK], F32)

    p.consts([LN_EPS, -0.5, -1.0, float(-np.log(D))])
    PA = p.ps("PA", [128, 1024])
    PB = p.ps("PB", [128, 1024])
    PC = p.ps("PC", [128, 1024])
    PD = p.ps("PD", [128, 512])
    PE = p.ps("PE", [128, 512])
    out_toks = []

    x1_sb = p.sb("x1_s", [128, DK * T], F32)
    scB = ExitStack()

    kd_sb = scB.enter_context(p.nc.sbuf_tensor("kd", [128, 2 * DK], F32))
    t1 = p.dma(kd_sb[:, 0:DK], kd0_in, key=("kd0",))
    t2 = p.dma(kd_sb[:, DK:2 * DK], kd1_in, key=("kd1",))
    t3 = p.tt(kd_sb[:, 0:DK], kd_sb[:, 0:DK], kd_sb[:, DK:2 * DK], ALU.add, deps=[t1, t2])
    t4 = p.recip(kd_sb[:, DK:2 * DK], kd_sb[:, 0:DK], deps=[t3])

    ctxn = scB.enter_context(p.nc.sbuf_tensor("ctxn", [128, DK * 1024], BF16))
    y_sb = scB.enter_context(p.nc.sbuf_tensor("y", [128, DK * T], BF16))
    scB2 = ExitStack()
    c_tmp = scB2.enter_context(p.nc.sbuf_tensor("ctmp", [128, 2 * 2048], F32))
    c_prev = [None, None]
    ctxn_toks = []
    for dt in range(DK):
        r = dt % 2
        ta = p.dma(c_tmp[:, r * 2048:r * 2048 + 1024], ctx0_in[dt * 128:(dt + 1) * 128, :], deps=[c_prev[r]], key=("c0", r))
        tb = p.dma(c_tmp[:, r * 2048 + 1024:r * 2048 + 2048], ctx1_in[dt * 128:(dt + 1) * 128, :],
                   deps=[c_prev[r]], key=("c1", r))
        tsum = p.tt(c_tmp[:, r * 2048:r * 2048 + 1024], c_tmp[:, r * 2048:r * 2048 + 1024],
                    c_tmp[:, r * 2048 + 1024:r * 2048 + 2048], ALU.add, deps=[ta, tb])
        tn = p.ts2(ctxn[:, dt * 1024:(dt + 1) * 1024], c_tmp[:, r * 2048:r * 2048 + 1024],
                   kd_sb[:, DK + dt:DK + dt + 1], 0.125, ALU.mult, ALU.mult, deps=[tsum, t4])
        c_prev[r] = tn
        ctxn_toks.append(tn)

    p.barrier()
    scB2.close()
    scB3 = ExitStack()
    qexp = scB3.enter_context(p.nc.sbuf_tensor("qexp_s", [128, DK * T], BF16))
    tq = [p.dma(qexp[:, k * T:(k + 1) * T], qexp_in[k * 128:(k + 1) * 128, :], key=("qe", k)) for k in range(DK)]

    y_toks = []
    pab_free = [None, None]
    for et in range(DK):
        for th in range(2):
            pa = PA if th == 0 else PB
            mt = []
            for k in range(DK):
                tm = p.mm(pa[:, 0:1024], ctxn[:, k * 1024 + et * 128:k * 1024 + (et + 1) * 128],
                          qexp[:, k * T + th * 1024:k * T + (th + 1) * 1024],
                          start=(k == 0), stop=(k == DK - 1),
                          deps=[ctxn_toks[k], tq[k], pab_free[th]])
                mt.append(tm)
            te = p.copy(y_sb[:, et * T + th * 1024:et * T + (th + 1) * 1024], pa[:, 0:1024],
                        deps=[mt[-1]] + ([y_toks[-1]] if y_toks else []), eng="vector")
            pab_free[th] = te
            y_toks.append(te)

    p.barrier()
    scB3.close()
    wo_sb, two = conv_weight_slices(p, "wo", wo_in, DK, 1024, ctx=scB)
    bo_sb = scB.enter_context(p.nc.sbuf_tensor("bo", [128, DK], F32))
    tbo = p.dma(bo_sb[:], bo_in, key=("bo",))

    qinv = scB.enter_context(p.nc.sbuf_tensor("qinv_s", [1, T], F32))
    tqd = p.dma(qinv[:], qden_in, key=("qd",))
    tq1 = p.act(qinv[:], qinv[:], AF.Ln, deps=[tqd])
    tq2 = p.act(qinv[:], qinv[:], AF.Exp, scale=-1.0, bias=float(-np.log(D)), deps=[tq1])
    onec_f = scB.enter_context(p.nc.sbuf_tensor("onecf3", [1, 128], F32))
    tof = p.memset(onec_f[:], 1.0)
    qrep = scB.enter_context(p.nc.sbuf_tensor("qrep", [128, T], F32))
    qr_toks = []
    for c in range(0, T, 512):
        tm = p.mm(PD[:, 0:512], onec_f[:], qinv[0:1, c:c + 512], start=True, stop=True,
                  deps=[tq2, tof] + ([qr_toks[-1]] if qr_toks else []))
        te = p.copy(qrep[:, c:c + 512], PD[:, 0:512], deps=[tm], eng="vector")
        qr_toks.append(te)

    xpf_sb = scB.enter_context(p.nc.sbuf_tensor("xpf_s", [128, 2 * T], F32))
    xp_prev = [None, None]
    x1_toks = []
    pab_free = [y_toks[-1], y_toks[-1]]
    for ot in range(DK):
        r = ot % 2
        mt = []
        for th in range(2):
            pa = PA if th == 0 else PB
            for k in range(DK):
                tm = p.mm(pa[:, 0:1024], wo_sb[:, k * 1024 + ot * 128:k * 1024 + (ot + 1) * 128],
                          y_sb[:, k * T + th * 1024:k * T + (th + 1) * 1024],
                          start=(k == 0), stop=(k == DK - 1),
                          deps=[two[k], y_toks[2 * k + th], pab_free[th]])
                mt.append(tm)
        txp = p.dma(xpf_sb[:, r * T:(r + 1) * T], xpf_in[ot * 128:(ot + 1) * 128, :], deps=[xp_prev[r]], key=("xp", r))
        ta1 = p.tt(x1_sb[:, ot * T:ot * T + 1024], PA[:, 0:1024], qrep[:, 0:1024], ALU.mult,
                   deps=[mt[DK - 1], qr_toks[-1]])
        ta2 = p.tt(x1_sb[:, ot * T + 1024:ot * T + 2048], PB[:, 0:1024], qrep[:, 1024:2048], ALU.mult,
                   deps=[mt[-1], qr_toks[-1], ta1])
        tx1 = p.stt(x1_sb[:, ot * T:(ot + 1) * T], x1_sb[:, ot * T:(ot + 1) * T],
                    bo_sb[:, ot:ot + 1], xpf_sb[:, r * T:(r + 1) * T], ALU.add, ALU.add,
                    deps=[ta1, ta2, txp, tbo])
        pab_free = [ta1, ta2]
        xp_prev[r] = tx1
        x1_toks.append(tx1)
        out_toks.append(p.dma(x1_out[ot * 128:(ot + 1) * 128, :], x1_sb[:, ot * T:(ot + 1) * T],
                              deps=[tx1], key=("x1o",)))

    p.barrier()
    scB.close()
    scC = ExitStack()
    xn1 = p.sb("xn1_s", [128, DK * T], BF16)
    xc0 = p.sb("xc0_s", [128, DK], F32)
    xn1_toks, tc0 = emit_ln(p, lambda k: x1_sb[:, k * T:(k + 1) * T], T, "ln1", x1_toks, PD, PE, xn1,
                            extra_c0=xc0, ctx=scC)
    for k in range(DK):
        out_toks.append(p.dma(xn1_out[k * 128:(k + 1) * 128, :], xn1[:, k * T:(k + 1) * T],
                              deps=[xn1_toks[k]], key=("xn1o",)))
    out_toks.append(p.dma(xc0_out, xc0[:], deps=tc0, key=("xc0o",)))
    scC.close()

    p.done(out_toks)
    return p.build()


def build_l3(debug=False):
    p = Prog()
    T = N
    xn_in = p.din("xnf", [D, T], BF16)
    inw_in = p.din("inW_t", [D, 2 * DI], F32)
    outw_in = p.din("outW_t", [DI, D], F32)
    xpw_in = p.din("xpW_t", [DI, 96], F32)
    dtw_in = p.din("dtW_t", [DTR, DI], F32)
    alog_in = p.din("A_lay", [128, SI * DS], F32)
    dtb_in = p.din("dtB_pm", [128, SI], F32)
    cvb_in = p.din("convB_pm", [128, SI], F32)
    dp_in = p.din("Dp_pm", [128, SI], F32)
    cd_in = p.din("convdiag", [SI, DC, 128, 128], F32)
    sel_in = p.din("sel_bc", [128, 32 * 128], F32)
    id_in = p.din("ident", [128, 128], F32)

    mout = p.dout("mout", [D, T], F32)
    if debug:
        dbg_u = p.dout("dbg_u", [128, SI * TCH], BF16)
        dbg_xdbl = p.dout("dbg_xdbl", [128, TCH], BF16)
        dbg_delta = p.dout("dbg_delta", [128, TCH], BF16)
        dbg_da = p.dout("dbg_da", [128, TCH], F32)
        dbg_b = p.dout("dbg_b", [128, TCH], F32)
        dbg_h = p.dout("dbg_h", [128, TCH], F32)
        dbg_brep = p.dout("dbg_brep", [128, TCH], BF16)
        dbg_y = p.dout("dbg_y", [128, SI * TCH], BF16)

    inw_i = p.dtmp("inw_i", [DK, 128, 2 * DI], BF16)
    outw_i = p.dtmp("outw_i", [SI, 128, D], BF16)
    cd_i = p.dtmp("cd_i", [SI, DC * 128, 128], BF16)

    PA = p.ps("PA", [128, 1024])
    PB = p.ps("PB", [128, 1024])
    PC = p.ps("PC", [128, 1024])
    PD = p.ps("PD", [128, 512])
    PE = p.ps("PE", [128, 512])
    p.consts([-1.0, 1.0])
    out_toks = []

    dtw_sb = p.sb("dtw", [64, DI], BF16)
    xpw_sb = p.sb("xpw", [128, SI * 96], BF16)
    sel_sb = p.sb("selbc", [128, 32 * 128], BF16)
    id_sb = p.sb("identb", [128, 128], BF16)
    scP = ExitStack()
    tmp_f = scP.enter_context(p.nc.sbuf_tensor("tmpf", [128, 2 * 4096], F32))
    tmp_b = scP.enter_context(p.nc.sbuf_tensor("tmpb", [128, 2 * 4096], BF16))
    prev = [None, None]
    pre_toks = []
    for k in range(DK):
        r = k % 2
        tl = p.dma(tmp_f[:, r * 4096:(r + 1) * 4096], inw_in[k * 128:(k + 1) * 128, :], deps=[prev[r]], key=("pre", r))
        tc = p.copy(tmp_b[:, r * 4096:(r + 1) * 4096], tmp_f[:, r * 4096:(r + 1) * 4096], deps=[tl])
        td = p.dma(inw_i[k, :, :], tmp_b[:, r * 4096:(r + 1) * 4096], deps=[tc], key=("prest", r))
        prev[r] = td
        pre_toks.append(td)
    for s in range(SI):
        r = s % 2
        tl = p.dma(tmp_f[:, r * 4096:r * 4096 + D], outw_in[s * 128:(s + 1) * 128, :], deps=[prev[r]], key=("pre", r))
        tc = p.copy(tmp_b[:, r * 4096:r * 4096 + D], tmp_f[:, r * 4096:r * 4096 + D], deps=[tl])
        td = p.dma(outw_i[s, :, :], tmp_b[:, r * 4096:r * 4096 + D], deps=[tc], key=("prest", r))
        prev[r] = td
        pre_toks.append(td)
    for s in range(SI):
        r = s % 2
        tl = p.dma(tmp_f[:, r * 4096:r * 4096 + 512],
                   cd_in[s, :, :, :].rearrange("j p c -> p j c"), deps=[prev[r]], key=("pre", r))
        tc = p.copy(tmp_b[:, r * 4096:r * 4096 + 512], tmp_f[:, r * 4096:r * 4096 + 512], deps=[tl])
        td = p.dma(cd_i[s, :, :].rearrange("(j p) c -> p j c", p=128), tmp_b[:, r * 4096:r * 4096 + 512],
                   deps=[tc], key=("prest", r))
        prev[r] = td
        pre_toks.append(td)
    dtw_f = scP.enter_context(p.nc.sbuf_tensor("dtwf", [64, DI], F32))
    t_dtw0 = p.dma(dtw_f[:], dtw_in, key=("dtwl",))
    t_dtw = p.copy(dtw_sb[:], dtw_f[:], deps=[t_dtw0])
    t_pre = [t for t in (prev[0], prev[1]) if t is not None] + pre_toks[-2:]
    tsel0 = p.dma(tmp_f[:, 0:4096], sel_in, deps=[prev[0]], key=("pre", 0))
    t_sel = p.copy(sel_sb[:], tmp_f[:, 0:4096], deps=[tsel0])
    prev[0] = t_sel
    tid0 = p.dma(tmp_f[:, 0:128], id_in, deps=[prev[0]], key=("pre", 0))
    t_id = p.copy(id_sb[:], tmp_f[:, 0:128], deps=[tid0])
    prev[0] = t_id
    t_xpw = []
    for s in range(SI):
        r = s % 2
        tl = p.dma(tmp_f[:, r * 4096:r * 4096 + 96], xpw_in[s * 128:(s + 1) * 128, :], deps=[prev[r]], key=("pre", r))
        tc = p.copy(xpw_sb[:, s * 96:(s + 1) * 96], tmp_f[:, r * 4096:r * 4096 + 96], deps=[tl])
        prev[r] = tc
        t_xpw.append(tc)
    p.barrier()
    scP.close()

    a_sb = p.sb("a_lay", [128, SI * DS], F32)
    t_a0 = p.dma(a_sb[:], alog_in, key=("a0",))
    t_a1 = p.act(a_sb[:], a_sb[:], AF.Exp, deps=[t_a0])
    t_a = p.ts(a_sb[:], a_sb[:], -1.0, ALU.mult, deps=[t_a1])
    dtb_sb = p.sb("dtb", [128, SI], F32)
    t_dtb = p.dma(dtb_sb[:], dtb_in, key=("dtb",))
    cvb_sb = p.sb("cvb", [128, SI], F32)
    t_cvb = p.dma(cvb_sb[:], cvb_in, key=("cvb",))
    dp_sb = p.sb("dp", [128, SI], F32)
    t_dp = p.dma(dp_sb[:], dp_in, key=("dp",))

    NG = 8
    xnc = p.sb("xnc", [128, DK * TCH], BF16)
    u_sb = p.sb("u", [128, SI * TCH], BF16)
    brep = p.sb("brep", [128, NG * TCH], BF16)
    crep = p.sb("crep", [128, NG * TCH], BF16)
    yacc = p.sb("yacc", [128, SI * TCH], BF16)
    xdbl = p.sb("xdbl", [128, TCH], BF16)
    xi_sb = p.sb("xi", [128, 2 * (TCH + 3)], BF16)
    carry = p.sb("carry", [128, SI * DS], F32)
    cdiag = p.sb("cdiag", [128, 2 * 512], BF16)
    inws = p.sb("inws", [128, 2 * 1024], BF16)
    outws = p.sb("outws", [128, 2 * 2048], BF16)
    delta = p.sb("delta", [128, TCH], BF16)
    du_sb = p.sb("du", [128, TCH], BF16)
    sp_r = p.sb("spr", [128, TCH], F32)
    sp_b = p.sb("spb", [128, TCH], F32)
    da_f = p.sb("daf", [128, 2 * TCH], BF16)
    b_f = p.sb("bf_", [128, 2 * TCH], BF16)
    h_f = p.sb("hf", [128, 2 * TCH], BF16)
    prod = p.sb("prod", [128, 2 * TCH], BF16)
    zsil = p.sb("zsil", [128, TCH], BF16)
    mo_sb = p.sb("mo", [128, 1024], F32)
    onec_b = p.sb("onecb", [1, 128], BF16)
    t_ocb = p.memset(onec_b[:], 1.0)

    t_carry0 = p.memset(carry[:], 0.0, eng="vector")
    carry_toks = {}
    xi_carry_toks = {}
    xi_tail = p.sb("xitail", [128, SI * 3], BF16)
    t_xit0 = p.memset(xi_tail[:], 0.0, eng="vector")

    xnc_prev = [None] * DK
    u_prev = [None] * SI
    y_prev = [None] * SI
    yacc_fin = [None] * SI
    brep_prev = [None] * NG
    crep_prev = [None] * NG
    xi_prev = [None, None]
    cd_prev = [None, None]
    inws_prev = [None, None]
    outws_prev = [None, None]
    da_prev = [None, None]
    b_prev = [None, None]
    h_prev = [None, None]
    prod_prev = [None, None]
    delta_prev = [None]
    du_prev = [None]
    sp_prev = [None]
    xdbl_prev = [None]
    zsil_prev = [None]
    mo_prev = [None]
    pa_free = None
    pb_free = None
    pc_free = None

    for ch in range(NCH):
        c0 = ch * TCH
        txn = []
        for k in range(DK):
            t = p.dma(xnc[:, k * TCH:(k + 1) * TCH], xn_in[k * 128:(k + 1) * 128, c0:c0 + TCH],
                      deps=[xnc_prev[k]], key=("xnc", k))
            txn.append(t)
        xpw_toks = []
        for s in range(SI):
            r = s % 2
            til = p.dma(inws[:, r * 1024:(r + 1) * 1024],
                        inw_i[:, :, s * 128:(s + 1) * 128].rearrange("k p c -> p k c"),
                        deps=[inws_prev[r], t_pre], key=("inws", r))
            mt = []
            for k in range(DK):
                tm = p.mm(PA[:, 0:TCH], inws[:, r * 1024 + k * 128:r * 1024 + (k + 1) * 128],
                          xnc[:, k * TCH:(k + 1) * TCH], start=(k == 0), stop=(k == DK - 1),
                          deps=[til, txn[k], pa_free])
                mt.append(tm)
            thd = p.copy(xi_sb[:, r * (TCH + 3):r * (TCH + 3) + 3], xi_tail[:, s * 3:(s + 1) * 3],
                         deps=[xi_prev[r], xi_carry_toks.get(s), t_xit0], eng="vector")
            tev = p.copy(xi_sb[:, r * (TCH + 3) + 3:(r + 1) * (TCH + 3)], PA[:, 0:TCH],
                         deps=[mt[-1], xi_prev[r]])
            pa_free = tev
            ttl = p.copy(xi_tail[:, s * 3:(s + 1) * 3],
                         xi_sb[:, r * (TCH + 3) + TCH:r * (TCH + 3) + TCH + 3],
                         deps=[tev], eng="vector")
            xi_carry_toks[s] = ttl
            tcd = p.dma(cdiag[:, (s % 2) * 512:(s % 2) * 512 + 512],
                        cd_i[s, :, :].rearrange("(j p) c -> p j c", p=128),
                        deps=[cd_prev[s % 2], t_pre], key=("cdg", s % 2))
            mc = []
            for j in range(DC):
                tm = p.mm(PB[:, 0:TCH], cdiag[:, (s % 2) * 512 + j * 128:(s % 2) * 512 + (j + 1) * 128],
                          xi_sb[:, r * (TCH + 3) + j:r * (TCH + 3) + j + TCH],
                          start=(j == 0), stop=(j == DC - 1), deps=[tcd, thd, tev, pb_free])
                mc.append(tm)
            cd_prev[s % 2] = mc[-1]
            xi_prev[r] = (mc[-1], ttl)
            tu = p.act(u_sb[:, s * TCH:(s + 1) * TCH], PB[:, 0:TCH], AF.Silu,
                       bias=cvb_sb[:, s:s + 1], deps=[mc[-1], t_cvb, u_prev[s]])
            pb_free = tu
            u_prev[s] = tu
            inws_prev[r] = mt[-1]
            txp = p.mm(PC[0:96, 0:TCH], xpw_sb[:, s * 96:(s + 1) * 96], u_sb[:, s * TCH:(s + 1) * TCH],
                       start=(s == 0), stop=(s == SI - 1), deps=[tu, t_xpw[s], pc_free])
            xpw_toks.append(txp)
            pc_free = None
        txd = p.copy(xdbl[0:96, :], PC[0:96, 0:TCH], deps=[xpw_toks[-1], xdbl_prev[0]])
        xdbl_prev[0] = txd
        pc_free = txd
        if debug and ch == 0:
            out_toks.append(p.dma(dbg_u, u_sb[:], deps=[u_prev[s2] for s2 in range(SI)], key=("dbg",)))
            out_toks.append(p.dma(dbg_xdbl, xdbl[:], deps=[txd], key=("dbg",)))
        for ng in range(2):
            br_toks = [None] * NG
            cr_toks = [None] * NG
            for nl in range(NG):
                n = ng * NG + nl
                tm = p.mm(PA[:, 0:TCH], sel_sb[64:96, n * 128:(n + 1) * 128], xdbl[64:96, :],
                          start=True, stop=True, deps=[txd, t_sel, pa_free])
                te = p.copy(brep[:, nl * TCH:(nl + 1) * TCH], PA[:, 0:TCH],
                            deps=[tm, brep_prev[nl]], eng="vector")
                pa_free = te
                br_toks[nl] = te
                brep_prev[nl] = te
                tm = p.mm(PB[:, 0:TCH], sel_sb[64:96, (16 + n) * 128:(16 + n + 1) * 128], xdbl[64:96, :],
                          start=True, stop=True, deps=[txd, t_sel, pb_free])
                te = p.copy(crep[:, nl * TCH:(nl + 1) * TCH], PB[:, 0:TCH],
                            deps=[tm, crep_prev[nl]], eng="vector")
                pb_free = te
                cr_toks[nl] = te
                crep_prev[nl] = te
            for s in range(SI):
                tdm = p.mm(PA[:, 0:TCH], dtw_sb[:, s * 128:(s + 1) * 128], xdbl[0:64, :],
                           start=True, stop=True, deps=[t_dtw, txd, pa_free])
                t1 = p.act(sp_r[:], PA[:, 0:TCH], AF.Relu, bias=dtb_sb[:, s:s + 1],
                           deps=[tdm, t_dtb, sp_prev[0]])
                t2 = p.act(sp_b[:], PA[:, 0:TCH], AF.Abs, bias=dtb_sb[:, s:s + 1], deps=[tdm, t1])
                pa_free = t2
                t3 = p.act(sp_b[:], sp_b[:], AF.Exp, scale=-1.0, deps=[t2])
                t4 = p.act(sp_b[:], sp_b[:], AF.Ln, bias=1.0, deps=[t3])
                t5 = p.tt(delta[:], sp_r[:], sp_b[:], ALU.add, deps=[t4, t1, delta_prev[0]])
                sp_prev[0] = t5
                t6 = p.tt(du_sb[:], delta[:], u_sb[:, s * TCH:(s + 1) * TCH], ALU.mult,
                          deps=[t5, u_prev[s], du_prev[0]])
                last_da_use = None
                last_tb = None
                ymm = None
                for nl in range(NG):
                    n = ng * NG + nl
                    rr = nl % 2
                    cidx = s * DS + n
                    tda = p.act(da_f[:, rr * TCH:(rr + 1) * TCH], delta[:], AF.Exp,
                                scale=a_sb[:, cidx:cidx + 1], deps=[t5, t_a, da_prev[rr]])
                    tb = p.tt(b_f[:, rr * TCH:(rr + 1) * TCH], du_sb[:],
                              brep[:, nl * TCH:(nl + 1) * TCH],
                              ALU.mult, deps=[t6, br_toks[nl], b_prev[rr]])
                    th = p.scan(h_f[:, rr * TCH:(rr + 1) * TCH], da_f[:, rr * TCH:(rr + 1) * TCH],
                                b_f[:, rr * TCH:(rr + 1) * TCH], carry[:, cidx:cidx + 1],
                                deps=[tda, tb, carry_toks.get(cidx, t_carry0), h_prev[rr]])
                    tcar = p.copy(carry[:, cidx:cidx + 1], h_f[:, rr * TCH + TCH - 1:rr * TCH + TCH],
                                  deps=[th], eng="gpsimd")
                    carry_toks[cidx] = tcar
                    tpr = p.tt(prod[:, rr * TCH:(rr + 1) * TCH], h_f[:, rr * TCH:(rr + 1) * TCH],
                               crep[:, nl * TCH:(nl + 1) * TCH],
                               ALU.mult, deps=[th, cr_toks[nl], prod_prev[rr]])
                    ymm = p.mm(PC[:, 0:TCH], id_sb[:], prod[:, rr * TCH:(rr + 1) * TCH],
                               start=(nl == 0), stop=(nl == NG - 1),
                               deps=[tpr, t_id, pc_free if nl == 0 else None])
                    prod_prev[rr] = ymm
                    da_prev[rr] = th
                    b_prev[rr] = th
                    h_prev[rr] = (tpr, tcar)
                    last_da_use = th
                    last_tb = tb
                if ng == 0:
                    te_y = p.copy(yacc[:, s * TCH:(s + 1) * TCH], PC[:, 0:TCH],
                                  deps=[ymm, y_prev[s]])
                else:
                    te_y = p.tt(yacc[:, s * TCH:(s + 1) * TCH], yacc[:, s * TCH:(s + 1) * TCH],
                                PC[:, 0:TCH], ALU.add, deps=[ymm, yacc_fin[s]])
                pc_free = te_y
                delta_prev[0] = last_da_use
                du_prev[0] = last_tb
                yacc_fin[s] = te_y
        for s in range(SI):
            tud = p.stt(yacc[:, s * TCH:(s + 1) * TCH], u_sb[:, s * TCH:(s + 1) * TCH],
                        dp_sb[:, s:s + 1], yacc[:, s * TCH:(s + 1) * TCH], ALU.mult, ALU.add,
                        deps=[yacc_fin[s], t_dp, u_prev[s]])
            r = s % 2
            til = p.dma(inws[:, r * 1024:(r + 1) * 1024],
                        inw_i[:, :, (SI + s) * 128:(SI + s + 1) * 128].rearrange("k p c -> p k c"),
                        deps=[inws_prev[r], t_pre], key=("inws", r))
            mt = []
            for k in range(DK):
                tm = p.mm(PB[:, 0:TCH], inws[:, r * 1024 + k * 128:r * 1024 + (k + 1) * 128],
                          xnc[:, k * TCH:(k + 1) * TCH], start=(k == 0), stop=(k == DK - 1),
                          deps=[til, txn[k], pb_free])
                mt.append(tm)
            inws_prev[r] = mt[-1]
            tz = p.act(zsil[:], PB[:, 0:TCH], AF.Silu, deps=[mt[-1], zsil_prev[0]])
            pb_free = tz
            ty = p.tt(yacc[:, s * TCH:(s + 1) * TCH], yacc[:, s * TCH:(s + 1) * TCH], zsil[:],
                      ALU.mult, deps=[tz, tud])
            zsil_prev[0] = ty
            y_prev[s] = ty
        if debug and ch == 0:
            out_toks.append(p.dma(dbg_y, yacc[:], deps=[y_prev[s2] for s2 in range(SI)], key=("dbg",)))
        for ot in range(DK):
            r = ot % 2
            tol = p.dma(outws[:, r * 2048:(r + 1) * 2048],
                        outw_i[:, :, ot * 128:(ot + 1) * 128].rearrange("s p c -> p s c"),
                        deps=[outws_prev[r], t_pre], key=("ow", r))
            mt = []
            for s in range(SI):
                tm = p.mm(PA[:, 0:TCH], outws[:, r * 2048 + s * 128:r * 2048 + (s + 1) * 128],
                          yacc[:, s * TCH:(s + 1) * TCH], start=(s == 0), stop=(s == SI - 1),
                          deps=[tol, y_prev[s], pa_free])
                mt.append(tm)
            tev = p.copy(mo_sb[:], PA[:, 0:TCH], deps=[mt[-1], mo_prev[0]], eng="vector")
            pa_free = tev
            tdm = p.dma(mout[ot * 128:(ot + 1) * 128, c0:c0 + TCH], mo_sb[:], deps=[tev], key=("mo",))
            mo_prev[0] = tdm
            outws_prev[r] = tdm
            out_toks.append(tdm)
        for k in range(DK):
            xnc_prev[k] = out_toks[-1]
    p.done(out_toks)
    return p.build()


def build_l4():
    p = Prog()
    T = HALF
    x1_in = p.din("x1", [D, T], F32)
    xn1_in = p.din("xn1", [D, T], BF16)
    xc0_in = p.din("xn1c0", [128, DK], F32)
    mf_in = p.din("mf", [D, T], F32)
    mb_in = p.din("mb", [D, T], F32)
    wr_in = p.din("wr_t", [D, EM], F32)
    at_in = p.din("a_t", [EM, D, RM], F32)
    bwt_in = p.din("bw_t", [EM, RM, D], F32)

    x2_out = p.dout("x2", [D, T], F32)

    PA = p.ps("PA", [128, 1024])
    PB = p.ps("PB", [128, 1024])
    PD = p.ps("PD", [128, 512])
    out_toks = []

    xn1 = p.sb("xn1_s4", [128, DK * T], BF16)
    txn = [p.dma(xn1[:, k * T:(k + 1) * T], xn1_in[k * 128:(k + 1) * 128, :], key=("xn1l", k)) for k in range(DK)]

    xc0 = p.sb("xc0_s4", [128, DK], F32)
    xc0b = p.sb("xc0b", [128, DK], BF16)
    t1 = p.dma(xc0[:], xc0_in, key=("xc0l",))
    t2 = p.copy(xc0b[:], xc0[:], deps=[t1])
    wr_f = p.sb("wrf", [128, DK * EM], F32)
    wr_b = p.sb("wrb", [128, DK * EM], BF16)
    t3 = p.dma(wr_f[:], wr_in.rearrange("(k p) e -> p k e", p=128), key=("wr",))
    t4 = p.copy(wr_b[:], wr_f[:], deps=[t3])
    lg = p.sb("lg", [1, 4 * EM], F32)
    mml = []
    for k in range(DK):
        tm = p.mm(PD[0:1, 0:EM], xc0b[:, k:k + 1], wr_b[:, k * EM:(k + 1) * EM],
                  start=(k == 0), stop=(k == DK - 1), deps=[t2, t4])
        mml.append(tm)
    L, S_, M2_, G_ = 0, EM, 2 * EM, 3 * EM
    t5 = p.copy(lg[0:1, L:L + EM], PD[0:1, 0:EM], deps=[mml[-1]], eng="vector")
    m1 = p.sb("m1", [1, 4], F32)
    t6 = p.red(m1[0:1, 0:1], lg[0:1, L:L + EM], ALU.max, deps=[t5])
    t7 = p.ts(lg[0:1, S_:S_ + EM], lg[0:1, L:L + EM], m1[0:1, 0:1], ALU.subtract, deps=[t6, t5])
    t8 = p.ts(lg[0:1, M2_:M2_ + EM], lg[0:1, S_:S_ + EM], 0.0, ALU.is_ge, deps=[t7])
    t9 = p.stt(lg[0:1, M2_:M2_ + EM], lg[0:1, M2_:M2_ + EM], -1e30, lg[0:1, L:L + EM],
               ALU.mult, ALU.add, deps=[t8, t5])
    t10 = p.red(m1[0:1, 1:2], lg[0:1, M2_:M2_ + EM], ALU.max, deps=[t9])
    t11 = p.ts(lg[0:1, M2_:M2_ + EM], lg[0:1, L:L + EM], m1[0:1, 1:2], ALU.is_ge, deps=[t10, t9])
    t12 = p.act(lg[0:1, G_:G_ + EM], lg[0:1, S_:S_ + EM], AF.Exp, deps=[t7, t11])
    t13 = p.tt(lg[0:1, G_:G_ + EM], lg[0:1, G_:G_ + EM], lg[0:1, M2_:M2_ + EM], ALU.mult,
               deps=[t12, t11])
    t14 = p.red(m1[0:1, 2:3], lg[0:1, G_:G_ + EM], ALU.add, deps=[t13])
    t15 = p.recip(m1[0:1, 3:4], m1[0:1, 2:3], deps=[t14])
    t16 = p.ts(lg[0:1, G_:G_ + EM], lg[0:1, G_:G_ + EM], m1[0:1, 3:4], ALU.mult, deps=[t15, t13])
    gb = p.sb("gb", [1, EM], BF16)
    t17 = p.copy(gb[:], lg[0:1, G_:G_ + EM], deps=[t16], eng="vector")
    ones64 = p.sb("ones64", [1, 64], BF16)
    t18 = p.memset(ones64[:], 1.0)
    gcol = p.sb("gcol", [64, EM], F32)
    gcol_toks = []
    for e in range(EM):
        tm = p.mm(PD[0:64, 0:1], ones64[:], gb[0:1, e:e + 1], start=True, stop=True,
                  deps=[t17, t18] + ([gcol_toks[-1]] if gcol_toks else []))
        te = p.copy(gcol[:, e:e + 1], PD[0:64, 0:1], deps=[tm], eng="vector")
        gcol_toks.append(te)

    at_sb = p.sb("at", [128, EM * DK * RM], BF16)
    at_f = p.sb("atf", [128, 2 * RM], F32)
    ap_prev = [None, None]
    at_toks = {}
    for e in range(EM):
        for k in range(DK):
            r = (e * DK + k) % 2
            tl = p.dma(at_f[:, r * RM:(r + 1) * RM], at_in[e, k * 128:(k + 1) * 128, :],
                       deps=[ap_prev[r]], key=("at", r))
            tc = p.copy(at_sb[:, (e * DK + k) * RM:(e * DK + k + 1) * RM], at_f[:, r * RM:(r + 1) * RM],
                        deps=[tl])
            ap_prev[r] = tc
            at_toks[(e, k)] = tc
    bwt_sb = p.sb("bwt", [64, EM * D], BF16)
    bwt_f = p.sb("bwtf", [64, 2 * D], F32)
    bp_prev = [None, None]
    bwt_toks = {}
    for e in range(EM):
        r = e % 2
        tl = p.dma(bwt_f[:, r * D:(r + 1) * D], bwt_in[e, :, :], deps=[bp_prev[r]], key=("bw", r))
        tc = p.copy(bwt_sb[:, e * D:(e + 1) * D], bwt_f[:, r * D:(r + 1) * D], deps=[tl])
        bp_prev[r] = tc
        bwt_toks[e] = tc

    midg = p.sb("midg", [64, EM * T], BF16)
    midg_toks = {}
    pa_free = None
    for e in range(EM):
        for th in range(2):
            mt = []
            for k in range(DK):
                tm = p.mm(PA[0:64, 0:1024], at_sb[:, (e * DK + k) * RM:(e * DK + k + 1) * RM],
                          xn1[:, k * T + th * 1024:k * T + (th + 1) * 1024],
                          start=(k == 0), stop=(k == DK - 1),
                          deps=[at_toks[(e, k)], txn[k], pa_free])
                mt.append(tm)
            te = p.ts(midg[0:64, e * T + th * 1024:e * T + (th + 1) * 1024], PA[0:64, 0:1024],
                      gcol[:, e:e + 1], ALU.mult, deps=[mt[-1], gcol_toks[e]])
            pa_free = te
            midg_toks[(e, th)] = te

    xtmp = p.sb("xtmp", [128, 2 * T], F32)
    mtmp = p.sb("mtmp", [128, 2 * T], F32)
    mtmp2 = p.sb("mtmp2", [128, 2 * T], F32)
    o_sb = p.sb("osb", [128, 2 * T], F32)
    prev_r = [None, None]
    pab_free = [pa_free, None]
    for ot in range(DK):
        r = ot % 2
        mt = []
        for th in range(2):
            pa = PA if th == 0 else PB
            for e in range(EM):
                tm = p.mm(pa[:, 0:1024], bwt_sb[0:64, e * D + ot * 128:e * D + (ot + 1) * 128],
                          midg[0:64, e * T + th * 1024:e * T + (th + 1) * 1024],
                          start=(e == 0), stop=(e == EM - 1),
                          deps=[bwt_toks[e], midg_toks[(e, th)], pab_free[th]])
                mt.append(tm)
        tmf = p.dma(mtmp[:, r * T:(r + 1) * T], mf_in[ot * 128:(ot + 1) * 128, :], deps=[prev_r[r]], key=("mf", r))
        tmb = p.dma(mtmp2[:, r * T:(r + 1) * T], mb_in[ot * 128:(ot + 1) * 128, :], deps=[prev_r[r]], key=("mb", r))
        tx1 = p.dma(xtmp[:, r * T:(r + 1) * T], x1_in[ot * 128:(ot + 1) * 128, :], deps=[prev_r[r]], key=("x1l", r))
        ta = p.tt(mtmp[:, r * T:(r + 1) * T], mtmp[:, r * T:(r + 1) * T], mtmp2[:, r * T:(r + 1) * T],
                  ALU.add, deps=[tmf, tmb])
        tb = p.stt(xtmp[:, r * T:(r + 1) * T], mtmp[:, r * T:(r + 1) * T], 0.5,
                   xtmp[:, r * T:(r + 1) * T], ALU.mult, ALU.add, deps=[ta, tx1])
        tc = p.tt(xtmp[:, r * T:(r + 1) * T], xtmp[:, r * T:(r + 1) * T], xn1[:, ot * T:(ot + 1) * T],
                  ALU.add, deps=[tb, txn[ot]])
        td1 = p.tt(o_sb[:, r * T:r * T + 1024], xtmp[:, r * T:r * T + 1024], PA[:, 0:1024], ALU.add,
                   deps=[tc, mt[EM - 1]])
        td2 = p.tt(o_sb[:, r * T + 1024:r * T + 2048], xtmp[:, r * T + 1024:r * T + 2048],
                   PB[:, 0:1024], ALU.add, deps=[tc, mt[-1], td1])
        pab_free = [td1, td2]
        tdm = p.dma(x2_out[ot * 128:(ot + 1) * 128, :], o_sb[:, r * T:(r + 1) * T], deps=[td1, td2], key=("x2o", r))
        prev_r[r] = tdm
        out_toks.append(tdm)

    p.done(out_toks)
    return p.build()


def _t(a):
    return np.ascontiguousarray(np.asarray(a).T)


DBG = None


def kernel(**kw):
    x = np.asarray(kw["L0_attn_x"], dtype=np.float32)
    cores = list(range(NCORE))

    nc1 = build_l1()
    wg_t = _t(kw["L0_ffn_w_gate"]).astype(np.int32)
    wu_t = _t(kw["L0_ffn_w_up"]).astype(np.int32)
    wd_t = _t(kw["L0_ffn_w_down"]).astype(np.int32)
    maps1 = []
    for c in cores:
        b, h = c // 2, c % 2
        maps1.append({
            "xT": _t(x[b, h * HALF:(h + 1) * HALF, :]),
            "Wq": np.asarray(kw["L0_attn_Wq"], np.float32),
            "Wk": np.asarray(kw["L0_attn_Wk"], np.float32),
            "Wv": np.asarray(kw["L0_attn_Wv"], np.float32),
            "wg_t": wg_t, "wu_t": wu_t, "wd_t": wd_t,
        })
    r1 = run_bass_kernel_spmd(nc1, maps1, core_ids=cores).results

    nc2 = build_l2()
    bo_pm = np.ascontiguousarray(np.asarray(kw["L0_attn_bo"], np.float32).reshape(DK, 128).T)
    maps2 = []
    for c in cores:
        b, h = c // 2, c % 2
        maps2.append({
            "ctxp0": r1[b * 2 + 0]["ctxp"], "ctxp1": r1[b * 2 + 1]["ctxp"],
            "kden0": r1[b * 2 + 0]["kdenp"], "kden1": r1[b * 2 + 1]["kdenp"],
            "qexp": r1[c]["qexp"], "qden": r1[c]["qden"], "xpf": r1[c]["xpf"],
            "Wo": np.asarray(kw["L0_attn_Wo"], np.float32), "bo_pm": bo_pm,
        })
    r2 = run_bass_kernel_spmd(nc2, maps2, core_ids=cores).results

    nc3 = build_l3()
    maps3 = []
    w3 = {}
    for d, sfx in ((0, "_f"), (1, "_b")):
        convW = np.asarray(kw[f"L1_attn_convW{sfx}"], np.float32)
        cdiag = np.zeros((SI, DC, 128, 128), np.float32)
        for s in range(SI):
            for j in range(DC):
                np.fill_diagonal(cdiag[s, j], convW[s * 128:(s + 1) * 128, 0, j])
        sel_bc = np.zeros((128, 32 * 128), np.float32)
        for j in range(32):
            sel_bc[64 + j, j * 128:(j + 1) * 128] = 1.0
        alog = np.asarray(kw[f"L1_attn_Alog{sfx}"], np.float32)
        a_lay = np.ascontiguousarray(alog.reshape(SI, 128, DS).transpose(1, 0, 2).reshape(128, SI * DS))
        w3[d] = {
            "inW_t": _t(kw[f"L1_attn_inW{sfx}"]),
            "outW_t": _t(kw[f"L1_attn_outW{sfx}"]),
            "xpW_t": _t(kw[f"L1_attn_xpW{sfx}"]),
            "dtW_t": _t(kw[f"L1_attn_dtW{sfx}"]),
            "A_lay": a_lay,
            "dtB_pm": np.ascontiguousarray(np.asarray(kw[f"L1_attn_dtB{sfx}"], np.float32).reshape(SI, 128).T),
            "convB_pm": np.ascontiguousarray(np.asarray(kw[f"L1_attn_convB{sfx}"], np.float32).reshape(SI, 128).T),
            "Dp_pm": np.ascontiguousarray(np.asarray(kw[f"L1_attn_Dp{sfx}"], np.float32).reshape(SI, 128).T),
            "convdiag": cdiag,
            "sel_bc": sel_bc,
            "ident": np.eye(128, dtype=np.float32),
        }
    for c in cores:
        b, d = c // 2, c % 2
        xnf = np.concatenate([np.asarray(r2[b * 2 + 0]["xn1"]), np.asarray(r2[b * 2 + 1]["xn1"])], axis=1)
        if d == 1:
            xnf = xnf[:, ::-1]
        m = {"xnf": np.ascontiguousarray(xnf)}
        m.update(w3[d])
        maps3.append(m)
    r3 = run_bass_kernel_spmd(nc3, maps3, core_ids=cores).results

    nc4 = build_l4()
    a_t = np.ascontiguousarray(np.asarray(kw["L1_ffn_A"], np.float32).transpose(0, 2, 1))
    bw_t = np.ascontiguousarray(np.asarray(kw["L1_ffn_Bw"], np.float32).transpose(0, 2, 1))
    wr_t = _t(kw["L1_ffn_Wr"])
    maps4 = []
    for c in cores:
        b, h = c // 2, c % 2
        mf = np.asarray(r3[b * 2 + 0]["mout"])
        mb = np.asarray(r3[b * 2 + 1]["mout"])[:, ::-1]
        maps4.append({
            "x1": r2[c]["x1"], "xn1": r2[c]["xn1"], "xn1c0": r2[b * 2 + 0]["xn1c0"],
            "mf": np.ascontiguousarray(mf[:, h * HALF:(h + 1) * HALF]),
            "mb": np.ascontiguousarray(mb[:, h * HALF:(h + 1) * HALF]),
            "wr_t": wr_t, "a_t": a_t, "bw_t": bw_t,
        })
    r4 = run_bass_kernel_spmd(nc4, maps4, core_ids=cores).results

    out = np.empty((B, N, D), np.float32)
    for c in cores:
        b, h = c // 2, c % 2
        out[b, h * HALF:(h + 1) * HALF, :] = np.asarray(r4[c]["x2"]).T
    if DBG is not None:
        DBG.update(r1=r1, r2=r2, r3=r3, r4=r4)
    return out


# version 07
# speedup vs baseline: 1.2562x; 1.0740x over previous
import numpy as np
import ml_dtypes
from contextlib import ExitStack
import concourse.bass as bass
import concourse.mybir as mybir
from concourse.bass_utils import run_bass_kernel_spmd

F32 = mybir.dt.float32
BF16 = mybir.dt.bfloat16
I32 = mybir.dt.int32
AF = mybir.ActivationFunctionType
ALU = mybir.AluOpType
AX = mybir.AxisListType
NPBF = ml_dtypes.bfloat16

B, N, D = 4, 4096, 1024
HALF = N // 2
DK = D // 128
H3 = 3072
HT3 = H3 // 128
QSC = float(1.0 / 127.0)
DI, DS, DTR, DC = 2048, 16, 64, 4
SI = DI // 128
EM, RM = 8, 64
LN_EPS = 1e-5
NCORE = 8
TCH = 1024
NCH = N // TCH


class Prog:
    ENGS = ("sync", "vector", "scalar", "tensor", "gpsimd")

    def __init__(self):
        self.nc = bass.Bass("TRN2", target_bir_lowering=False)
        self.ctx = ExitStack()
        self.q = {e: [] for e in self.ENGS}
        self.cnt = {e: 0 for e in self.ENGS}

    def sb(self, name, shape, dt):
        return self.ctx.enter_context(self.nc.sbuf_tensor(name, shape, dt))

    def ps(self, name, shape, dt=F32):
        return self.ctx.enter_context(self.nc.psum_tensor(name, shape, dt))

    def din(self, name, shape, dt):
        return self.nc.dram_tensor(name, shape, dt, kind="ExternalInput").ap()

    def dout(self, name, shape, dt):
        return self.nc.dram_tensor(name, shape, dt, kind="ExternalOutput").ap()

    def dtmp(self, name, shape, dt):
        return self.nc.dram_tensor(name, shape, dt, kind="Internal").ap()

    @staticmethod
    def _is_tok(d):
        return (isinstance(d, tuple) and len(d) == 2 and isinstance(d[1], int)
                and (isinstance(d[0], str)
                     or (isinstance(d[0], tuple) and len(d[0]) == 2 and d[0][0] == "lane")))

    @staticmethod
    def _flat(deps):
        out = []
        for d in deps:
            if d is None:
                continue
            if Prog._is_tok(d):
                out.append(d)
            else:
                out.extend(Prog._flat(d))
        return out

    def _op(self, eng, fn, deps, dma=False, lane=None):
        if lane is not None:
            key = ("lane", lane)
            self.cnt.setdefault(key, 0)
            self.cnt[key] += 16
            self.q[eng].append((fn, Prog._flat(deps), 16, key))
            return (key, self.cnt[key])
        inc = 16 if dma else 1
        self.cnt[eng] += inc
        self.q[eng].append((fn, Prog._flat(deps), inc, None))
        return (eng, self.cnt[eng])

    def dma(self, out, in_, deps=(), eng="sync", key=None):
        assert key is not None, "dma requires a lane key"
        return self._op(eng, lambda e, o=out, i=in_: e.dma_start(out=o, in_=i), deps,
                        dma=True, lane=key)

    def mm(self, out, lhsT, rhs, start, stop, deps=()):
        nf = int(out.ap[-1][1])
        if nf <= 512:
            return self._op("tensor", lambda e, o=out, l=lhsT, r=rhs, s1=start, s2=stop:
                            e.matmul(out=o, lhsT=l, rhs=r, start=s1, stop=s2), deps)
        tok = None
        for c in range(0, nf, 512):
            w = min(512, nf - c)
            tok = self._op("tensor",
                           lambda e, o=out[:, c:c + w], l=lhsT, r=rhs[:, c:c + w], s1=start, s2=stop:
                           e.matmul(out=o, lhsT=l, rhs=r, start=s1, stop=s2), deps)
        return tok

    def tt(self, out, a, b, op, deps=(), eng="vector"):
        return self._op(eng, lambda e, o=out, x=a, y=b, p=op: e.tensor_tensor(out=o, in0=x, in1=y, op=p), deps)

    def ts(self, out, a, scalar, op, deps=(), eng="vector"):
        return self._op(eng, lambda e, o=out, x=a, s=scalar, p=op:
                        e.tensor_single_scalar(out=o, in_=x, scalar=s, op=p), deps)

    def ts2(self, out, a, s1, s2, op0, op1, deps=(), eng="vector"):
        return self._op(eng, lambda e, o=out, x=a, a1=s1, a2=s2, p0=op0, p1=op1:
                        e.tensor_scalar(out=o, in0=x, scalar1=a1, scalar2=a2, op0=p0, op1=p1), deps)

    def stt(self, out, in0, scalar, in1, op0, op1, deps=(), eng="vector"):
        return self._op(eng, lambda e, o=out, x=in0, s=scalar, y=in1, p0=op0, p1=op1:
                        e.scalar_tensor_tensor(out=o, in0=x, scalar=s, in1=y, op0=p0, op1=p1), deps)

    def act(self, out, in_, func, deps=(), scale=None, bias=None):
        kw = {}
        if scale is not None:
            kw["scale"] = scale
        if bias is not None:
            kw["bias"] = bias
        return self._op("scalar", lambda e, o=out, i=in_, f=func, k=kw:
                        e.activation(out=o, in_=i, func=f, **k), deps)

    def copy(self, out, in_, deps=(), eng="scalar"):
        if eng == "scalar":
            return self._op(eng, lambda e, o=out, i=in_: e.copy(out=o, in_=i), deps)
        return self._op(eng, lambda e, o=out, i=in_: e.tensor_copy(out=o, in_=i), deps)

    def red(self, out, in_, op, deps=()):
        return self._op("vector", lambda e, o=out, i=in_, p=op:
                        e.tensor_reduce(out=o, in_=i, axis=AX.X, op=p), deps)

    def scan(self, out, a, b, initial, deps=()):
        return self._op("vector", lambda e, o=out, x=a, y=b, h=initial:
                        e.tensor_tensor_scan(out=o, data0=x, data1=y, initial=h,
                                             op0=ALU.mult, op1=ALU.add), deps)

    def memset(self, ap, v, deps=(), eng="gpsimd"):
        return self._op(eng, lambda e, a=ap, c=v: e.memset(a, c), deps)

    def recip(self, out, in_, deps=()):
        return self._op("vector", lambda e, o=out, i=in_: e.reciprocal(out=o, in_=i), deps)

    def barrier(self):
        toks = [(k, v) for k, v in self.cnt.items() if v > 0]
        for e in self.ENGS:
            self.q[e].append((None, list(toks), 0, None))
        return toks

    def consts(self, values):
        vals = sorted(set(float(v) for v in values))
        if not vals:
            return
        cp = self.sb("constpool", [128, len(vals)], F32)
        toks = []
        for i, v in enumerate(vals):
            toks.append(self.memset(cp[:, i:i + 1], v))
            self.nc.const_aps.register(F32, v, cp[:, i:i + 1])
        for e in self.ENGS:
            self.q[e].append((None, list(toks), 0, None))

    def done(self, tokens):
        self.q["sync"].append((None, Prog._flat(tokens), 0, None))

    def build(self):
        sems = {e: self.ctx.enter_context(self.nc.semaphore(f"sem_{e}")) for e in self.ENGS}
        li = 0
        for k in self.cnt:
            if isinstance(k, tuple):
                sems[k] = self.ctx.enter_context(self.nc.semaphore(f"lane_{li}"))
                li += 1
        block = self.ctx.enter_context(self.nc.Block())

        def body(eng_name):
            def run(engine):
                waited = {}
                for fn, deps, inc, lane in self.q[eng_name]:
                    best = {}
                    for (de, dv) in deps:
                        if de == eng_name and eng_name == "tensor":
                            continue
                        if de == eng_name and fn is None:
                            continue
                        if waited.get(de, -1) >= dv:
                            continue
                        best[de] = max(best.get(de, 0), dv)
                    for de in sorted(best, key=str):
                        engine.wait_ge(sems[de], best[de])
                        waited[de] = best[de]
                    if fn is not None:
                        fn(engine).then_inc(sems[lane if lane is not None else eng_name], inc)
            return run

        block.sync(body("sync"))
        block.vector(body("vector"))
        block.scalar(body("scalar"))
        block.tensor(body("tensor"))
        block.gpsimd(body("gpsimd"))
        self.ctx.close()
        return self.nc


def conv_weight_slices(p, name, src_ap, n_slices, cols, rot=2, dt_in=F32, ctx=None):
    al = (lambda nm, sh, dt: ctx.enter_context(p.nc.sbuf_tensor(nm, sh, dt))) if ctx is not None else p.sb
    dst = al(name + "_bf", [128, n_slices * cols], BF16)
    tmp = al(name + "_tmp", [128, rot * cols], dt_in)
    toks = []
    prev = [None] * rot
    for k in range(n_slices):
        r = k % rot
        tl = p.dma(tmp[:, r * cols:(r + 1) * cols], src_ap[k * 128:(k + 1) * 128, :], deps=[prev[r]], key=(name, r))
        tc = p.copy(dst[:, k * cols:(k + 1) * cols], tmp[:, r * cols:(r + 1) * cols], deps=[tl])
        prev[r] = tc
        toks.append(tc)
    return dst, toks


def emit_ln(p, x_view, T, tag, x_deps, pa_row, pd_rep, xn_bf, extra_c0=None, ctx=None):
    al = (lambda nm, sh, dt: ctx.enter_context(p.nc.sbuf_tensor(nm, sh, dt))) if ctx is not None else p.sb
    onec_b = al(f"{tag}onecb", [128, 1], BF16)
    onec_f = al(f"{tag}onecf", [128, 1], F32)
    onerow_f = al(f"{tag}onerf", [1, 128], F32)
    t1 = p.memset(onec_b[:], 1.0)
    t2 = p.memset(onec_f[:], 1.0)
    t2b = p.memset(onerow_f[:], 1.0)
    sq = al(f"{tag}sq", [128, 2 * T], BF16)
    mean = al(f"{tag}mean", [1, T], F32)
    rstd = al(f"{tag}rstd", [1, T], F32)
    mrs = al(f"{tag}mrs", [1, T], F32)
    reps = al(f"{tag}reps", [128, 2 * T], F32)

    pa_free = None
    sum_toks = []
    for c in range(0, T, 512):
        for k in range(DK):
            tm = p.mm(pa_row[0:1, 0:512],
                      onec_f[:], x_view(k)[:, c:c + 512], start=(k == 0), stop=(k == DK - 1),
                      deps=[t2, x_deps[k], pa_free])
            sum_toks.append(tm)
        te = p.copy(mean[0:1, c:c + 512], pa_row[0:1, 0:512], deps=[sum_toks[-1]], eng="vector")
        pa_free = te
        sum_toks.append(te)
    t_mean = p.ts(mean[:], mean[:], float(1.0 / D), ALU.mult, deps=[sum_toks[-1]])

    var_toks = []
    last_use = [None, None]
    for c in range(0, T, 512):
        for k in range(DK):
            r = k % 2
            tq = p.act(sq[:, r * T:r * T + 512], x_view(k)[:, c:c + 512], AF.Square,
                       deps=[x_deps[k], last_use[r]])
            tm = p.mm(pa_row[0:1, 0:512], onec_b[:], sq[:, r * T:r * T + 512],
                      start=(k == 0), stop=(k == DK - 1), deps=[t1, tq, pa_free])
            last_use[r] = tm
            var_toks.append(tm)
        te = p.copy(rstd[0:1, c:c + 512], pa_row[0:1, 0:512], deps=[var_toks[-1]], eng="vector")
        pa_free = te
        var_toks.append(te)
    tv1 = p.stt(mrs[:], mean[:], -1.0, mean[:], ALU.mult, ALU.mult, deps=[t_mean, var_toks[-1]])
    tv2 = p.stt(rstd[:], rstd[:], float(1.0 / D), mrs[:], ALU.mult, ALU.add, deps=[tv1, var_toks[-1]])
    tv3 = p.act(rstd[:], rstd[:], AF.Ln, bias=float(LN_EPS), deps=[tv2])
    tv4 = p.act(rstd[:], rstd[:], AF.Exp, scale=-0.5, deps=[tv3])
    tv5 = p.stt(mrs[:], mean[:], -1.0, rstd[:], ALU.mult, ALU.mult, deps=[tv4, tv1])
    rep_toks = []
    for (row, base) in ((rstd, 0), (mrs, T)):
        for c in range(0, T, 512):
            tm = p.mm(pd_rep[:, 0:512], onerow_f[:], row[0:1, c:c + 512], start=True, stop=True,
                      deps=[tv4, tv5, t2b] + ([rep_toks[-1]] if rep_toks else []))
            te = p.copy(reps[:, base + c:base + c + 512], pd_rep[:, 0:512], deps=[tm], eng="vector")
            rep_toks.append(te)
    t_rep = rep_toks[-1]
    out_toks = []
    for k in range(DK):
        ta = p.tt(xn_bf[:, k * T:(k + 1) * T], x_view(k), reps[:, 0:T], ALU.mult,
                  deps=[t_rep, x_deps[k]] + rep_toks)
        tb = p.tt(xn_bf[:, k * T:(k + 1) * T], xn_bf[:, k * T:(k + 1) * T], reps[:, T:2 * T], ALU.add,
                  deps=[ta])
        out_toks.append(tb)
    tc0 = None
    if extra_c0 is not None:
        cts = []
        for k in range(DK):
            tx = p.tt(extra_c0[:, k:k + 1], x_view(k)[:, 0:1], reps[:, 0:1], ALU.mult,
                      deps=[t_rep, x_deps[k]])
            ty = p.tt(extra_c0[:, k:k + 1], extra_c0[:, k:k + 1], reps[:, T:T + 1], ALU.add, deps=[tx])
            cts.append(ty)
        tc0 = cts
    return out_toks, tc0


def build_l1():
    p = Prog()
    T = HALF
    xin = p.din("xT", [D, T], F32)
    wq_in = p.din("Wq", [D, D], F32)
    wk_in = p.din("Wk", [D, D], F32)
    wv_in = p.din("Wv", [D, D], F32)
    wg_in = p.din("wg_t", [D, H3], I32)
    wu_in = p.din("wu_t", [D, H3], I32)
    wd_in = p.din("wd_t", [H3, D], I32)

    ctx_out = p.dout("ctxp", [D, D], F32)
    kden_out = p.dout("kdenp", [128, DK], F32)
    qexp_out = p.dout("qexp", [D, T], BF16)
    qden_out = p.dout("qden", [1, T], F32)
    xpf_out = p.dout("xpf", [D, T], F32)

    wg_i = p.dtmp("wg_i", [DK, 128, H3], BF16)
    wu_i = p.dtmp("wu_i", [DK, 128, H3], BF16)
    wd_i = p.dtmp("wd_i", [HT3, 128, D], BF16)

    PA = p.ps("PA", [128, 1024])
    PB = p.ps("PB", [128, 1024])
    PC = p.ps("PC", [128, 1024])
    PD = p.ps("PD", [128, 512])
    PE = p.ps("PE", [128, 512])
    p.consts([LN_EPS, -0.5, QSC])

    out_toks = []

    scA = ExitStack()
    xn = p.sb("xn", [128, DK * T], BF16)
    x_sb = scA.enter_context(p.nc.sbuf_tensor("x", [128, DK * T], F32))
    tx = [p.dma(x_sb[:, k * T:(k + 1) * T], xin[k * 128:(k + 1) * 128, :], key=("x", k)) for k in range(DK)]
    xn_toks, _ = emit_ln(p, lambda k: x_sb[:, k * T:(k + 1) * T], T, "ln", tx, PD, PE, xn, ctx=scA)
    p.barrier()
    scA.close()

    scB = ExitStack()
    wq_sb, twq = conv_weight_slices(p, "wq", wq_in, DK, 1024, ctx=scB)
    wk_sb, twk = conv_weight_slices(p, "wk", wk_in, DK, 1024, ctx=scB)
    wv_sb, twv = conv_weight_slices(p, "wv", wv_in, DK, 1024, ctx=scB)

    qexp_bf = scB.enter_context(p.nc.sbuf_tensor("qexp_s", [128, 2 * T], BF16))
    qden_sb = scB.enter_context(p.nc.sbuf_tensor("qden_s", [1, T], F32))
    tqden_z = p.memset(qden_sb[:], 0.0, eng="vector")
    onec_b = scB.enter_context(p.nc.sbuf_tensor("onecb2", [128, 1], BF16))
    t_oc = p.memset(onec_b[:], 1.0)
    q_prev = [None, None]
    qd_prev = tqden_z
    pa_free = None
    pb_free = None
    for o in range(DK):
        mm_toks = []
        for th in range(2):
            for k in range(DK):
                tm = p.mm(PA[:, 0:1024] if th == 0 else PB[:, 0:1024],
                          wq_sb[:, k * 1024 + o * 128:k * 1024 + (o + 1) * 128],
                          xn[:, k * T + th * 1024:k * T + (th + 1) * 1024],
                          start=(k == 0), stop=(k == DK - 1),
                          deps=[twq[k], xn_toks[k], pa_free if th == 0 else pb_free])
                mm_toks.append(tm)
        r = o % 2
        te1 = p.act(qexp_bf[:, r * T:r * T + 1024], PA[:, 0:1024], AF.Exp,
                    deps=[mm_toks[DK - 1], q_prev[r]])
        te2 = p.act(qexp_bf[:, r * T + 1024:r * T + 2048], PB[:, 0:1024], AF.Exp,
                    deps=[mm_toks[2 * DK - 1], te1])
        pa_free = te1
        pb_free = te2
        tden = []
        for c in range(0, T, 512):
            tm = p.mm(PD[0:1, 0:512], onec_b[:], qexp_bf[:, r * T + c:r * T + c + 512],
                      start=True, stop=True, deps=[t_oc, te1, te2] + ([tden[-1]] if tden else []))
            ta = p.tt(qden_sb[0:1, c:c + 512], qden_sb[0:1, c:c + 512], PD[0:1, 0:512], ALU.add,
                      deps=[tm, qd_prev])
            tden.append(ta)
        qd_prev = tden[-1]
        tdma = p.dma(qexp_out[o * 128:(o + 1) * 128, :], qexp_bf[:, r * T:(r + 1) * T],
                     deps=[te1, te2], key=("qexpo", r))
        q_prev[r] = tdma
        out_toks.append(tdma)
    out_toks.append(p.dma(qden_out, qden_sb[:], deps=[qd_prev], key=("qdeno",)))

    kbuf = scB.enter_context(p.nc.sbuf_tensor("kbuf", [128, 4 * 1024], BF16))
    vbuf = scB.enter_context(p.nc.sbuf_tensor("vbuf", [128, 4 * 1024], BF16))
    ones_col = scB.enter_context(p.nc.sbuf_tensor("onescol", [128, 1], BF16))
    t_ocol = p.memset(ones_col[:], 1.0)
    ctx_sb = scB.enter_context(p.nc.sbuf_tensor("ctx", [128, DK * 1024], F32))
    tz_ctx = [p.memset(ctx_sb[:, dt * 1024:(dt + 1) * 1024], 0.0, eng="vector") for dt in range(DK)]
    kv_prev = {}
    kden_toks = []
    ctx_toks = {dt: tz_ctx[dt] for dt in range(DK)}
    pa_free = pa_free
    pb_free = pb_free
    pc_free = None
    n_tiles = T // 128
    for g in range(n_tiles // 4):
        tile_toks = []
        for tt in range(4):
            tile = g * 4 + tt
            mmk = []
            mmv = []
            for k in range(DK):
                tm = p.mm(PA[:, 0:1024], xn[:, k * T + tile * 128:k * T + (tile + 1) * 128],
                          wk_sb[:, k * 1024:(k + 1) * 1024], start=(k == 0), stop=(k == DK - 1),
                          deps=[twk[k], xn_toks[k], pa_free])
                mmk.append(tm)
                tm = p.mm(PB[:, 0:1024], xn[:, k * T + tile * 128:k * T + (tile + 1) * 128],
                          wv_sb[:, k * 1024:(k + 1) * 1024], start=(k == 0), stop=(k == DK - 1),
                          deps=[twv[k], xn_toks[k], pb_free])
                mmv.append(tm)
            tke = p.act(kbuf[:, tt * 1024:(tt + 1) * 1024], PA[:, 0:1024], AF.Exp,
                        deps=[mmk[-1], kv_prev.get(("k", tt))])
            tve = p.copy(vbuf[:, tt * 1024:(tt + 1) * 1024], PB[:, 0:1024],
                         deps=[mmv[-1], kv_prev.get(("v", tt))], eng="vector")
            pa_free = tke
            pb_free = tve
            tile_toks.append((tile, tke, tve))
        for dt in range(DK):
            gm = []
            for (tile, tke, tve) in tile_toks:
                tt = tile % 4
                tm = p.mm(PC[:, 0:1024], kbuf[:, tt * 1024 + dt * 128:tt * 1024 + (dt + 1) * 128],
                          vbuf[:, tt * 1024:(tt + 1) * 1024],
                          start=(tt == 0), stop=(tt == 3), deps=[tke, tve, pc_free])
                gm.append(tm)
                tkd = p.mm(PE[:, dt:dt + 1], kbuf[:, tt * 1024 + dt * 128:tt * 1024 + (dt + 1) * 128],
                           ones_col[:], start=(g == 0 and tt == 0), stop=(g == n_tiles // 4 - 1 and tt == 3),
                           deps=[tke, t_ocol])
                kden_toks.append(tkd)
            ta = p.tt(ctx_sb[:, dt * 1024:(dt + 1) * 1024], ctx_sb[:, dt * 1024:(dt + 1) * 1024],
                      PC[:, 0:1024], ALU.add, deps=[gm[-1], ctx_toks[dt]])
            ctx_toks[dt] = ta
            pc_free = ta
        for (tile, tke, tve) in tile_toks:
            kv_prev[("k", tile % 4)] = (ctx_toks[DK - 1], kden_toks[-1])
            kv_prev[("v", tile % 4)] = (ctx_toks[DK - 1],)
    kden_sb = scB.enter_context(p.nc.sbuf_tensor("kden", [128, DK], F32))
    tkd_e = p.copy(kden_sb[:], PE[:, 0:DK], deps=[kden_toks[-1]], eng="vector")
    out_toks.append(p.dma(kden_out, kden_sb[:], deps=[tkd_e], key=("kdeno",)))
    for dt in range(DK):
        out_toks.append(p.dma(ctx_out[dt * 128:(dt + 1) * 128, :], ctx_sb[:, dt * 1024:(dt + 1) * 1024],
                              deps=[ctx_toks[dt]], key=("ctxo",)))
    p.barrier()
    scB.close()

    scC = ExitStack()
    witmp = scC.enter_context(p.nc.sbuf_tensor("witmp", [128, 2 * H3], I32))
    wbtmp = scC.enter_context(p.nc.sbuf_tensor("wbtmp", [128, 2 * H3], BF16))
    pc_prev = [None, None]
    pre_toks = []
    for (src, dst, nsl, cols) in ((wg_in, wg_i, DK, H3), (wu_in, wu_i, DK, H3), (wd_in, wd_i, HT3, D)):
        for k in range(nsl):
            r = k % 2
            tl = p.dma(witmp[:, r * H3:r * H3 + cols], src[k * 128:(k + 1) * 128, :], deps=[pc_prev[r]], key=("wi", r))
            tc = p.copy(wbtmp[:, r * H3:r * H3 + cols], witmp[:, r * H3:r * H3 + cols], deps=[tl])
            td = p.dma(dst[k, :, :], wbtmp[:, r * H3:r * H3 + cols], deps=[tc], key=("wst", r))
            pc_prev[r] = td
            pre_toks.append(td)
    pre_tails = [pc_prev[0], pc_prev[1]]
    p.barrier()
    scC.close()

    hact = p.sb("hact", [128, HT3 * T], BF16)
    gbuf = p.sb("gbuf", [128, 2 * 1024], BF16)
    wg_sb = p.sb("wg_sb", [128, 2 * 1024], BF16)
    wu_sb = p.sb("wu_sb", [128, 2 * 1024], BF16)
    wd_sb = p.sb("wd_sb", [128, 2 * H3], BF16)
    xtmp = p.sb("xtmp", [128, 2 * T], F32)
    ftmp = p.sb("ftmp", [128, 2 * T], F32)
    wgu_prev = [None, None]
    wd_prev = [None, None]
    x_prev = [None, None]
    f_prev = [None, None]
    hact_prev = {}
    pa_free2 = None
    pb_free2 = None
    pc_free2 = None
    gb_prev = [None, None]
    for ht in range(HT3):
        r = ht % 2
        tg = p.dma(wg_sb[:, r * 1024:(r + 1) * 1024],
                   wg_i[:, :, ht * 128:(ht + 1) * 128].rearrange("k p c -> p k c"),
                   deps=[wgu_prev[r]] + pre_tails, key=("wg", r))
        tu = p.dma(wu_sb[:, r * 1024:(r + 1) * 1024],
                   wu_i[:, :, ht * 128:(ht + 1) * 128].rearrange("k p c -> p k c"),
                   deps=[wgu_prev[r]] + pre_tails, key=("wu", r))
        for th in range(2):
            c0 = th * 1024
            mg = []
            mu = []
            for k in range(DK):
                tm = p.mm(PA[:, 0:1024], wg_sb[:, r * 1024 + k * 128:r * 1024 + (k + 1) * 128],
                          xn[:, k * T + c0:k * T + c0 + 1024], start=(k == 0), stop=(k == DK - 1),
                          deps=[tg, xn_toks[k], pa_free2])
                mg.append(tm)
                tm = p.mm(PB[:, 0:1024], wu_sb[:, r * 1024 + k * 128:r * 1024 + (k + 1) * 128],
                          xn[:, k * T + c0:k * T + c0 + 1024], start=(k == 0), stop=(k == DK - 1),
                          deps=[tu, xn_toks[k], pb_free2])
                mu.append(tm)
            tsil = p.act(gbuf[:, r * 1024:(r + 1) * 1024], PA[:, 0:1024], AF.Silu, scale=QSC,
                         deps=[mg[-1], gb_prev[r]])
            thh = p.tt(hact[:, ht * T + c0:ht * T + c0 + 1024], gbuf[:, r * 1024:(r + 1) * 1024],
                       PB[:, 0:1024], ALU.mult,
                       deps=[tsil, mu[-1], hact_prev.get((ht, th))])
            pa_free2 = tsil
            pb_free2 = thh
            gb_prev[r] = thh
            hact_prev[(ht, th)] = thh
            wgu_prev[r] = thh
    for dt in range(DK):
        r = dt % 2
        td = p.dma(wd_sb[:, r * H3:(r + 1) * H3],
                   wd_i[:, :, dt * 128:(dt + 1) * 128].rearrange("k p c -> p k c"),
                   deps=[wd_prev[r]] + pre_tails, key=("wd", r))
        txl = p.dma(xtmp[:, r * T:(r + 1) * T], xin[dt * 128:(dt + 1) * 128, :],
                    deps=[x_prev[r]], key=("xt", r))
        tf_parts = []
        for th in range(2):
            c0 = th * 1024
            mf = []
            for ht in range(HT3):
                tm = p.mm(PC[:, 0:1024], wd_sb[:, r * H3 + ht * 128:r * H3 + (ht + 1) * 128],
                          hact[:, ht * T + c0:ht * T + c0 + 1024], start=(ht == 0), stop=(ht == HT3 - 1),
                          deps=[td, hact_prev[(ht, th)], pc_free2])
                mf.append(tm)
            tf = p.stt(ftmp[:, r * T + c0:r * T + c0 + 1024], PC[:, 0:1024], float(QSC * QSC),
                       xtmp[:, r * T + c0:r * T + c0 + 1024], ALU.mult, ALU.add,
                       deps=[mf[-1], txl, f_prev[r]])
            pc_free2 = tf
            tf_parts.append(tf)
        tdma = p.dma(xpf_out[dt * 128:(dt + 1) * 128, :], ftmp[:, r * T:(r + 1) * T],
                     deps=tf_parts, key=("xpfo", r))
        x_prev[r] = tf_parts[-1]
        f_prev[r] = tdma
        wd_prev[r] = tdma
        out_toks.append(tdma)

    p.done(out_toks)
    return p.build()


def build_l2():
    p = Prog()
    T = HALF
    ctx0_in = p.din("ctxp0", [D, D], F32)
    ctx1_in = p.din("ctxp1", [D, D], F32)
    kd0_in = p.din("kden0", [128, DK], F32)
    kd1_in = p.din("kden1", [128, DK], F32)
    qexp_in = p.din("qexp", [D, T], BF16)
    qden_in = p.din("qden", [1, T], F32)
    xpf_in = p.din("xpf", [D, T], F32)
    wo_in = p.din("Wo", [D, D], F32)
    bo_in = p.din("bo_pm", [128, DK], F32)

    x1_out = p.dout("x1", [D, T], F32)
    xn1_out = p.dout("xn1", [D, T], BF16)
    xc0_out = p.dout("xn1c0", [128, DK], F32)

    p.consts([LN_EPS, -0.5, -1.0, float(-np.log(D))])
    PA = p.ps("PA", [128, 1024])
    PB = p.ps("PB", [128, 1024])
    PC = p.ps("PC", [128, 1024])
    PD = p.ps("PD", [128, 512])
    PE = p.ps("PE", [128, 512])
    out_toks = []

    x1_sb = p.sb("x1_s", [128, DK * T], F32)
    scB = ExitStack()

    kd_sb = scB.enter_context(p.nc.sbuf_tensor("kd", [128, 2 * DK], F32))
    t1 = p.dma(kd_sb[:, 0:DK], kd0_in, key=("kd0",))
    t2 = p.dma(kd_sb[:, DK:2 * DK], kd1_in, key=("kd1",))
    t3 = p.tt(kd_sb[:, 0:DK], kd_sb[:, 0:DK], kd_sb[:, DK:2 * DK], ALU.add, deps=[t1, t2])
    t4 = p.recip(kd_sb[:, DK:2 * DK], kd_sb[:, 0:DK], deps=[t3])

    ctxn = scB.enter_context(p.nc.sbuf_tensor("ctxn", [128, DK * 1024], BF16))
    y_sb = scB.enter_context(p.nc.sbuf_tensor("y", [128, DK * T], BF16))
    scB2 = ExitStack()
    c_tmp = scB2.enter_context(p.nc.sbuf_tensor("ctmp", [128, 2 * 2048], F32))
    c_prev = [None, None]
    ctxn_toks = []
    for dt in range(DK):
        r = dt % 2
        ta = p.dma(c_tmp[:, r * 2048:r * 2048 + 1024], ctx0_in[dt * 128:(dt + 1) * 128, :], deps=[c_prev[r]], key=("c0", r))
        tb = p.dma(c_tmp[:, r * 2048 + 1024:r * 2048 + 2048], ctx1_in[dt * 128:(dt + 1) * 128, :],
                   deps=[c_prev[r]], key=("c1", r))
        tsum = p.tt(c_tmp[:, r * 2048:r * 2048 + 1024], c_tmp[:, r * 2048:r * 2048 + 1024],
                    c_tmp[:, r * 2048 + 1024:r * 2048 + 2048], ALU.add, deps=[ta, tb])
        tn = p.ts2(ctxn[:, dt * 1024:(dt + 1) * 1024], c_tmp[:, r * 2048:r * 2048 + 1024],
                   kd_sb[:, DK + dt:DK + dt + 1], 0.125, ALU.mult, ALU.mult, deps=[tsum, t4])
        c_prev[r] = tn
        ctxn_toks.append(tn)

    p.barrier()
    scB2.close()
    scB3 = ExitStack()
    qexp = scB3.enter_context(p.nc.sbuf_tensor("qexp_s", [128, DK * T], BF16))
    tq = [p.dma(qexp[:, k * T:(k + 1) * T], qexp_in[k * 128:(k + 1) * 128, :], key=("qe", k)) for k in range(DK)]

    y_toks = []
    pab_free = [None, None]
    for et in range(DK):
        for th in range(2):
            pa = PA if th == 0 else PB
            mt = []
            for k in range(DK):
                tm = p.mm(pa[:, 0:1024], ctxn[:, k * 1024 + et * 128:k * 1024 + (et + 1) * 128],
                          qexp[:, k * T + th * 1024:k * T + (th + 1) * 1024],
                          start=(k == 0), stop=(k == DK - 1),
                          deps=[ctxn_toks[k], tq[k], pab_free[th]])
                mt.append(tm)
            te = p.copy(y_sb[:, et * T + th * 1024:et * T + (th + 1) * 1024], pa[:, 0:1024],
                        deps=[mt[-1]] + ([y_toks[-1]] if y_toks else []), eng="vector")
            pab_free[th] = te
            y_toks.append(te)

    p.barrier()
    scB3.close()
    wo_sb, two = conv_weight_slices(p, "wo", wo_in, DK, 1024, ctx=scB)
    bo_sb = scB.enter_context(p.nc.sbuf_tensor("bo", [128, DK], F32))
    tbo = p.dma(bo_sb[:], bo_in, key=("bo",))

    qinv = scB.enter_context(p.nc.sbuf_tensor("qinv_s", [1, T], F32))
    tqd = p.dma(qinv[:], qden_in, key=("qd",))
    tq1 = p.act(qinv[:], qinv[:], AF.Ln, deps=[tqd])
    tq2 = p.act(qinv[:], qinv[:], AF.Exp, scale=-1.0, bias=float(-np.log(D)), deps=[tq1])
    onec_f = scB.enter_context(p.nc.sbuf_tensor("onecf3", [1, 128], F32))
    tof = p.memset(onec_f[:], 1.0)
    qrep = scB.enter_context(p.nc.sbuf_tensor("qrep", [128, T], F32))
    qr_toks = []
    for c in range(0, T, 512):
        tm = p.mm(PD[:, 0:512], onec_f[:], qinv[0:1, c:c + 512], start=True, stop=True,
                  deps=[tq2, tof] + ([qr_toks[-1]] if qr_toks else []))
        te = p.copy(qrep[:, c:c + 512], PD[:, 0:512], deps=[tm], eng="vector")
        qr_toks.append(te)

    xpf_sb = scB.enter_context(p.nc.sbuf_tensor("xpf_s", [128, 2 * T], F32))
    xp_prev = [None, None]
    x1_toks = []
    pab_free = [y_toks[-1], y_toks[-1]]
    for ot in range(DK):
        r = ot % 2
        mt = []
        for th in range(2):
            pa = PA if th == 0 else PB
            for k in range(DK):
                tm = p.mm(pa[:, 0:1024], wo_sb[:, k * 1024 + ot * 128:k * 1024 + (ot + 1) * 128],
                          y_sb[:, k * T + th * 1024:k * T + (th + 1) * 1024],
                          start=(k == 0), stop=(k == DK - 1),
                          deps=[two[k], y_toks[2 * k + th], pab_free[th]])
                mt.append(tm)
        txp = p.dma(xpf_sb[:, r * T:(r + 1) * T], xpf_in[ot * 128:(ot + 1) * 128, :], deps=[xp_prev[r]], key=("xp", r))
        ta1 = p.tt(x1_sb[:, ot * T:ot * T + 1024], PA[:, 0:1024], qrep[:, 0:1024], ALU.mult,
                   deps=[mt[DK - 1], qr_toks[-1]])
        ta2 = p.tt(x1_sb[:, ot * T + 1024:ot * T + 2048], PB[:, 0:1024], qrep[:, 1024:2048], ALU.mult,
                   deps=[mt[-1], qr_toks[-1], ta1])
        tx1 = p.stt(x1_sb[:, ot * T:(ot + 1) * T], x1_sb[:, ot * T:(ot + 1) * T],
                    bo_sb[:, ot:ot + 1], xpf_sb[:, r * T:(r + 1) * T], ALU.add, ALU.add,
                    deps=[ta1, ta2, txp, tbo])
        pab_free = [ta1, ta2]
        xp_prev[r] = tx1
        x1_toks.append(tx1)
        out_toks.append(p.dma(x1_out[ot * 128:(ot + 1) * 128, :], x1_sb[:, ot * T:(ot + 1) * T],
                              deps=[tx1], key=("x1o",)))

    p.barrier()
    scB.close()
    scC = ExitStack()
    xn1 = p.sb("xn1_s", [128, DK * T], BF16)
    xc0 = p.sb("xc0_s", [128, DK], F32)
    xn1_toks, tc0 = emit_ln(p, lambda k: x1_sb[:, k * T:(k + 1) * T], T, "ln1", x1_toks, PD, PE, xn1,
                            extra_c0=xc0, ctx=scC)
    for k in range(DK):
        out_toks.append(p.dma(xn1_out[k * 128:(k + 1) * 128, :], xn1[:, k * T:(k + 1) * T],
                              deps=[xn1_toks[k]], key=("xn1o",)))
    out_toks.append(p.dma(xc0_out, xc0[:], deps=tc0, key=("xc0o",)))
    scC.close()

    p.done(out_toks)
    return p.build()


def build_l3(debug=False):
    p = Prog()
    T = N
    xn_in = p.din("xnf", [D, T], BF16)
    inw_in = p.din("inW_t", [D, 2 * DI], F32)
    outw_in = p.din("outW_t", [DI, D], F32)
    xpw_in = p.din("xpW_t", [DI, 96], F32)
    dtw_in = p.din("dtW_t", [DTR, DI], F32)
    alog_in = p.din("A_lay", [128, SI * DS], F32)
    dtb_in = p.din("dtB_pm", [128, SI], F32)
    cvb_in = p.din("convB_pm", [128, SI], F32)
    dp_in = p.din("Dp_pm", [128, SI], F32)
    cd_in = p.din("convdiag", [SI, DC, 128, 128], F32)
    sel_in = p.din("sel_bc", [128, 32 * 128], F32)
    id_in = p.din("ident", [128, 128], F32)

    mout = p.dout("mout", [D, T], F32)
    if debug:
        dbg_u = p.dout("dbg_u", [128, SI * TCH], BF16)
        dbg_xdbl = p.dout("dbg_xdbl", [128, TCH], BF16)
        dbg_delta = p.dout("dbg_delta", [128, TCH], BF16)
        dbg_da = p.dout("dbg_da", [128, TCH], F32)
        dbg_b = p.dout("dbg_b", [128, TCH], F32)
        dbg_h = p.dout("dbg_h", [128, TCH], F32)
        dbg_brep = p.dout("dbg_brep", [128, TCH], BF16)
        dbg_y = p.dout("dbg_y", [128, SI * TCH], BF16)

    inw_i = p.dtmp("inw_i", [DK, 128, 2 * DI], BF16)
    outw_i = p.dtmp("outw_i", [SI, 128, D], BF16)
    cd_i = p.dtmp("cd_i", [SI, DC * 128, 128], BF16)

    PA = p.ps("PA", [128, 1024])
    PB = p.ps("PB", [128, 1024])
    PC = p.ps("PC", [128, 1024])
    PD = p.ps("PD", [128, 512])
    PE = p.ps("PE", [128, 512])
    p.consts([-1.0, 1.0])
    out_toks = []

    dtw_sb = p.sb("dtw", [64, DI], BF16)
    xpw_sb = p.sb("xpw", [128, SI * 96], BF16)
    sel_sb = p.sb("selbc", [128, 32 * 128], BF16)
    id_sb = p.sb("identb", [128, 128], BF16)
    scP = ExitStack()
    tmp_f = scP.enter_context(p.nc.sbuf_tensor("tmpf", [128, 2 * 4096], F32))
    tmp_b = scP.enter_context(p.nc.sbuf_tensor("tmpb", [128, 2 * 4096], BF16))
    prev = [None, None]
    pre_toks = []
    for k in range(DK):
        r = k % 2
        tl = p.dma(tmp_f[:, r * 4096:(r + 1) * 4096], inw_in[k * 128:(k + 1) * 128, :], deps=[prev[r]], key=("pre", r))
        tc = p.copy(tmp_b[:, r * 4096:(r + 1) * 4096], tmp_f[:, r * 4096:(r + 1) * 4096], deps=[tl])
        td = p.dma(inw_i[k, :, :], tmp_b[:, r * 4096:(r + 1) * 4096], deps=[tc], key=("prest", r))
        prev[r] = td
        pre_toks.append(td)
    for s in range(SI):
        r = s % 2
        tl = p.dma(tmp_f[:, r * 4096:r * 4096 + D], outw_in[s * 128:(s + 1) * 128, :], deps=[prev[r]], key=("pre", r))
        tc = p.copy(tmp_b[:, r * 4096:r * 4096 + D], tmp_f[:, r * 4096:r * 4096 + D], deps=[tl])
        td = p.dma(outw_i[s, :, :], tmp_b[:, r * 4096:r * 4096 + D], deps=[tc], key=("prest", r))
        prev[r] = td
        pre_toks.append(td)
    for s in range(SI):
        r = s % 2
        tl = p.dma(tmp_f[:, r * 4096:r * 4096 + 512],
                   cd_in[s, :, :, :].rearrange("j p c -> p j c"), deps=[prev[r]], key=("pre", r))
        tc = p.copy(tmp_b[:, r * 4096:r * 4096 + 512], tmp_f[:, r * 4096:r * 4096 + 512], deps=[tl])
        td = p.dma(cd_i[s, :, :].rearrange("(j p) c -> p j c", p=128), tmp_b[:, r * 4096:r * 4096 + 512],
                   deps=[tc], key=("prest", r))
        prev[r] = td
        pre_toks.append(td)
    dtw_f = scP.enter_context(p.nc.sbuf_tensor("dtwf", [64, DI], F32))
    t_dtw0 = p.dma(dtw_f[:], dtw_in, key=("dtwl",))
    t_dtw = p.copy(dtw_sb[:], dtw_f[:], deps=[t_dtw0])
    t_pre = [t for t in (prev[0], prev[1]) if t is not None] + pre_toks[-2:]
    tsel0 = p.dma(tmp_f[:, 0:4096], sel_in, deps=[prev[0]], key=("pre", 0))
    t_sel = p.copy(sel_sb[:], tmp_f[:, 0:4096], deps=[tsel0])
    prev[0] = t_sel
    tid0 = p.dma(tmp_f[:, 0:128], id_in, deps=[prev[0]], key=("pre", 0))
    t_id = p.copy(id_sb[:], tmp_f[:, 0:128], deps=[tid0])
    prev[0] = t_id
    t_xpw = []
    for s in range(SI):
        r = s % 2
        tl = p.dma(tmp_f[:, r * 4096:r * 4096 + 96], xpw_in[s * 128:(s + 1) * 128, :], deps=[prev[r]], key=("pre", r))
        tc = p.copy(xpw_sb[:, s * 96:(s + 1) * 96], tmp_f[:, r * 4096:r * 4096 + 96], deps=[tl])
        prev[r] = tc
        t_xpw.append(tc)
    p.barrier()
    scP.close()

    a_sb = p.sb("a_lay", [128, SI * DS], F32)
    t_a0 = p.dma(a_sb[:], alog_in, key=("a0",))
    t_a1 = p.act(a_sb[:], a_sb[:], AF.Exp, deps=[t_a0])
    t_a = p.ts(a_sb[:], a_sb[:], -1.0, ALU.mult, deps=[t_a1])
    dtb_sb = p.sb("dtb", [128, SI], F32)
    t_dtb = p.dma(dtb_sb[:], dtb_in, key=("dtb",))
    cvb_sb = p.sb("cvb", [128, SI], F32)
    t_cvb = p.dma(cvb_sb[:], cvb_in, key=("cvb",))
    dp_sb = p.sb("dp", [128, SI], F32)
    t_dp = p.dma(dp_sb[:], dp_in, key=("dp",))

    NG = 8
    xnc = p.sb("xnc", [128, DK * TCH], BF16)
    u_sb = p.sb("u", [128, SI * TCH], BF16)
    brep = p.sb("brep", [128, NG * TCH], BF16)
    crep = p.sb("crep", [128, NG * TCH], BF16)
    yacc = p.sb("yacc", [128, SI * TCH], BF16)
    xdbl = p.sb("xdbl", [128, TCH], BF16)
    xi_sb = p.sb("xi", [128, 2 * (TCH + 3)], BF16)
    carry = p.sb("carry", [128, SI * DS], F32)
    cdiag = p.sb("cdiag", [128, 2 * 512], BF16)
    inws = p.sb("inws", [128, 2 * 1024], BF16)
    outws = p.sb("outws", [128, 2 * 2048], BF16)
    delta = p.sb("delta", [128, TCH], BF16)
    du_sb = p.sb("du", [128, TCH], BF16)
    sp_r = p.sb("spr", [128, TCH], F32)
    sp_b = p.sb("spb", [128, TCH], F32)
    da_f = p.sb("daf", [128, 2 * TCH], BF16)
    b_f = p.sb("bf_", [128, 2 * TCH], BF16)
    h_f = p.sb("hf", [128, 2 * TCH], BF16)
    prod = p.sb("prod", [128, 2 * TCH], BF16)
    zsil = p.sb("zsil", [128, TCH], BF16)
    mo_sb = p.sb("mo", [128, 1024], F32)
    onec_b = p.sb("onecb", [1, 128], BF16)
    t_ocb = p.memset(onec_b[:], 1.0)

    t_carry0 = p.memset(carry[:], 0.0, eng="vector")
    carry_toks = {}
    xi_carry_toks = {}
    xi_tail = p.sb("xitail", [128, SI * 3], BF16)
    t_xit0 = p.memset(xi_tail[:], 0.0, eng="vector")

    xnc_prev = [None] * DK
    u_prev = [None] * SI
    y_prev = [None] * SI
    yacc_fin = [None] * SI
    brep_prev = [None] * NG
    crep_prev = [None] * NG
    xi_prev = [None, None]
    cd_prev = [None, None]
    inws_prev = [None, None]
    outws_prev = [None, None]
    da_prev = [None, None]
    b_prev = [None, None]
    h_prev = [None, None]
    prod_prev = [None, None]
    delta_prev = [None]
    du_prev = [None]
    sp_prev = [None]
    xdbl_prev = [None]
    zsil_prev = [None]
    mo_prev = [None]
    pa_free = None
    pb_free = None
    pc_free = None

    for ch in range(NCH):
        c0 = ch * TCH
        txn = []
        for k in range(DK):
            t = p.dma(xnc[:, k * TCH:(k + 1) * TCH], xn_in[k * 128:(k + 1) * 128, c0:c0 + TCH],
                      deps=[xnc_prev[k]], key=("xnc", k))
            txn.append(t)
        xpw_toks = []
        for s in range(SI):
            r = s % 2
            til = p.dma(inws[:, r * 1024:(r + 1) * 1024],
                        inw_i[:, :, s * 128:(s + 1) * 128].rearrange("k p c -> p k c"),
                        deps=[inws_prev[r], t_pre], key=("inws", r))
            mt = []
            for k in range(DK):
                tm = p.mm(PA[:, 0:TCH], inws[:, r * 1024 + k * 128:r * 1024 + (k + 1) * 128],
                          xnc[:, k * TCH:(k + 1) * TCH], start=(k == 0), stop=(k == DK - 1),
                          deps=[til, txn[k], pa_free])
                mt.append(tm)
            thd = p.copy(xi_sb[:, r * (TCH + 3):r * (TCH + 3) + 3], xi_tail[:, s * 3:(s + 1) * 3],
                         deps=[xi_prev[r], xi_carry_toks.get(s), t_xit0], eng="vector")
            tev = p.copy(xi_sb[:, r * (TCH + 3) + 3:(r + 1) * (TCH + 3)], PA[:, 0:TCH],
                         deps=[mt[-1], xi_prev[r]])
            pa_free = tev
            ttl = p.copy(xi_tail[:, s * 3:(s + 1) * 3],
                         xi_sb[:, r * (TCH + 3) + TCH:r * (TCH + 3) + TCH + 3],
                         deps=[tev], eng="vector")
            xi_carry_toks[s] = ttl
            tcd = p.dma(cdiag[:, (s % 2) * 512:(s % 2) * 512 + 512],
                        cd_i[s, :, :].rearrange("(j p) c -> p j c", p=128),
                        deps=[cd_prev[s % 2], t_pre], key=("cdg", s % 2))
            mc = []
            for j in range(DC):
                tm = p.mm(PB[:, 0:TCH], cdiag[:, (s % 2) * 512 + j * 128:(s % 2) * 512 + (j + 1) * 128],
                          xi_sb[:, r * (TCH + 3) + j:r * (TCH + 3) + j + TCH],
                          start=(j == 0), stop=(j == DC - 1), deps=[tcd, thd, tev, pb_free])
                mc.append(tm)
            cd_prev[s % 2] = mc[-1]
            xi_prev[r] = (mc[-1], ttl)
            tu = p.act(u_sb[:, s * TCH:(s + 1) * TCH], PB[:, 0:TCH], AF.Silu,
                       bias=cvb_sb[:, s:s + 1], deps=[mc[-1], t_cvb, u_prev[s]])
            pb_free = tu
            u_prev[s] = tu
            inws_prev[r] = mt[-1]
            txp = p.mm(PC[0:96, 0:TCH], xpw_sb[:, s * 96:(s + 1) * 96], u_sb[:, s * TCH:(s + 1) * TCH],
                       start=(s == 0), stop=(s == SI - 1), deps=[tu, t_xpw[s], pc_free])
            xpw_toks.append(txp)
            pc_free = None
        txd = p.copy(xdbl[0:96, :], PC[0:96, 0:TCH], deps=[xpw_toks[-1], xdbl_prev[0]])
        xdbl_prev[0] = txd
        pc_free = txd
        if debug and ch == 0:
            out_toks.append(p.dma(dbg_u, u_sb[:], deps=[u_prev[s2] for s2 in range(SI)], key=("dbg",)))
            out_toks.append(p.dma(dbg_xdbl, xdbl[:], deps=[txd], key=("dbg",)))
        for ng in range(2):
            br_toks = [None] * NG
            cr_toks = [None] * NG
            for nl in range(NG):
                n = ng * NG + nl
                tm = p.mm(PA[:, 0:TCH], sel_sb[64:96, n * 128:(n + 1) * 128], xdbl[64:96, :],
                          start=True, stop=True, deps=[txd, t_sel, pa_free])
                te = p.copy(brep[:, nl * TCH:(nl + 1) * TCH], PA[:, 0:TCH],
                            deps=[tm, brep_prev[nl]], eng="vector")
                pa_free = te
                br_toks[nl] = te
                brep_prev[nl] = te
                tm = p.mm(PB[:, 0:TCH], sel_sb[64:96, (16 + n) * 128:(16 + n + 1) * 128], xdbl[64:96, :],
                          start=True, stop=True, deps=[txd, t_sel, pb_free])
                te = p.copy(crep[:, nl * TCH:(nl + 1) * TCH], PB[:, 0:TCH],
                            deps=[tm, crep_prev[nl]], eng="vector")
                pb_free = te
                cr_toks[nl] = te
                crep_prev[nl] = te
            for s in range(SI):
                tdm = p.mm(PA[:, 0:TCH], dtw_sb[:, s * 128:(s + 1) * 128], xdbl[0:64, :],
                           start=True, stop=True, deps=[t_dtw, txd, pa_free])
                t1 = p.act(sp_r[:], PA[:, 0:TCH], AF.Relu, bias=dtb_sb[:, s:s + 1],
                           deps=[tdm, t_dtb, sp_prev[0]])
                t2 = p.act(sp_b[:], PA[:, 0:TCH], AF.Abs, bias=dtb_sb[:, s:s + 1], deps=[tdm, t1])
                pa_free = t2
                t3 = p.act(sp_b[:], sp_b[:], AF.Exp, scale=-1.0, deps=[t2])
                t4 = p.act(sp_b[:], sp_b[:], AF.Ln, bias=1.0, deps=[t3])
                t5 = p.tt(delta[:], sp_r[:], sp_b[:], ALU.add, deps=[t4, t1, delta_prev[0]])
                sp_prev[0] = t5
                t6 = p.tt(du_sb[:], delta[:], u_sb[:, s * TCH:(s + 1) * TCH], ALU.mult,
                          deps=[t5, u_prev[s], du_prev[0]])
                last_da_use = None
                last_tb = None
                ymm = None
                for nl in range(NG):
                    n = ng * NG + nl
                    rr = nl % 2
                    cidx = s * DS + n
                    tda = p.act(da_f[:, rr * TCH:(rr + 1) * TCH], delta[:], AF.Exp,
                                scale=a_sb[:, cidx:cidx + 1], deps=[t5, t_a, da_prev[rr]])
                    tb = p.tt(b_f[:, rr * TCH:(rr + 1) * TCH], du_sb[:],
                              brep[:, nl * TCH:(nl + 1) * TCH],
                              ALU.mult, deps=[t6, br_toks[nl], b_prev[rr]])
                    th = p.scan(h_f[:, rr * TCH:(rr + 1) * TCH], da_f[:, rr * TCH:(rr + 1) * TCH],
                                b_f[:, rr * TCH:(rr + 1) * TCH], carry[:, cidx:cidx + 1],
                                deps=[tda, tb, carry_toks.get(cidx, t_carry0), h_prev[rr]])
                    tcar = p.copy(carry[:, cidx:cidx + 1], h_f[:, rr * TCH + TCH - 1:rr * TCH + TCH],
                                  deps=[th], eng="gpsimd")
                    carry_toks[cidx] = tcar
                    tpr = p.tt(prod[:, rr * TCH:(rr + 1) * TCH], h_f[:, rr * TCH:(rr + 1) * TCH],
                               crep[:, nl * TCH:(nl + 1) * TCH],
                               ALU.mult, deps=[th, cr_toks[nl], prod_prev[rr]])
                    ymm = p.mm(PC[:, 0:TCH], id_sb[:], prod[:, rr * TCH:(rr + 1) * TCH],
                               start=(nl == 0), stop=(nl == NG - 1),
                               deps=[tpr, t_id, pc_free if nl == 0 else None])
                    prod_prev[rr] = ymm
                    da_prev[rr] = th
                    b_prev[rr] = th
                    h_prev[rr] = (tpr, tcar)
                    last_da_use = th
                    last_tb = tb
                if ng == 0:
                    te_y = p.copy(yacc[:, s * TCH:(s + 1) * TCH], PC[:, 0:TCH],
                                  deps=[ymm, y_prev[s]])
                else:
                    te_y = p.tt(yacc[:, s * TCH:(s + 1) * TCH], yacc[:, s * TCH:(s + 1) * TCH],
                                PC[:, 0:TCH], ALU.add, deps=[ymm, yacc_fin[s]])
                pc_free = te_y
                delta_prev[0] = last_da_use
                du_prev[0] = last_tb
                yacc_fin[s] = te_y
        for s in range(SI):
            tud = p.stt(yacc[:, s * TCH:(s + 1) * TCH], u_sb[:, s * TCH:(s + 1) * TCH],
                        dp_sb[:, s:s + 1], yacc[:, s * TCH:(s + 1) * TCH], ALU.mult, ALU.add,
                        deps=[yacc_fin[s], t_dp, u_prev[s]])
            r = s % 2
            til = p.dma(inws[:, r * 1024:(r + 1) * 1024],
                        inw_i[:, :, (SI + s) * 128:(SI + s + 1) * 128].rearrange("k p c -> p k c"),
                        deps=[inws_prev[r], t_pre], key=("inws", r))
            mt = []
            for k in range(DK):
                tm = p.mm(PB[:, 0:TCH], inws[:, r * 1024 + k * 128:r * 1024 + (k + 1) * 128],
                          xnc[:, k * TCH:(k + 1) * TCH], start=(k == 0), stop=(k == DK - 1),
                          deps=[til, txn[k], pb_free])
                mt.append(tm)
            inws_prev[r] = mt[-1]
            tz = p.act(zsil[:], PB[:, 0:TCH], AF.Silu, deps=[mt[-1], zsil_prev[0]])
            pb_free = tz
            ty = p.tt(yacc[:, s * TCH:(s + 1) * TCH], yacc[:, s * TCH:(s + 1) * TCH], zsil[:],
                      ALU.mult, deps=[tz, tud])
            zsil_prev[0] = ty
            y_prev[s] = ty
        if debug and ch == 0:
            out_toks.append(p.dma(dbg_y, yacc[:], deps=[y_prev[s2] for s2 in range(SI)], key=("dbg",)))
        for ot in range(DK):
            r = ot % 2
            tol = p.dma(outws[:, r * 2048:(r + 1) * 2048],
                        outw_i[:, :, ot * 128:(ot + 1) * 128].rearrange("s p c -> p s c"),
                        deps=[outws_prev[r], t_pre], key=("ow", r))
            mt = []
            for s in range(SI):
                tm = p.mm(PA[:, 0:TCH], outws[:, r * 2048 + s * 128:r * 2048 + (s + 1) * 128],
                          yacc[:, s * TCH:(s + 1) * TCH], start=(s == 0), stop=(s == SI - 1),
                          deps=[tol, y_prev[s], pa_free])
                mt.append(tm)
            tev = p.copy(mo_sb[:], PA[:, 0:TCH], deps=[mt[-1], mo_prev[0]], eng="vector")
            pa_free = tev
            tdm = p.dma(mout[ot * 128:(ot + 1) * 128, c0:c0 + TCH], mo_sb[:], deps=[tev], key=("mo",))
            mo_prev[0] = tdm
            outws_prev[r] = tdm
            out_toks.append(tdm)
        for k in range(DK):
            xnc_prev[k] = out_toks[-1]
    p.done(out_toks)
    return p.build()


def build_l4():
    p = Prog()
    T = HALF
    x1_in = p.din("x1", [D, T], F32)
    xn1_in = p.din("xn1", [D, T], BF16)
    xc0_in = p.din("xn1c0", [128, DK], F32)
    mf_in = p.din("mf", [D, T], F32)
    mb_in = p.din("mb", [D, T], F32)
    wr_in = p.din("wr_t", [D, EM], F32)
    at_in = p.din("a_t", [EM, D, RM], F32)
    bwt_in = p.din("bw_t", [EM, RM, D], F32)

    x2_out = p.dout("x2", [D, T], F32)

    PA = p.ps("PA", [128, 1024])
    PB = p.ps("PB", [128, 1024])
    PD = p.ps("PD", [128, 512])
    out_toks = []

    xn1 = p.sb("xn1_s4", [128, DK * T], BF16)
    txn = [p.dma(xn1[:, k * T:(k + 1) * T], xn1_in[k * 128:(k + 1) * 128, :], key=("xn1l", k)) for k in range(DK)]

    xc0 = p.sb("xc0_s4", [128, DK], F32)
    xc0b = p.sb("xc0b", [128, DK], BF16)
    t1 = p.dma(xc0[:], xc0_in, key=("xc0l",))
    t2 = p.copy(xc0b[:], xc0[:], deps=[t1])
    wr_f = p.sb("wrf", [128, DK * EM], F32)
    wr_b = p.sb("wrb", [128, DK * EM], BF16)
    t3 = p.dma(wr_f[:], wr_in.rearrange("(k p) e -> p k e", p=128), key=("wr",))
    t4 = p.copy(wr_b[:], wr_f[:], deps=[t3])
    lg = p.sb("lg", [1, 4 * EM], F32)
    mml = []
    for k in range(DK):
        tm = p.mm(PD[0:1, 0:EM], xc0b[:, k:k + 1], wr_b[:, k * EM:(k + 1) * EM],
                  start=(k == 0), stop=(k == DK - 1), deps=[t2, t4])
        mml.append(tm)
    L, S_, M2_, G_ = 0, EM, 2 * EM, 3 * EM
    t5 = p.copy(lg[0:1, L:L + EM], PD[0:1, 0:EM], deps=[mml[-1]], eng="vector")
    m1 = p.sb("m1", [1, 4], F32)
    t6 = p.red(m1[0:1, 0:1], lg[0:1, L:L + EM], ALU.max, deps=[t5])
    t7 = p.ts(lg[0:1, S_:S_ + EM], lg[0:1, L:L + EM], m1[0:1, 0:1], ALU.subtract, deps=[t6, t5])
    t8 = p.ts(lg[0:1, M2_:M2_ + EM], lg[0:1, S_:S_ + EM], 0.0, ALU.is_ge, deps=[t7])
    t9 = p.stt(lg[0:1, M2_:M2_ + EM], lg[0:1, M2_:M2_ + EM], -1e30, lg[0:1, L:L + EM],
               ALU.mult, ALU.add, deps=[t8, t5])
    t10 = p.red(m1[0:1, 1:2], lg[0:1, M2_:M2_ + EM], ALU.max, deps=[t9])
    t11 = p.ts(lg[0:1, M2_:M2_ + EM], lg[0:1, L:L + EM], m1[0:1, 1:2], ALU.is_ge, deps=[t10, t9])
    t12 = p.act(lg[0:1, G_:G_ + EM], lg[0:1, S_:S_ + EM], AF.Exp, deps=[t7, t11])
    t13 = p.tt(lg[0:1, G_:G_ + EM], lg[0:1, G_:G_ + EM], lg[0:1, M2_:M2_ + EM], ALU.mult,
               deps=[t12, t11])
    t14 = p.red(m1[0:1, 2:3], lg[0:1, G_:G_ + EM], ALU.add, deps=[t13])
    t15 = p.recip(m1[0:1, 3:4], m1[0:1, 2:3], deps=[t14])
    t16 = p.ts(lg[0:1, G_:G_ + EM], lg[0:1, G_:G_ + EM], m1[0:1, 3:4], ALU.mult, deps=[t15, t13])
    gb = p.sb("gb", [1, EM], BF16)
    t17 = p.copy(gb[:], lg[0:1, G_:G_ + EM], deps=[t16], eng="vector")
    ones64 = p.sb("ones64", [1, 64], BF16)
    t18 = p.memset(ones64[:], 1.0)
    gcol = p.sb("gcol", [64, EM], F32)
    gcol_toks = []
    for e in range(EM):
        tm = p.mm(PD[0:64, 0:1], ones64[:], gb[0:1, e:e + 1], start=True, stop=True,
                  deps=[t17, t18] + ([gcol_toks[-1]] if gcol_toks else []))
        te = p.copy(gcol[:, e:e + 1], PD[0:64, 0:1], deps=[tm], eng="vector")
        gcol_toks.append(te)

    at_sb = p.sb("at", [128, EM * DK * RM], BF16)
    at_f = p.sb("atf", [128, 2 * RM], F32)
    ap_prev = [None, None]
    at_toks = {}
    for e in range(EM):
        for k in range(DK):
            r = (e * DK + k) % 2
            tl = p.dma(at_f[:, r * RM:(r + 1) * RM], at_in[e, k * 128:(k + 1) * 128, :],
                       deps=[ap_prev[r]], key=("at", r))
            tc = p.copy(at_sb[:, (e * DK + k) * RM:(e * DK + k + 1) * RM], at_f[:, r * RM:(r + 1) * RM],
                        deps=[tl])
            ap_prev[r] = tc
            at_toks[(e, k)] = tc
    bwt_sb = p.sb("bwt", [64, EM * D], BF16)
    bwt_f = p.sb("bwtf", [64, 2 * D], F32)
    bp_prev = [None, None]
    bwt_toks = {}
    for e in range(EM):
        r = e % 2
        tl = p.dma(bwt_f[:, r * D:(r + 1) * D], bwt_in[e, :, :], deps=[bp_prev[r]], key=("bw", r))
        tc = p.copy(bwt_sb[:, e * D:(e + 1) * D], bwt_f[:, r * D:(r + 1) * D], deps=[tl])
        bp_prev[r] = tc
        bwt_toks[e] = tc

    midg = p.sb("midg", [64, EM * T], BF16)
    midg_toks = {}
    pa_free = None
    for e in range(EM):
        for th in range(2):
            mt = []
            for k in range(DK):
                tm = p.mm(PA[0:64, 0:1024], at_sb[:, (e * DK + k) * RM:(e * DK + k + 1) * RM],
                          xn1[:, k * T + th * 1024:k * T + (th + 1) * 1024],
                          start=(k == 0), stop=(k == DK - 1),
                          deps=[at_toks[(e, k)], txn[k], pa_free])
                mt.append(tm)
            te = p.ts(midg[0:64, e * T + th * 1024:e * T + (th + 1) * 1024], PA[0:64, 0:1024],
                      gcol[:, e:e + 1], ALU.mult, deps=[mt[-1], gcol_toks[e]])
            pa_free = te
            midg_toks[(e, th)] = te

    xtmp = p.sb("xtmp", [128, 2 * T], F32)
    mtmp = p.sb("mtmp", [128, 2 * T], F32)
    mtmp2 = p.sb("mtmp2", [128, 2 * T], F32)
    o_sb = p.sb("osb", [128, 2 * T], F32)
    prev_r = [None, None]
    pab_free = [pa_free, None]
    for ot in range(DK):
        r = ot % 2
        mt = []
        for th in range(2):
            pa = PA if th == 0 else PB
            for e in range(EM):
                tm = p.mm(pa[:, 0:1024], bwt_sb[0:64, e * D + ot * 128:e * D + (ot + 1) * 128],
                          midg[0:64, e * T + th * 1024:e * T + (th + 1) * 1024],
                          start=(e == 0), stop=(e == EM - 1),
                          deps=[bwt_toks[e], midg_toks[(e, th)], pab_free[th]])
                mt.append(tm)
        tmf = p.dma(mtmp[:, r * T:(r + 1) * T], mf_in[ot * 128:(ot + 1) * 128, :], deps=[prev_r[r]], key=("mf", r))
        tmb = p.dma(mtmp2[:, r * T:(r + 1) * T], mb_in[ot * 128:(ot + 1) * 128, :], deps=[prev_r[r]], key=("mb", r))
        tx1 = p.dma(xtmp[:, r * T:(r + 1) * T], x1_in[ot * 128:(ot + 1) * 128, :], deps=[prev_r[r]], key=("x1l", r))
        ta = p.tt(mtmp[:, r * T:(r + 1) * T], mtmp[:, r * T:(r + 1) * T], mtmp2[:, r * T:(r + 1) * T],
                  ALU.add, deps=[tmf, tmb])
        tb = p.stt(xtmp[:, r * T:(r + 1) * T], mtmp[:, r * T:(r + 1) * T], 0.5,
                   xtmp[:, r * T:(r + 1) * T], ALU.mult, ALU.add, deps=[ta, tx1])
        tc = p.tt(xtmp[:, r * T:(r + 1) * T], xtmp[:, r * T:(r + 1) * T], xn1[:, ot * T:(ot + 1) * T],
                  ALU.add, deps=[tb, txn[ot]])
        td1 = p.tt(o_sb[:, r * T:r * T + 1024], xtmp[:, r * T:r * T + 1024], PA[:, 0:1024], ALU.add,
                   deps=[tc, mt[EM - 1]])
        td2 = p.tt(o_sb[:, r * T + 1024:r * T + 2048], xtmp[:, r * T + 1024:r * T + 2048],
                   PB[:, 0:1024], ALU.add, deps=[tc, mt[-1], td1])
        pab_free = [td1, td2]
        tdm = p.dma(x2_out[ot * 128:(ot + 1) * 128, :], o_sb[:, r * T:(r + 1) * T], deps=[td1, td2], key=("x2o", r))
        prev_r[r] = tdm
        out_toks.append(tdm)

    p.done(out_toks)
    return p.build()


def _t(a):
    return np.ascontiguousarray(np.asarray(a).T)


DBG = None


def kernel(**kw):
    x = np.asarray(kw["L0_attn_x"], dtype=np.float32)
    cores = list(range(NCORE))

    nc1 = build_l1()
    wg_t = _t(kw["L0_ffn_w_gate"]).astype(np.int32)
    wu_t = _t(kw["L0_ffn_w_up"]).astype(np.int32)
    wd_t = _t(kw["L0_ffn_w_down"]).astype(np.int32)
    maps1 = []
    for c in cores:
        b, h = c // 2, c % 2
        maps1.append({
            "xT": _t(x[b, h * HALF:(h + 1) * HALF, :]),
            "Wq": np.asarray(kw["L0_attn_Wq"], np.float32),
            "Wk": np.asarray(kw["L0_attn_Wk"], np.float32),
            "Wv": np.asarray(kw["L0_attn_Wv"], np.float32),
            "wg_t": wg_t, "wu_t": wu_t, "wd_t": wd_t,
        })
    r1 = run_bass_kernel_spmd(nc1, maps1, core_ids=cores).results

    nc2 = build_l2()
    bo_pm = np.ascontiguousarray(np.asarray(kw["L0_attn_bo"], np.float32).reshape(DK, 128).T)
    maps2 = []
    for c in cores:
        b, h = c // 2, c % 2
        maps2.append({
            "ctxp0": r1[b * 2 + 0]["ctxp"], "ctxp1": r1[b * 2 + 1]["ctxp"],
            "kden0": r1[b * 2 + 0]["kdenp"], "kden1": r1[b * 2 + 1]["kdenp"],
            "qexp": r1[c]["qexp"], "qden": r1[c]["qden"], "xpf": r1[c]["xpf"],
            "Wo": np.asarray(kw["L0_attn_Wo"], np.float32), "bo_pm": bo_pm,
        })
    r2 = run_bass_kernel_spmd(nc2, maps2, core_ids=cores).results

    nc3 = build_l3()
    maps3 = []
    w3 = {}
    for d, sfx in ((0, "_f"), (1, "_b")):
        convW = np.asarray(kw[f"L1_attn_convW{sfx}"], np.float32)
        cdiag = np.zeros((SI, DC, 128, 128), np.float32)
        for s in range(SI):
            for j in range(DC):
                np.fill_diagonal(cdiag[s, j], convW[s * 128:(s + 1) * 128, 0, j])
        sel_bc = np.zeros((128, 32 * 128), np.float32)
        for j in range(32):
            sel_bc[64 + j, j * 128:(j + 1) * 128] = 1.0
        alog = np.asarray(kw[f"L1_attn_Alog{sfx}"], np.float32)
        a_lay = np.ascontiguousarray(alog.reshape(SI, 128, DS).transpose(1, 0, 2).reshape(128, SI * DS))
        w3[d] = {
            "inW_t": _t(kw[f"L1_attn_inW{sfx}"]),
            "outW_t": _t(kw[f"L1_attn_outW{sfx}"]),
            "xpW_t": _t(kw[f"L1_attn_xpW{sfx}"]),
            "dtW_t": _t(kw[f"L1_attn_dtW{sfx}"]),
            "A_lay": a_lay,
            "dtB_pm": np.ascontiguousarray(np.asarray(kw[f"L1_attn_dtB{sfx}"], np.float32).reshape(SI, 128).T),
            "convB_pm": np.ascontiguousarray(np.asarray(kw[f"L1_attn_convB{sfx}"], np.float32).reshape(SI, 128).T),
            "Dp_pm": np.ascontiguousarray(np.asarray(kw[f"L1_attn_Dp{sfx}"], np.float32).reshape(SI, 128).T),
            "convdiag": cdiag,
            "sel_bc": sel_bc,
            "ident": np.eye(128, dtype=np.float32),
        }
    for c in cores:
        b, d = c // 2, c % 2
        xnf = np.concatenate([np.asarray(r2[b * 2 + 0]["xn1"]), np.asarray(r2[b * 2 + 1]["xn1"])], axis=1)
        if d == 1:
            xnf = xnf[:, ::-1]
        m = {"xnf": np.ascontiguousarray(xnf)}
        m.update(w3[d])
        maps3.append(m)
    r3 = run_bass_kernel_spmd(nc3, maps3, core_ids=cores).results

    nc4 = build_l4()
    a_t = np.ascontiguousarray(np.asarray(kw["L1_ffn_A"], np.float32).transpose(0, 2, 1))
    bw_t = np.ascontiguousarray(np.asarray(kw["L1_ffn_Bw"], np.float32).transpose(0, 2, 1))
    wr_t = _t(kw["L1_ffn_Wr"])
    maps4 = []
    for c in cores:
        b, h = c // 2, c % 2
        mf = np.asarray(r3[b * 2 + 0]["mout"])
        mb = np.asarray(r3[b * 2 + 1]["mout"])[:, ::-1]
        maps4.append({
            "x1": r2[c]["x1"], "xn1": r2[c]["xn1"], "xn1c0": r2[b * 2 + 0]["xn1c0"],
            "mf": np.ascontiguousarray(mf[:, h * HALF:(h + 1) * HALF]),
            "mb": np.ascontiguousarray(mb[:, h * HALF:(h + 1) * HALF]),
            "wr_t": wr_t, "a_t": a_t, "bw_t": bw_t,
        })
    r4 = run_bass_kernel_spmd(nc4, maps4, core_ids=cores).results

    out = np.empty((B, N, D), np.float32)
    for c in cores:
        b, h = c // 2, c % 2
        out[b, h * HALF:(h + 1) * HALF, :] = np.asarray(r4[c]["x2"]).T
    if DBG is not None:
        DBG.update(r1=r1, r2=r2, r3=r3, r4=r4)
    return out
